# version 02
# baseline (speedup 1.0000x reference)
import math
import numpy as np
from contextlib import ExitStack

import concourse.bass as bass
import concourse.mybir as mybir
from concourse.bass_utils import run_bass_kernel_spmd
from concourse.expressions_rust import RuntimeValue

F32 = mybir.dt.float32
F32R = mybir.dt.float32r
BF16 = mybir.dt.bfloat16
I32 = mybir.dt.int32
AF = mybir.ActivationFunctionType
ALU = mybir.AluOpType
AX = mybir.AxisListType

D = 2048
T = 1024
NCH = D // 128
HFF0 = 8192
HFF1 = 5632
NH0 = 16
NH1 = 32
EPS_LN = 1e-5
EPS_BL = 1e-8
MAGIC_BASE = 0x5A5A0000
RND_MAGIC = 12582912.0

QBLOCKS = [[0, 1, 2, 3, 12, 13, 14, 15], [4, 5, 6, 7, 8, 9, 10, 11]]
GATHER_BLOCKS = QBLOCKS[0] + QBLOCKS[1]

LAUNCH_GROUPS = [[0, 1, 2, 3]]
STOP_AFTER = None
G2_SKIP = set()
LN_UPTO = None
LN_LVL = {None: 9, 't1': 1, 'sq': 2, 'var': 3, 'vb': 35, 'varnop': 36, 'ssq': 4, 'rs': 5}


def _positions(rank):
    out = []
    for b in QBLOCKS[rank]:
        out.extend(range(b * 128, (b + 1) * 128))
    return np.array(out, dtype=np.int64)


def _l1_kept(rank, qt):
    pq = QBLOCKS[rank][qt]
    out = []
    for kt in range(16):
        pk = GATHER_BLOCKS[kt]
        if pk < pq:
            out.append((kt, "full"))
        elif pk == pq:
            out.append((kt, "diag"))
    return out



def _hadamard128():
    h = np.array([[1.0]], dtype=np.float32)
    while h.shape[0] < 128:
        h = np.block([[h, h], [h, -h]])
    return h


def _rsw():
    m = np.arange(128)
    s = np.where((m % 64) < 32, -1.0, 1.0)
    r = np.zeros((128, 128), dtype=np.float32)
    r[m ^ 32, m] = s
    return r


def _diag_mask():
    q = np.arange(128)[:, None]
    k = np.arange(128)[None, :]
    return np.where(k <= q, 0.0, -1e9).astype(np.float32)


def _tile_lhsT(L):
    L = np.ascontiguousarray(L)
    Kd, Jd = L.shape
    t = L.reshape(Kd // 128, 128, Jd // 128, 128).transpose(2, 1, 0, 3)
    return np.ascontiguousarray(t.reshape(Jd, Kd))


def stage_inputs(inp):
    import ml_dtypes
    asc = np.ascontiguousarray
    x = np.asarray(inp["L0_attn_hidden_state"])
    cos = np.asarray(inp["L1_attn_cos"])
    sin = np.asarray(inp["L1_attn_sin"])

    sw = {
        "l0_wq": _tile_lhsT(np.asarray(inp["L0_attn_Wq"]).T),
        "l0_wk": _tile_lhsT(np.asarray(inp["L0_attn_Wk"]).T),
        "l0_wv": _tile_lhsT(np.asarray(inp["L0_attn_Wv"]).T),
        "l0_wo": _tile_lhsT(np.asarray(inp["L0_attn_Wo"]).T),
        "l0_bq": asc(np.asarray(inp["L0_attn_bq"]).reshape(16, 128).T),
        "l0_bk": asc(np.asarray(inp["L0_attn_bk"]).reshape(16, 128).T),
        "l0_bv": asc(np.asarray(inp["L0_attn_bv"]).reshape(16, 128).T),
        "l0_bo": asc(np.asarray(inp["L0_attn_bo"]).reshape(16, 128).T),
        "l0_w1": _tile_lhsT(np.asarray(inp["L0_ffn_W1"]).T),
        "l0_w3": _tile_lhsT(np.asarray(inp["L0_ffn_W3"]).T),
        "l0_w2": _tile_lhsT(np.asarray(inp["L0_ffn_W2"]).T),
        "l1_wq": _tile_lhsT(np.asarray(inp["L1_attn_Wq"])),
        "l1_wk": _tile_lhsT(np.asarray(inp["L1_attn_Wk"])),
        "l1_wv_r": asc(np.asarray(inp["L1_attn_Wv"], dtype=np.float32)),
        "l1_wo": _tile_lhsT(np.asarray(inp["L1_attn_Wo"])),
        "l1_w1": _tile_lhsT(np.asarray(inp["L1_ffn_w1"]).T),
        "l1_w2": _tile_lhsT(np.asarray(inp["L1_ffn_w2"]).T),
        "l1_w3": _tile_lhsT(np.asarray(inp["L1_ffn_w3"]).T),
        "ident_bf": np.eye(128, dtype=ml_dtypes.bfloat16),
        "ident_f": np.eye(128, dtype=np.float32),
        "magic_f": np.full((128, 1), 12582912.0, dtype=np.float32),
        "ones_f": np.ones((128, 128), dtype=np.float32),
        "w128_bf": _hadamard128().astype(ml_dtypes.bfloat16),
        "rsw": _rsw(),
        "diag_mask": _diag_mask(),
    }

    col32 = np.tile(np.arange(32), 4)
    in_maps = []
    for c in range(8):
        b, r = divmod(c, 2)
        pos = _positions(r)
        m = dict(sw)
        m["x_t"] = asc(x[b][pos].T)
        m["cos_rep"] = asc(cos[pos][:, col32].T)
        m["sin_rep"] = asc(sin[pos][:, col32].T)
        m["meta"] = np.array([[r, r ^ 1, r * 2048, r * 1024,
                               r * 512, r * 1024, r, 0]], dtype=np.int32)
        in_maps.append(m)
    return in_maps


def unstage_output(results):
    out = np.empty((4, 2048, 2048), dtype=np.float32)
    for c in range(8):
        b, r = divmod(c, 2)
        out[b][_positions(r)] = np.asarray(results[c]["y_out"]).T
    return out



GLOBAL_SEMS = {"gst", "mld", "xfl", "xpl", "xchm"}


class Ctx:
    def __init__(self, nc, es):
        self.nc = nc
        self.es = es
        self.sb = {}
        self.dram = {}
        self.sems = {}
        self.regs = {}
        self.magic_seq = 0
        self.gst = 0
        self.scoped_es = None
        self.group_sems = []

    def sem(self, name):
        if name not in self.sems:
            if self.scoped_es is not None and name not in GLOBAL_SEMS:
                self.sems[name] = self.scoped_es.enter_context(
                    self.nc.semaphore(name))
                self.group_sems.append(name)
            else:
                self.sems[name] = self.es.enter_context(self.nc.semaphore(name))
        return self.sems[name]

    def begin_group(self):
        self.scoped_es = ExitStack()
        self.group_sems = []

    def end_group(self):
        nc = self.nc
        self._clr = getattr(self, "_clr", 0) + 1
        sems = [self.sems[n] for n in self.group_sems]
        if sems:
            with nc.Block(f"semclr{self._clr}") as blk:
                @blk.sync
                def _(sy):
                    for s in sems:
                        sy.sem_clear(s)
        for n in self.group_sems:
            del self.sems[n]
        self.scoped_es.close()
        self.scoped_es = None
        self.group_sems = []


def _wblock(wdram, jb, k0=0, kn=None):
    kn = kn if kn is not None else wdram.shape[1] - k0
    return wdram.ap()[jb * 128:(jb + 1) * 128, k0:k0 + kn]



class LN:

    def __init__(self, ctx, tag, stream, ones_sb, sq_buf, rstd_sb, rstd2_sb,
                 ps_sum, ps_var, rdy, dst, sum_done=False, ve_rdy=None):
        self.ctx, self.tag = ctx, tag
        self.stream, self.ones, self.sq, self.rstd = stream, ones_sb, sq_buf, rstd_sb
        self.rstd2 = rstd2_sb
        self.ps_sum, self.ps_var = ps_sum, ps_var
        self.rdy_sem, self.rdy_val = rdy
        self.ve_rdy = ve_rdy
        self.dst = dst
        self.sum_done = sum_done
        for s in ("sum", "t1", "sq", "varj", "var", "ssq", "nm", "rs", "vb", "vc"):
            ctx.sem(f"{tag}_{s}")

    def te(self, te):
        c, tag = self.ctx, self.tag
        lvl = LN_LVL[LN_UPTO]
        if lvl == 36:
            lvl = 3
        if not self.sum_done:
            for j in range(NCH):
                te.wait_ge(c.sems[self.rdy_sem], self.rdy_val(j))
                for half in range(2):
                    mm = te.matmul(
                        self.ps_sum[:, half * 512:(half + 1) * 512],
                        self.ones[:, :],
                        self.stream[j][:, half * 512:(half + 1) * 512],
                        start=(j == 0), stop=(j == NCH - 1))
                if j == NCH - 1:
                    mm.then_inc(c.sems[f"{tag}_sum"], 1)
        if lvl < 3:
            return
        for j in range(NCH):
            te.wait_ge(c.sems[f"{tag}_sq"], j + 1)
            for half in range(2):
                mm = te.matmul(
                    self.ps_var[:, half * 512:(half + 1) * 512],
                    self.ones[:, :],
                    self.sq[j % 2][:, half * 512:(half + 1) * 512],
                    start=(j == 0), stop=(j == NCH - 1))
            mm.then_inc(c.sems[f"{tag}_varj"], 1)

    def ve(self, ve):
        c, tag = self.ctx, self.tag
        lvl = LN_LVL[LN_UPTO]
        for j in range(NCH):
            if j == 0 and not self.sum_done:
                ve.wait_ge(c.sems[f"{tag}_sum"], 1)
            if self.ve_rdy is not None:
                sems, vfn = self.ve_rdy
                ve.wait_ge(sems[j % 2], vfn(j))
            ve.scalar_tensor_tensor(
                self.stream[j][:, :], self.ps_sum[:, :], -1.0 / D,
                self.stream[j][:, :], ALU.mult, ALU.add,
            ).then_inc(c.sems[f"{tag}_t1"], 1)
        if lvl < 9:
            if lvl >= 2:
                ve.wait_ge(c.sems[f"{tag}_sq"], NCH)
            if lvl >= 4 or lvl == 35:
                ve.tensor_copy(self.rstd2[:, :], self.rstd[:, :]).then_inc(
                    c.sems[f"{tag}_vb"], 1)
            if lvl == 36:
                ve.engine_nop()
            if lvl >= 4:
                ve.wait_ge(c.sems[f"{tag}_ssq"], 1)
            if lvl >= 5:
                ve.reciprocal(self.rstd[:, :], self.rstd2[:, :]).then_inc(
                    c.sems[f"{tag}_rs"], 1)
                ve.wait_ge(c.sems[f"{tag}_rs"], 1)
            for j in range(NCH):
                ve.sem_inc(c.sems[f"{tag}_nm"], 1)
            return
        ve.wait_ge(c.sems[f"{tag}_varj"], NCH)
        eps_b = c.sb["eps"][:, 0:1].broadcast_to([128, T])
        ve.scalar_tensor_tensor(
            self.rstd[:, :], self.ps_var[:, :], 1.0 / D, eps_b,
            ALU.mult, ALU.add).then_inc(c.sems[f"{tag}_vb"], 1)
        ve.wait_ge(c.sems[f"{tag}_ssq"], 1)
        ve.reciprocal(self.rstd[:, :], self.rstd2[:, :]).then_inc(
            c.sems[f"{tag}_rs"], 1)
        ve.wait_ge(c.sems[f"{tag}_rs"], 1)
        for j in range(NCH):
            ve.tensor_mul(self.stream[j][:, :], self.stream[j][:, :],
                          self.rstd[:, :]).then_inc(c.sems[f"{tag}_nm"], 1)

    def sc(self, sc):
        c, tag = self.ctx, self.tag
        lvl = LN_LVL[LN_UPTO]
        if lvl in (35, 36):
            lvl = 3
        if lvl < 2:
            return
        for j in range(NCH):
            sc.wait_ge(c.sems[f"{tag}_t1"], j + 1)
            if j >= 2 and lvl >= 3:
                sc.wait_ge(c.sems[f"{tag}_varj"], j - 1)
            sc.activation(self.sq[j % 2][:, :], self.stream[j][:, :],
                          AF.Square).then_inc(c.sems[f"{tag}_sq"], 1)
        if lvl < 4:
            return
        sc.wait_ge(c.sems[f"{tag}_vb"], 1)
        sc.sqrt(self.rstd2[:, :], self.rstd[:, :]).then_inc(
            c.sems[f"{tag}_ssq"], 1)

    def store(self, sy):
        c, tag = self.ctx, self.tag
        for j in range(NCH):
            sy.wait_ge(c.sems[f"{tag}_nm"], j + 1)
            sy.dma_start(out=self.dst.ap()[j * 128:(j + 1) * 128, :],
                         in_=self.stream[j][:, :]).then_inc(c.sem("gst"), 16)
        c.gst += 16 * NCH



def _group_l0_attn(ctx, src, dst):
    nc = ctx.nc
    w = ctx.dram["w"]
    shr = ctx.dram["shr"]
    SQ = 1.0 / math.sqrt(128.0)
    NBLK = 48
    NIT = NH0 * 8

    with ExitStack() as ph:
        e = ph.enter_context
        stream = [e(nc.sbuf_tensor(f"st{c}", [128, T], F32R)) for c in range(NCH)]
        o_f = [e(nc.sbuf_tensor(f"o{c}", [128, T], F32R)) for c in range(NCH)]
        bias_sb = e(nc.sbuf_tensor("bias4", [128, 64], F32))
        bqs_sb = e(nc.sbuf_tensor("bqs", [128, 16], F32))
        ident = e(nc.sbuf_tensor("ident", [128, 128], BF16))
        ones_sb = ctx.sb["ones"]
        ph12 = ph.enter_context(ExitStack())
        e12 = ph12.enter_context
        q_bf = [e12(nc.sbuf_tensor(f"q{c}", [128, T], BF16)) for c in range(NCH)]

        gst0 = ctx.gst
        for s in ("p1_pre", "p1_wA", "p1_wB", "p1_mm", "p1_ev", "p1_vtr",
                  "p1_vte", "p1_kwrA", "p1_kwrB", "p1_vwrA", "p1_vwrB",
                  "p1_xch"):
            ctx.sem(s)
        p1_w = [ctx.sems["p1_wA"], ctx.sems["p1_wB"]]
        p1_kwr = [ctx.sems["p1_kwrA"], ctx.sems["p1_kwrB"]]
        p1_vwr = [ctx.sems["p1_vwrA"], ctx.sems["p1_vwrB"]]

        ph1 = ph12.enter_context(ExitStack())
        e1 = ph1.enter_context
        wbuf = [e1(nc.sbuf_tensor(f"wb{i}", [128, D], F32R)) for i in range(2)]
        kbuf = [e1(nc.sbuf_tensor(f"kb{i}", [128, T], BF16)) for i in range(2)]
        vbuf = [e1(nc.sbuf_tensor(f"vb{i}", [128, T], BF16)) for i in range(2)]
        vt_sb = [e1(nc.sbuf_tensor(f"vt{i}", [128, T], BF16)) for i in range(2)]
        ps_a = e1(nc.psum_tensor("ps_a", [128, T], F32))
        ps_b = e1(nc.psum_tensor("ps_b", [128, T], F32))
        ps_t = e1(nc.psum_tensor("ps_t", [128, T], BF16))
        psq = [ps_a, ps_b]
        with nc.Block("l0_qkv") as blk:
            @blk.sync
            def _(sy):
                if gst0:
                    sy.wait_ge(ctx.sems["gst"], gst0)
                for c in range(NCH):
                    sy.dma_start(out=stream[c][:, :],
                                 in_=src.ap()[c * 128:(c + 1) * 128, :]
                                 ).then_inc(ctx.sems["p1_pre"], 16)
                sy.dma_start(out=ident[:, :],
                             in_=w["ident_bf"][:, :]).then_inc(ctx.sems["p1_pre"], 16)
                for bi, bn in enumerate(("l0_bq", "l0_bk", "l0_bv", "l0_bo")):
                    sy.dma_start(out=bias_sb[:, bi * 16:(bi + 1) * 16],
                                 in_=w[bn][:, :]).then_inc(ctx.sems["p1_pre"], 16)
                for i in range(NBLK):
                    mat = ("l0_wq", "l0_wk", "l0_wv")[i // 16]
                    if i >= 2:
                        sy.wait_ge(ctx.sems["p1_mm"], i - 1)
                    sy.dma_start(out=wbuf[i % 2][:, :],
                                 in_=_wblock(w[mat], i % 16)
                                 ).then_inc(p1_w[i % 2], 16)

            @blk.tensor
            def _(te):
                def trans(ci):
                    te.wait_ge(ctx.sems["p1_ev"], 33 + ci)
                    if ci >= 1:
                        te.wait_ge(ctx.sems["p1_vte"], ci)
                    for tt in range(8):
                        mm = te.transpose(ps_t[:, tt * 128:(tt + 1) * 128],
                                          vbuf[ci % 2][:, tt * 128:(tt + 1) * 128],
                                          ident[:, :])
                    mm.then_inc(ctx.sems["p1_vtr"], 1)

                for i in range(NBLK):
                    if i == 0:
                        te.wait_ge(ctx.sems["p1_pre"], 16 * 21)
                    te.wait_ge(p1_w[i % 2], 16 * (i // 2 + 1))
                    if i >= 2:
                        te.wait_ge(ctx.sems["p1_ev"], i - 1)
                    for k in range(NCH):
                        lhs = wbuf[i % 2][:, k * 128:(k + 1) * 128]
                        for half in range(2):
                            mm = te.matmul(
                                psq[i % 2][:, half * 512:(half + 1) * 512],
                                lhs, stream[k][:, half * 512:(half + 1) * 512],
                                start=(k == 0), stop=(k == NCH - 1))
                    mm.then_inc(ctx.sems["p1_mm"], 1)
                    if i >= 34:
                        trans(i - 34)
                trans(14)
                trans(15)

            @blk.scalar
            def _(sc):
                sc.wait_ge(ctx.sems["p1_pre"], 16 * 21)
                sc.mul(bqs_sb[:, :], bias_sb[:, 0:16], SQ).then_inc(
                    ctx.sem("p1_bq"), 1)
                sc.wait_ge(ctx.sems["p1_bq"], 1)
                for i in range(NBLK):
                    sc.wait_ge(ctx.sems["p1_mm"], i + 1)
                    if i < 16:
                        sc.activation(q_bf[i][:, :], psq[i % 2][:, :], AF.Identity,
                                      bias=bqs_sb[:, i:i + 1], scale=SQ
                                      ).then_inc(ctx.sems["p1_ev"], 1)
                    elif i < 32:
                        ci = i - 16
                        if ci >= 2:
                            sc.wait_ge(p1_kwr[ci % 2], 16 * ((ci - 2) // 2 + 1))
                        sc.activation(kbuf[ci % 2][:, :], psq[i % 2][:, :], AF.Identity,
                                      bias=bias_sb[:, 16 + ci:17 + ci], scale=1.0
                                      ).then_inc(ctx.sems["p1_ev"], 1)
                    else:
                        ci = i - 32
                        if ci >= 2:
                            sc.wait_ge(ctx.sems["p1_vtr"], ci - 1)
                        sc.activation(vbuf[ci % 2][:, :], psq[i % 2][:, :], AF.Identity,
                                      bias=bias_sb[:, 32 + ci:33 + ci], scale=1.0
                                      ).then_inc(ctx.sems["p1_ev"], 1)

            @blk.vector
            def _(ve):
                for ci in range(16):
                    ve.wait_ge(ctx.sems["p1_vtr"], ci + 1)
                    if ci >= 2:
                        ve.wait_ge(p1_vwr[ci % 2], 16 * ((ci - 2) // 2 + 1))
                    ve.tensor_copy(vt_sb[ci % 2][:, :], ps_t[:, :]).then_inc(
                        ctx.sems["p1_vte"], 1)

            @blk.gpsimd
            def _(gp):
                ctx.magic_seq += 1
                magic = MAGIC_BASE + ctx.magic_seq

                def emit(slot):
                    for ci in range(16):
                        gp.wait_ge(ctx.sems["p1_ev"], 17 + ci)
                        out = shr["k0"].ap()[slot * 2048 + ci * 128:
                                             slot * 2048 + (ci + 1) * 128, :]
                        gp.dma_start(out=out, in_=kbuf[ci % 2][:, :]).then_inc(
                            p1_kwr[ci % 2], 16)
                    for ci in range(16):
                        gp.wait_ge(ctx.sems["p1_vte"], ci + 1)
                        out = shr["v0"].ap()[slot * 128:(slot + 1) * 128,
                                             ci * 1024:(ci + 1) * 1024]
                        gp.dma_start(out=out, in_=vt_sb[ci % 2][:, :]).then_inc(
                            p1_vwr[ci % 2], 16)
                    gp.wait_ge(p1_kwr[0], 16 * 8)
                    gp.wait_ge(p1_kwr[1], 16 * 8)
                    gp.wait_ge(p1_vwr[0], 16 * 8)
                    gp.wait_ge(p1_vwr[1], 16 * 8)
                    _exchange_flag(gp, ctx, slot, magic)
                with gp.If_eq(ctx.regs["rank"], 0):
                    emit(0)
                with gp.Else():
                    emit(1)
                _exchange_poll(gp, ctx, magic)
                gp.sem_inc(ctx.sems["p1_xch"], 1)

        ph1.close()

        if STOP_AFTER == "p1":
            with nc.Block("dbg_store") as blk:
                @blk.sync
                def _(sy):
                    for j in range(NCH):
                        sy.dma_start(out=dst.ap()[j * 128:(j + 1) * 128, :],
                                     in_=stream[j][:, :]).then_inc(ctx.sem("gst"), 16)
            ctx.gst += 16 * NCH
            return

        with ExitStack() as ph2:
            e2 = ph2.enter_context
            kh = [e2(nc.sbuf_tensor(f"kh{i}", [128, 2048], BF16)) for i in range(2)]
            vh = [e2(nc.sbuf_tensor(f"vh{i}", [128, 2048], BF16)) for i in range(2)]
            p_bf = [e2(nc.sbuf_tensor(f"pb{i}", [128, 2048], BF16)) for i in range(2)]
            pt_sb = [e2(nc.sbuf_tensor(f"pt{i}", [128, 2048], BF16)) for i in range(2)]
            negmax = [e2(nc.sbuf_tensor(f"ngm{i}", [128, 1], F32)) for i in range(2)]
            sumexp = [e2(nc.sbuf_tensor(f"sme{i}", [128, 1], F32)) for i in range(2)]
            rsum = [e2(nc.sbuf_tensor(f"rsm{i}", [128, 1], F32)) for i in range(2)]
            ps_sc = e2(nc.psum_tensor("ps_sc", [128, 2048], F32))
            ps_pt = e2(nc.psum_tensor("ps_pt", [128, 2048], BF16))
            ps_o = [e2(nc.psum_tensor(f"ps_o{i}", [128, 512], F32)) for i in range(2)]

            for s in ("p2_ldA", "p2_ldB", "p2_sc", "p2_mx", "p2_ex", "p2_pn",
                      "p2_tr", "p2_pte", "p2_mm", "p2_oe", "p2_rc"):
                ctx.sem(s)
            p2_ld = [ctx.sems["p2_ldA"], ctx.sems["p2_ldB"]]

            with nc.Block("l0_attn") as blk:
                @blk.sync
                def _(sy):
                    sy.wait_ge(ctx.sems["p1_xch"], 1)
                    for h in range(NH0):
                        if h >= 2:
                            sy.wait_ge(ctx.sems["p2_mm"], 8 * (h - 1))
                        for slot in range(2):
                            sy.dma_start(
                                out=kh[h % 2][:, slot * 1024:(slot + 1) * 1024],
                                in_=shr["k0"].ap()[slot * 2048 + h * 128:
                                                   slot * 2048 + (h + 1) * 128, :],
                            ).then_inc(p2_ld[h % 2], 16)
                        for slot in range(2):
                            sy.dma_start(
                                out=vh[h % 2][:, slot * 1024:(slot + 1) * 1024],
                                in_=shr["v0"].ap()[slot * 128:(slot + 1) * 128,
                                                   h * 1024:(h + 1) * 1024],
                            ).then_inc(p2_ld[h % 2], 16)

                def _S4(te, it):
                    h, par = it // 8, it % 2
                    te.wait_ge(ctx.sems["p2_pn"], it + 1)
                    for kt in range(16):
                        mm = te.transpose(ps_pt[:, kt * 128:(kt + 1) * 128],
                                          p_bf[par][:, kt * 128:(kt + 1) * 128],
                                          ident[:, :])
                    mm.then_inc(ctx.sems["p2_tr"], 1)

                def _S6(te, it):
                    h, par = it // 8, it % 2
                    te.wait_ge(ctx.sems["p2_pte"], it + 1)
                    if it >= 2:
                        te.wait_ge(ctx.sems["p2_oe"], it - 1)
                    for kt in range(16):
                        mm = te.matmul(ps_o[par][:, 0:128],
                                       vh[h % 2][:, kt * 128:(kt + 1) * 128],
                                       pt_sb[par][:, kt * 128:(kt + 1) * 128],
                                       start=(kt == 0), stop=(kt == 15))
                    mm.then_inc(ctx.sems["p2_mm"], 1)

                @blk.tensor
                def _(te):
                    for it in range(NIT):
                        h, qt = divmod(it, 8)
                        if qt == 0:
                            te.wait_ge(p2_ld[h % 2], 64 * (h // 2 + 1))
                        if it >= 1:
                            te.wait_ge(ctx.sems["p2_ex"], it)
                        for kb in range(4):
                            mm = te.matmul(ps_sc[:, kb * 512:(kb + 1) * 512],
                                           q_bf[h][:, qt * 128:(qt + 1) * 128],
                                           kh[h % 2][:, kb * 512:(kb + 1) * 512],
                                           start=True, stop=True)
                        mm.then_inc(ctx.sems["p2_sc"], 1)
                        if it >= 1:
                            _S4(te, it - 1)
                            _S6(te, it - 1)
                    _S4(te, NIT - 1)
                    _S6(te, NIT - 1)

                @blk.vector
                def _(ve):
                    for it in range(NIT):
                        par = it % 2
                        ve.wait_ge(ctx.sems["p2_sc"], it + 1)
                        if it >= 2:
                            ve.wait_ge(ctx.sems["p2_ex"], it - 1)
                        ve.reduce_max(negmax[par][:, :], ps_sc[:, :], axis=AX.X,
                                      negate=True).then_inc(ctx.sems["p2_mx"], 1)
                        if it >= 1:
                            parp = (it - 1) % 2
                            ve.wait_ge(ctx.sems["p2_ex"], it)
                            ve.reciprocal(rsum[parp][:, :], sumexp[parp][:, :]
                                          ).then_inc(ctx.sems["p2_rc"], 1)
                            ve.wait_ge(ctx.sems["p2_rc"], it)
                            ve.tensor_scalar_mul(
                                p_bf[parp][:, :], p_bf[parp][:, :],
                                rsum[parp][:, 0:1]).then_inc(ctx.sems["p2_pn"], 1)
                    parp = (NIT - 1) % 2
                    ve.wait_ge(ctx.sems["p2_ex"], NIT)
                    ve.reciprocal(rsum[parp][:, :], sumexp[parp][:, :]).then_inc(
                        ctx.sems["p2_rc"], 1)
                    ve.wait_ge(ctx.sems["p2_rc"], NIT)
                    ve.tensor_scalar_mul(p_bf[parp][:, :], p_bf[parp][:, :],
                                         rsum[parp][:, 0:1]).then_inc(
                        ctx.sems["p2_pn"], 1)

                def _S5(sc, it):
                    sc.wait_ge(ctx.sems["p2_tr"], it + 1)
                    sc.copy(pt_sb[it % 2][:, :], ps_pt[:, :]).then_inc(
                        ctx.sems["p2_pte"], 1)

                def _S7(sc, it):
                    h, qt = divmod(it, 8)
                    sc.wait_ge(ctx.sems["p2_mm"], it + 1)
                    sc.copy(o_f[h][:, qt * 128:(qt + 1) * 128],
                            ps_o[it % 2][:, 0:128]).then_inc(ctx.sems["p2_oe"], 1)

                @blk.scalar
                def _(sc):
                    for it in range(NIT):
                        par = it % 2
                        sc.wait_ge(ctx.sems["p2_mx"], it + 1)
                        if it >= 2:
                            sc.wait_ge(ctx.sems["p2_tr"], it - 1)
                        sc.activation(p_bf[par][:, :], ps_sc[:, :], AF.Exp,
                                      bias=negmax[par][:, 0:1], scale=1.0,
                                      accum_out=sumexp[par][:, :]).then_inc(
                            ctx.sems["p2_ex"], 1)
                        if it >= 1:
                            _S5(sc, it - 1)
                        if it >= 2:
                            _S7(sc, it - 2)
                    _S5(sc, NIT - 1)
                    _S7(sc, NIT - 2)
                    _S7(sc, NIT - 1)

        ph12.close()

        if STOP_AFTER == "p2":
            with nc.Block("dbg_store2") as blk:
                @blk.sync
                def _(sy):
                    for j in range(NCH):
                        sy.dma_start(out=dst.ap()[j * 128:(j + 1) * 128, :],
                                     in_=o_f[j][:, :]).then_inc(ctx.sem("gst"), 16)
            ctx.gst += 16 * NCH
            return

        with ExitStack() as ph3:
            e3 = ph3.enter_context
            wbuf = [e3(nc.sbuf_tensor(f"wb3{i}", [128, D], F32R)) for i in range(2)]
            sq_buf = [e3(nc.sbuf_tensor(f"sq{i}", [128, T], F32R)) for i in range(2)]
            rstd_sb = e3(nc.sbuf_tensor("rstd", [128, T], F32))
            rstd2_sb = e3(nc.sbuf_tensor("rstd2", [128, T], F32))
            ps_sum = e3(nc.psum_tensor("ps_sum", [128, T], F32))
            ps_var = e3(nc.psum_tensor("ps_var", [128, T], F32))
            psq = [e3(nc.psum_tensor(f"ps3_{i}", [128, T], F32)) for i in range(2)]

            for s in ("p3_wA", "p3_wB", "p3_mm", "p3_res"):
                ctx.sem(s)
            p3_w = [ctx.sems["p3_wA"], ctx.sems["p3_wB"]]
            ln = LN(ctx, "ln1", stream, ones_sb, sq_buf, rstd_sb, rstd2_sb,
                    ps_sum, ps_var, ("p3_res", lambda j: j + 1), dst)

            skip_ln = STOP_AFTER == "p3res"
            with nc.Block("l0_wo") as blk:
                @blk.sync
                def _(sy):
                    for j in range(NCH):
                        if j >= 2:
                            sy.wait_ge(ctx.sems["p3_mm"], j - 1)
                        sy.dma_start(out=wbuf[j % 2][:, :],
                                     in_=_wblock(w["l0_wo"], j)
                                     ).then_inc(p3_w[j % 2], 16)
                    if skip_ln:
                        for j in range(NCH):
                            sy.wait_ge(ctx.sems["p3_res"], j + 1)
                            sy.dma_start(out=dst.ap()[j * 128:(j + 1) * 128, :],
                                         in_=stream[j][:, :]).then_inc(
                                ctx.sem("gst"), 16)
                        ctx.gst += 16 * NCH
                    else:
                        ln.store(sy)

                @blk.tensor
                def _(te):
                    for j in range(NCH):
                        te.wait_ge(p3_w[j % 2], 16 * (j // 2 + 1))
                        if j >= 2:
                            te.wait_ge(ctx.sems["p3_res"], j - 1)
                        for oc in range(NCH):
                            lhs = wbuf[j % 2][:, oc * 128:(oc + 1) * 128]
                            for half in range(2):
                                mm = te.matmul(
                                    psq[j % 2][:, half * 512:(half + 1) * 512],
                                    lhs, o_f[oc][:, half * 512:(half + 1) * 512],
                                    start=(oc == 0),
                                    stop=(oc == NCH - 1))
                        mm.then_inc(ctx.sems["p3_mm"], 1)
                    if not skip_ln:
                        ln.te(te)

                @blk.vector
                def _(ve):
                    for j in range(NCH):
                        ve.wait_ge(ctx.sems["p3_mm"], j + 1)
                        ve.scalar_tensor_tensor(
                            stream[j][:, :], psq[j % 2][:, :],
                            bias_sb[:, 48 + j:49 + j], stream[j][:, :],
                            ALU.add, ALU.add).then_inc(ctx.sems["p3_res"], 1)
                    if not skip_ln:
                        ln.ve(ve)

                @blk.scalar
                def _(sc):
                    if not skip_ln:
                        ln.sc(sc)




def _group_l0_ffn(ctx, src, dst):
    nc = ctx.nc
    w = ctx.dram["w"]
    spill = ctx.dram["spill"]
    NHC = HFF0 // 128

    with ExitStack() as ph:
        e = ph.enter_context
        stat_acc = e(nc.sbuf_tensor("stat6", [128, T], F32))
        ph_h = ph.enter_context(ExitStack())
        h = [ph_h.enter_context(nc.sbuf_tensor(f"h{c}", [128, T], BF16))
             for c in range(NHC)]

        with ExitStack() as p4:
            e4 = p4.enter_context
            xh = [e4(nc.sbuf_tensor(f"xh{c}", [128, 512], F32R)) for c in range(NCH)]
            w1b = [e4(nc.sbuf_tensor(f"w1b{i}", [128, D], F32R)) for i in range(2)]
            w3b = [e4(nc.sbuf_tensor(f"w3b{i}", [128, D], F32R)) for i in range(2)]
            gs = [e4(nc.sbuf_tensor(f"gs{i}", [128, 512], F32)) for i in range(2)]
            ps_g = [e4(nc.psum_tensor(f"ps_g{i}", [128, 512], F32)) for i in range(2)]
            ps_u = [e4(nc.psum_tensor(f"ps_u{i}", [128, 512], F32)) for i in range(2)]
            for s in ("p4_pre", "p4_wA", "p4_wB", "p4_mm", "p4_sl", "p4_ev"):
                ctx.sem(s)
            p4_w = [ctx.sems["p4_wA"], ctx.sems["p4_wB"]]
            gst0 = ctx.gst

            with nc.Block("l0_up") as blk:
                @blk.sync
                def _(sy):
                    for hf in range(2):
                        if hf == 0 and gst0:
                            sy.wait_ge(ctx.sems["gst"], gst0)
                        if hf == 1:
                            sy.wait_ge(ctx.sems["p4_mm"], NHC)
                        for c in range(NCH):
                            sy.dma_start(
                                out=xh[c][:, :],
                                in_=src.ap()[c * 128:(c + 1) * 128,
                                             hf * 512:(hf + 1) * 512],
                            ).then_inc(ctx.sems["p4_pre"], 16)
                        for hc in range(NHC):
                            g = hf * NHC + hc
                            if g >= 2:
                                sy.wait_ge(ctx.sems["p4_mm"], g - 1)
                            sy.dma_start(out=w1b[g % 2][:, :],
                                         in_=_wblock(w["l0_w1"], hc)
                                         ).then_inc(p4_w[g % 2], 16)
                            sy.dma_start(out=w3b[g % 2][:, :],
                                         in_=_wblock(w["l0_w3"], hc)
                                         ).then_inc(p4_w[g % 2], 16)

                @blk.tensor
                def _(te):
                    for hf in range(2):
                        for hc in range(NHC):
                            g = hf * NHC + hc
                            if hc == 0:
                                te.wait_ge(ctx.sems["p4_pre"], 16 * 16 * (hf + 1))
                            te.wait_ge(p4_w[g % 2], 32 * (g // 2 + 1))
                            if g >= 2:
                                te.wait_ge(ctx.sems["p4_ev"], g - 1)
                            for k in range(NCH):
                                mm = te.matmul(ps_g[g % 2][:, :],
                                               w1b[g % 2][:, k * 128:(k + 1) * 128],
                                               xh[k][:, :],
                                               start=(k == 0), stop=(k == NCH - 1))
                            for k in range(NCH):
                                mm = te.matmul(ps_u[g % 2][:, :],
                                               w3b[g % 2][:, k * 128:(k + 1) * 128],
                                               xh[k][:, :],
                                               start=(k == 0), stop=(k == NCH - 1))
                            mm.then_inc(ctx.sems["p4_mm"], 1)

                @blk.scalar
                def _(sc):
                    for g in range(2 * NHC):
                        sc.wait_ge(ctx.sems["p4_mm"], g + 1)
                        if g >= 2:
                            sc.wait_ge(ctx.sems["p4_ev"], g - 1)
                        sc.activation(gs[g % 2][:, :], ps_g[g % 2][:, :],
                                      AF.Silu).then_inc(ctx.sems["p4_sl"], 1)

                @blk.vector
                def _(ve):
                    for hf in range(2):
                        for hc in range(NHC):
                            g = hf * NHC + hc
                            ve.wait_ge(ctx.sems["p4_sl"], g + 1)
                            ve.tensor_mul(
                                h[hc][:, hf * 512:(hf + 1) * 512],
                                gs[g % 2][:, :], ps_u[g % 2][:, :],
                            ).then_inc(ctx.sems["p4_ev"], 1)

        if "uponly" in G2_SKIP:
            with nc.Block("dbg_g2") as blk:
                @blk.sync
                def _(sy):
                    for j in range(NCH):
                        sy.dma_start(out=dst.ap()[j * 128:(j + 1) * 128, :],
                                     in_=src.ap()[j * 128:(j + 1) * 128, :]
                                     ).then_inc(ctx.sem("gst"), 16)
            ctx.gst += 16 * NCH
            return

        if "fwht" in G2_SKIP:
            p5 = None
        with ExitStack() as p5:
            e5 = p5.enter_context
            ta = e5(nc.sbuf_tensor("fw_ta", [128, T], BF16))
            tb = e5(nc.sbuf_tensor("fw_tb", [128, T], BF16))
            w128sb = e5(nc.sbuf_tensor("w128sb", [128, 128], BF16))
            ps_w = [e5(nc.psum_tensor(f"ps_w{i}", [128, T], F32)) for i in range(2)]
            for s in ("p5_pre", "p5_ch", "p5_mm", "p5_ev"):
                ctx.sem(s)

            with nc.Block("l0_fwht") as blk:
                if "fwht" in G2_SKIP:
                    blk
                @blk.sync
                def _(sy):
                    if "fwht" in G2_SKIP:
                        return
                    sy.dma_start(out=w128sb[:, :],
                                 in_=w["w128_bf"][:, :]).then_inc(
                        ctx.sems["p5_pre"], 16)

                @blk.vector
                def _(ve):
                    if "fwht" in G2_SKIP:
                        return
                    ch = ctx.sems["p5_ch"]
                    n = 0
                    for s in range(6):
                        step = 1 << s
                        pairs = [(a, a + step) for a in range(NHC)
                                 if (a // step) % 2 == 0]
                        for (a, b) in pairs:
                            ve.tensor_add(ta[:, :], h[a][:, :], h[b][:, :]
                                          ).then_inc(ch, 1)
                            ve.tensor_sub(tb[:, :], h[a][:, :], h[b][:, :]
                                          ).then_inc(ch, 1)
                            n += 2
                            ve.wait_ge(ch, n)
                            ve.tensor_copy(h[a][:, :], ta[:, :]).then_inc(ch, 1)
                            ve.tensor_copy(h[b][:, :], tb[:, :]).then_inc(ch, 1)
                            n += 2
                            ve.wait_ge(ch, n)
                    for hc in range(NHC):
                        ve.wait_ge(ctx.sems["p5_mm"], hc + 1)
                        ve.tensor_copy(h[hc][:, :], ps_w[hc % 2][:, :]).then_inc(
                            ctx.sems["p5_ev"], 1)

                @blk.tensor
                def _(te):
                    if "fwht" in G2_SKIP:
                        return
                    te.wait_ge(ctx.sems["p5_pre"], 16)
                    te.wait_ge(ctx.sems["p5_ch"], 4 * 6 * 32)
                    for hc in range(NHC):
                        if hc >= 2:
                            te.wait_ge(ctx.sems["p5_ev"], hc - 1)
                        for half in range(2):
                            mm = te.matmul(
                                ps_w[hc % 2][:, half * 512:(half + 1) * 512],
                                w128sb[:, :],
                                h[hc][:, half * 512:(half + 1) * 512],
                                start=True, stop=True)
                        mm.then_inc(ctx.sems["p5_mm"], 1)

        with ExitStack() as p6:
            e6 = p6.enter_context
            w2r = [e6(nc.sbuf_tensor(f"w2r{i}", [128, D], F32R)) for i in range(2)]
            w2q = [e6(nc.sbuf_tensor(f"w2q{i}", [128, D], BF16)) for i in range(2)]
            xres = [e6(nc.sbuf_tensor(f"xr{i}", [128, T], F32R)) for i in range(2)]
            ybuf = [e6(nc.sbuf_tensor(f"yb{i}", [128, T], F32R)) for i in range(2)]
            ps_y = [e6(nc.psum_tensor(f"ps_y{i}", [128, T], F32)) for i in range(2)]
            ps_stat = e6(nc.psum_tensor("ps_stat", [128, T], F32))
            for s in ("p6_wA", "p6_wB", "p6_xA", "p6_xB", "p6_c", "p6_mmq",
                      "p6_sj", "p6_res", "p6_stA", "p6_stB", "p6_sums"):
                ctx.sem(s)
            p6_w = [ctx.sems["p6_wA"], ctx.sems["p6_wB"]]
            p6_x = [ctx.sems["p6_xA"], ctx.sems["p6_xB"]]
            p6_st = [ctx.sems["p6_stA"], ctx.sems["p6_stB"]]
            SC128 = 1.0 / math.sqrt(float(HFF0))

            with nc.Block("l0_down") as blk:
                @blk.sync
                def _(sy):
                    for j in range(NCH):
                        if j >= 2:
                            sy.wait_ge(ctx.sems["p6_res"], j - 1)
                        sy.dma_start(out=xres[j % 2][:, :],
                                     in_=src.ap()[j * 128:(j + 1) * 128, :]
                                     ).then_inc(p6_x[j % 2], 16)
                        for q in range(4):
                            gq = j * 4 + q
                            if gq >= 2:
                                sy.wait_ge(ctx.sems["p6_c"], gq - 1)
                            sy.dma_start(
                                out=w2r[gq % 2][:, :],
                                in_=_wblock(w["l0_w2"], j, q * 2048, 2048),
                            ).then_inc(p6_w[gq % 2], 16)
                        sy.wait_ge(ctx.sems["p6_res"], j + 1)
                        sy.dma_start(out=spill.ap()[j * 128:(j + 1) * 128, :],
                                     in_=ybuf[j % 2][:, :]).then_inc(
                            p6_st[j % 2], 16)

                @blk.vector
                def _(ve):
                    for j in range(NCH):
                        for q in range(4):
                            gq = j * 4 + q
                            ve.wait_ge(p6_w[gq % 2], 16 * (gq // 2 + 1))
                            if gq >= 2:
                                ve.wait_ge(ctx.sems["p6_mmq"], gq - 1)
                            ve.tensor_scalar_mul(w2q[gq % 2][:, :],
                                                 w2r[gq % 2][:, :], SC128
                                                 ).then_inc(ctx.sems["p6_c"], 1)
                        ve.wait_ge(ctx.sems["p6_mmq"], 4 * (j + 1))
                        if j >= 2:
                            ve.wait_ge(p6_st[j % 2], 16 * ((j - 2) // 2 + 1))
                            ve.wait_ge(ctx.sems["p6_sums"], j - 1)
                        ve.wait_ge(p6_x[j % 2], 16 * (j // 2 + 1))
                        ve.scalar_tensor_tensor(
                            ybuf[j % 2][:, :], ps_y[j % 2][:, :], 1.0,
                            xres[j % 2][:, :], ALU.mult, ALU.add,
                        ).then_inc(ctx.sems["p6_res"], 1)
                        ve.wait_ge(ctx.sems["p6_sj"], j + 1)
                        if j == 0:
                            ve.tensor_scalar_mul(stat_acc[:, :], ps_stat[:, :],
                                                 1.0).then_inc(
                                ctx.sems["p6_sums"], 1)
                        else:
                            ve.tensor_add(stat_acc[:, :], stat_acc[:, :],
                                          ps_stat[:, :]).then_inc(
                                ctx.sems["p6_sums"], 1)

                @blk.tensor
                def _(te):
                    for j in range(NCH):
                        for q in range(4):
                            gq = j * 4 + q
                            te.wait_ge(ctx.sems["p6_c"], gq + 1)
                            if j >= 2 and q == 0:
                                te.wait_ge(ctx.sems["p6_res"], j - 1)
                            for f in range(16):
                                hc = q * 16 + f
                                for half in range(2):
                                    mm = te.matmul(
                                        ps_y[j % 2][:, half * 512:(half + 1) * 512],
                                        w2q[gq % 2][:, f * 128:(f + 1) * 128],
                                        h[hc][:, half * 512:(half + 1) * 512],
                                        start=(hc == 0), stop=(hc == NHC - 1))
                            mm.then_inc(ctx.sems["p6_mmq"], 1)
                        te.wait_ge(ctx.sems["p6_res"], j + 1)
                        if j >= 1:
                            te.wait_ge(ctx.sems["p6_sums"], j)
                        for half in range(2):
                            mm = te.matmul(
                                ps_stat[:, half * 512:(half + 1) * 512],
                                ctx.sb["ones"][:, :],
                                ybuf[j % 2][:, half * 512:(half + 1) * 512],
                                start=True, stop=True)
                        mm.then_inc(ctx.sems["p6_sj"], 1)

        ph_h.close()

        if "noln" in G2_SKIP:
            with ExitStack() as pdbg:
                bt = pdbg.enter_context(nc.sbuf_tensor("dbg_bt", [128, T], F32R))
                with nc.Block("dbg_g2c") as blk:
                    @blk.sync
                    def _(sy):
                        sy.wait_ge(p6_st[0], 16 * 8)
                        sy.wait_ge(p6_st[1], 16 * 8)
                        n = 0
                        for j in range(NCH):
                            sy.dma_start(out=bt[:, :],
                                         in_=spill.ap()[j * 128:(j + 1) * 128, :]
                                         ).then_inc(ctx.sem("dbgl"), 16)
                            n += 16
                            sy.wait_ge(ctx.sems["dbgl"], n)
                            sy.dma_start(out=dst.ap()[j * 128:(j + 1) * 128, :],
                                         in_=bt[:, :]).then_inc(ctx.sem("gst"), 16)
                ctx.gst += 16 * NCH
            return

        with ExitStack() as p6b:
            eb = p6b.enter_context
            stream = [eb(nc.sbuf_tensor(f"s6_{c}", [128, T], F32R))
                      for c in range(NCH)]
            sq_buf = [eb(nc.sbuf_tensor(f"sq6_{i}", [128, T], F32R))
                      for i in range(2)]
            rstd_sb = eb(nc.sbuf_tensor("rstd6", [128, T], F32))
            rstd2_sb = eb(nc.sbuf_tensor("rstd6b", [128, T], F32))
            ps_var = eb(nc.psum_tensor("ps6_var", [128, T], F32))
            for s in ("p6b_lA", "p6b_lB"):
                ctx.sem(s)
            p6b_l = [ctx.sems["p6b_lA"], ctx.sems["p6b_lB"]]

            ln = LN(ctx, "ln2", stream, ctx.sb["ones"], sq_buf, rstd_sb,
                    rstd2_sb, stat_acc, ps_var,
                    ("p6b_lA", lambda j: 16 * (j // 2 + 1)), dst, sum_done=True,
                    ve_rdy=(p6b_l, lambda j: 16 * (j // 2 + 1)))

            with nc.Block("l0_ln2") as blk:
                @blk.sync
                def _(sy):
                    sy.wait_ge(p6_st[0], 16 * 8)
                    sy.wait_ge(p6_st[1], 16 * 8)
                    for j in range(NCH):
                        sy.dma_start(out=stream[j][:, :],
                                     in_=spill.ap()[j * 128:(j + 1) * 128, :]
                                     ).then_inc(p6b_l[j % 2], 16)
                    ln.store(sy)

                @blk.tensor
                def _(te):
                    ln.te(te)

                @blk.vector
                def _(ve):
                    ln.ve(ve)

                @blk.scalar
                def _(sc):
                    ln.sc(sc)


def _group_l1_attn(ctx, src, dst):
    nc = ctx.nc
    w = ctx.dram["w"]
    shr = ctx.dram["shr"]
    SQ8 = 1.0 / 8.0

    with ExitStack() as ph:
        e = ph.enter_context
        stream = [e(nc.sbuf_tensor(f"s7_{c}", [128, T], F32R)) for c in range(NCH)]
        ctxh = [e(nc.sbuf_tensor(f"cx{c}", [128, T], BF16)) for c in range(NCH)]
        q_bf = [e(nc.sbuf_tensor(f"q7_{c}", [128, T], BF16)) for c in range(NCH)]
        ident = e(nc.sbuf_tensor("id7", [128, 128], BF16))
        diag_sb = e(nc.sbuf_tensor("dg7", [128, 128], F32))

        with ExitStack() as p7:
            e7 = p7.enter_context
            wbuf = [e7(nc.sbuf_tensor(f"w7_{i}", [128, D], F32R)) for i in range(2)]
            qf = [e7(nc.sbuf_tensor(f"qf{i}", [128, T], F32R)) for i in range(2)]
            kobf = [e7(nc.sbuf_tensor(f"ko{i}", [128, T], BF16)) for i in range(2)]
            tmp1 = e7(nc.sbuf_tensor("rt1", [128, T], F32))
            tmp2 = e7(nc.sbuf_tensor("rt2", [128, T], F32))
            cosb = e7(nc.sbuf_tensor("cosb", [128, T], F32))
            sinb = e7(nc.sbuf_tensor("sinb", [128, T], F32))
            rswb = e7(nc.sbuf_tensor("rswb", [128, 128], F32R))
            ps_q = [e7(nc.psum_tensor(f"pq{i}", [128, T], F32)) for i in range(2)]
            ps_r = [e7(nc.psum_tensor(f"pr{i}", [128, T], F32)) for i in range(2)]
            for s in ("p7_pre", "p7_wA", "p7_wB", "p7_mm", "p7_qe", "p7_rm",
                      "p7_t1", "p7_t2", "p7_qd", "p7_kwrA", "p7_kwrB",
                      "p7_vm", "p7_ve", "p7_vwrA", "p7_vwrB"):
                ctx.sem(s)
            p7_w = [ctx.sems["p7_wA"], ctx.sems["p7_wB"]]
            p7_kwr = [ctx.sems["p7_kwrA"], ctx.sems["p7_kwrB"]]
            p7_vwr = [ctx.sems["p7_vwrA"], ctx.sems["p7_vwrB"]]
            gst0 = ctx.gst
            NB = 20
            ctx.magic_seq += 1
            magic = MAGIC_BASE + ctx.magic_seq

            with nc.Block("l1_qkv") as blk:
                @blk.sync
                def _(sy):
                    sem = ctx.sems["p7_pre"]
                    if gst0:
                        sy.wait_ge(ctx.sems["gst"], gst0)
                    for c in range(NCH):
                        sy.dma_start(out=stream[c][:, :],
                                     in_=src.ap()[c * 128:(c + 1) * 128, :]
                                     ).then_inc(sem, 16)
                    sy.dma_start(out=ident[:, :], in_=w["ident_bf"][:, :]
                                 ).then_inc(sem, 16)
                    sy.dma_start(out=diag_sb[:, :], in_=w["diag_mask"][:, :]
                                 ).then_inc(sem, 16)
                    sy.dma_start(out=cosb[:, :], in_=w["cos_rep"][:, :]
                                 ).then_inc(sem, 16)
                    sy.dma_start(out=sinb[:, :], in_=w["sin_rep"][:, :]
                                 ).then_inc(sem, 16)
                    sy.dma_start(out=rswb[:, :], in_=w["rsw"][:, :]
                                 ).then_inc(sem, 16)
                    for i in range(NB):
                        mat, jb = ("l1_wq", i) if i < 16 else ("l1_wk", i - 16)
                        if i >= 2:
                            sy.wait_ge(ctx.sems["p7_mm"], i - 1)
                        sy.dma_start(out=wbuf[i % 2][:, :],
                                     in_=_wblock(w[mat], jb)
                                     ).then_inc(p7_w[i % 2], 16)

                @blk.tensor
                def _(te):
                    def rot(i):
                        te.wait_ge(ctx.sems["p7_qe"], i + 1)
                        if i >= 2:
                            te.wait_ge(ctx.sems["p7_t2"], i - 1)
                        for half in range(2):
                            mm = te.matmul(
                                ps_r[i % 2][:, half * 512:(half + 1) * 512],
                                rswb[:, :],
                                qf[i % 2][:, half * 512:(half + 1) * 512],
                                start=True, stop=True)
                        mm.then_inc(ctx.sems["p7_rm"], 1)

                    for i in range(NB):
                        if i == 0:
                            te.wait_ge(ctx.sems["p7_pre"], 16 * 21)
                        te.wait_ge(p7_w[i % 2], 16 * (i // 2 + 1))
                        if i >= 2:
                            te.wait_ge(ctx.sems["p7_qe"], i - 1)
                        for k in range(NCH):
                            for half in range(2):
                                mm = te.matmul(
                                    ps_q[i % 2][:, half * 512:(half + 1) * 512],
                                    wbuf[i % 2][:, k * 128:(k + 1) * 128],
                                    stream[k][:, half * 512:(half + 1) * 512],
                                    start=(k == 0), stop=(k == NCH - 1))
                        mm.then_inc(ctx.sems["p7_mm"], 1)
                        if i >= 1:
                            rot(i - 1)
                    rot(NB - 1)

                @blk.scalar
                def _(sc):
                    for i in range(NB):
                        sc.wait_ge(ctx.sems["p7_mm"], i + 1)
                        if i >= 2:
                            sc.wait_ge(ctx.sems["p7_t1"], i - 1)
                        scale = SQ8 if i < 16 else 1.0
                        sc.activation(qf[i % 2][:, :], ps_q[i % 2][:, :],
                                      AF.Identity, bias=0.0, scale=scale
                                      ).then_inc(ctx.sems["p7_qe"], 1)


                @blk.vector
                def _(ve):
                    for i in range(NB):
                        ve.wait_ge(ctx.sems["p7_qe"], i + 1)
                        if i >= 1:
                            ve.wait_ge(ctx.sems["p7_qd"], i)
                        ve.tensor_mul(tmp1[:, :], qf[i % 2][:, :], cosb[:, :]
                                      ).then_inc(ctx.sems["p7_t1"], 1)
                        ve.wait_ge(ctx.sems["p7_rm"], i + 1)
                        ve.tensor_mul(tmp2[:, :], ps_r[i % 2][:, :], sinb[:, :]
                                      ).then_inc(ctx.sems["p7_t2"], 1)
                        ve.wait_ge(ctx.sems["p7_t1"], i + 1)
                        ve.wait_ge(ctx.sems["p7_t2"], i + 1)
                        dstt = q_bf[i] if i < 16 else kobf[(i - 16) % 2]
                        if i >= 18:
                            ve.wait_ge(p7_kwr[i % 2], 16 * ((i - 18) // 2 + 1))
                        ve.tensor_add(dstt[:, :], tmp1[:, :], tmp2[:, :]
                                      ).then_inc(ctx.sems["p7_qd"], 1)

                @blk.gpsimd
                def _(gp):
                    def emit(slot):
                        for i in range(4):
                            gp.wait_ge(ctx.sems["p7_qd"], 17 + i)
                            out = shr["k1"].ap()[slot * 512 + i * 128:
                                                 slot * 512 + (i + 1) * 128, :]
                            gp.dma_start(out=out, in_=kobf[i % 2][:, :]
                                         ).then_inc(p7_kwr[i % 2], 16)
                    with gp.If_eq(ctx.regs["rank"], 0):
                        emit(0)
                    with gp.Else():
                        emit(1)

        with ExitStack() as p7b:
            e7b = p7b.enter_context
            wv_sb = [e7b(nc.sbuf_tensor(f"wv{k}", [128, 512], F32R))
                     for k in range(NCH)]
            vt = [e7b(nc.sbuf_tensor(f"vt7{i}", [128, 512], BF16))
                  for i in range(2)]
            ps_v = [e7b(nc.psum_tensor(f"pv7{i}", [128, 512], F32))
                    for i in range(2)]
            with nc.Block("l1_v") as blk:
                @blk.sync
                def _(sy):
                    for k in range(NCH):
                        sy.dma_start(out=wv_sb[k][:, :],
                                     in_=w["l1_wv_r"].ap()[k * 128:(k + 1) * 128, :]
                                     ).then_inc(ctx.sem("p7b_pre"), 16)

                @blk.tensor
                def _(te):
                    te.wait_ge(ctx.sems["p7b_pre"], 16 * 16)
                    for tt in range(8):
                        if tt >= 2:
                            te.wait_ge(ctx.sems["p7_ve"], tt - 1)
                        for k in range(NCH):
                            mm = te.matmul(
                                ps_v[tt % 2][:, :],
                                stream[k][:, tt * 128:(tt + 1) * 128],
                                wv_sb[k][:, :],
                                start=(k == 0), stop=(k == NCH - 1))
                        mm.then_inc(ctx.sems["p7_vm"], 1)

                @blk.scalar
                def _(sc):
                    for tt in range(8):
                        sc.wait_ge(ctx.sems["p7_vm"], tt + 1)
                        if tt >= 2:
                            sc.wait_ge(p7_vwr[tt % 2], 16 * ((tt - 2) // 2 + 1))
                        sc.activation(vt[tt % 2][:, :], ps_v[tt % 2][:, :],
                                      AF.Identity, bias=0.0, scale=1.0
                                      ).then_inc(ctx.sems["p7_ve"], 1)

                @blk.gpsimd
                def _(gp):
                    def emit(slot):
                        for tt in range(8):
                            gp.wait_ge(ctx.sems["p7_ve"], tt + 1)
                            out = shr["v1"].ap()[slot * 128:(slot + 1) * 128,
                                                 tt * 512:(tt + 1) * 512]
                            gp.dma_start(out=out, in_=vt[tt % 2][:, :]
                                         ).then_inc(p7_vwr[tt % 2], 16)
                        gp.wait_ge(p7_kwr[0], 16 * 2)
                        gp.wait_ge(p7_kwr[1], 16 * 2)
                        gp.wait_ge(p7_vwr[0], 16 * 4)
                        gp.wait_ge(p7_vwr[1], 16 * 4)
                        _exchange_flag(gp, ctx, slot, magic)
                    with gp.If_eq(ctx.regs["rank"], 0):
                        emit(0)
                    with gp.Else():
                        emit(1)
                    _exchange_poll(gp, ctx, magic)
                    gp.sem_inc(ctx.sem("p7_xch"), 1)

        with ExitStack() as p8:
            e8 = p8.enter_context
            kg = [e8(nc.sbuf_tensor(f"kg{i}", [128, 2048], BF16)) for i in range(2)]
            v_all = e8(nc.sbuf_tensor("vall", [128, 16 * 512], BF16))
            p_bf = [e8(nc.sbuf_tensor(f"p8b{i}", [128, 2048], BF16))
                    for i in range(2)]
            pt_sb = [e8(nc.sbuf_tensor(f"pt8{i}", [128, 2048], BF16))
                     for i in range(2)]
            negmax = [e8(nc.sbuf_tensor(f"nm8{i}", [128, 1], F32)) for i in range(2)]
            sumexp = [e8(nc.sbuf_tensor(f"se8{i}", [128, 1], F32)) for i in range(2)]
            rsum = [e8(nc.sbuf_tensor(f"rs8{i}", [128, 1], F32)) for i in range(2)]
            ps_sc = e8(nc.psum_tensor("ps8_sc", [128, 2048], F32))
            ps_pt = e8(nc.psum_tensor("ps8_pt", [128, 2048], BF16))
            ps_o = [e8(nc.psum_tensor(f"ps8_o{i}", [128, 512], F32))
                    for i in range(2)]
            for s in ("p8_ldA", "p8_ldB", "p8_vl", "p8_sc", "p8_msk", "p8_mx",
                      "p8_ex", "p8_pn", "p8_tr", "p8_pte", "p8_mm", "p8_oe",
                      "p8_rc"):
                ctx.sem(s)
            p8_ld = [ctx.sems["p8_ldA"], ctx.sems["p8_ldB"]]
            NIT = 8 * 4 * 8

            def its():
                out = []
                for g in range(8):
                    for hh in range(4):
                        for qt in range(8):
                            out.append((g, hh, qt))
                return out

            ITS = its()

            with nc.Block("l1_attn") as blk:
                @blk.sync
                def _(sy):
                    sy.wait_ge(ctx.sems["p7_xch"], 1)
                    for slot in range(2):
                        sy.dma_start(
                            out=v_all[:, slot * 4096:(slot + 1) * 4096],
                            in_=shr["v1"].ap()[slot * 128:(slot + 1) * 128, :],
                        ).then_inc(ctx.sems["p8_vl"], 16)
                    for g in range(8):
                        if g >= 2:
                            sy.wait_ge(ctx.sems["p8_mm"], 32 * (g - 1))
                        for slot in range(2):
                            for dup in range(2):
                                sy.dma_start(
                                    out=kg[g % 2][dup * 64:dup * 64 + 64,
                                                  slot * 1024:(slot + 1) * 1024],
                                    in_=shr["k1"].ap()[slot * 512 + g * 64:
                                                       slot * 512 + (g + 1) * 64, :],
                                ).then_inc(p8_ld[g % 2], 16)

                def q_slice(hh, g, qt):
                    H = g * 4 + hh
                    ch, par = H // 2, H % 2
                    return q_bf[ch][par * 64:par * 64 + 64,
                                    qt * 128:(qt + 1) * 128]

                def _S4(te, it, rank):
                    g, hh, qt = ITS[it]
                    nk = len(_l1_kept(rank, qt))
                    par = it % 2
                    te.wait_ge(ctx.sems["p8_pn"], it + 1)
                    for i in range(nk):
                        mm = te.transpose(ps_pt[:, i * 128:(i + 1) * 128],
                                          p_bf[par][:, i * 128:(i + 1) * 128],
                                          ident[:, :])
                    mm.then_inc(ctx.sems["p8_tr"], 1)

                def _S6(te, it, rank):
                    g, hh, qt = ITS[it]
                    par = it % 2
                    kept = _l1_kept(rank, qt)
                    te.wait_ge(ctx.sems["p8_pte"], it + 1)
                    if it >= 2:
                        te.wait_ge(ctx.sems["p8_oe"], it - 1)
                    for i, (kt, kind) in enumerate(kept):
                        mm = te.matmul(
                            ps_o[par][0:64, 0:128],
                            v_all[:, kt * 512 + g * 64:kt * 512 + g * 64 + 64],
                            pt_sb[par][:, i * 128:(i + 1) * 128],
                            start=(i == 0), stop=(i == len(kept) - 1))
                    mm.then_inc(ctx.sems["p8_mm"], 1)

                def te_body(te, rank):
                    for it, (g, hh, qt) in enumerate(ITS):
                        kept = _l1_kept(rank, qt)
                        par = it % 2
                        H = g * 4 + hh
                        hb = (H % 2) * 64
                        if it == 0:
                            te.wait_ge(ctx.sems["p8_vl"], 32)
                        if qt == 0 and hh == 0:
                            te.wait_ge(p8_ld[g % 2], 64 * (g // 2 + 1))
                        if it >= 1:
                            te.wait_ge(ctx.sems["p8_ex"], it)
                        for i, (kt, kind) in enumerate(kept):
                            mm = te.matmul(
                                ps_sc[:, i * 128:(i + 1) * 128],
                                q_slice(hh, g, qt),
                                kg[g % 2][hb:hb + 64, kt * 128:(kt + 1) * 128],
                                start=True, stop=True)
                        mm.then_inc(ctx.sems["p8_sc"], 1)
                        if it >= 1:
                            _S4(te, it - 1, rank)
                            _S6(te, it - 1, rank)
                    _S4(te, NIT - 1, rank)
                    _S6(te, NIT - 1, rank)

                @blk.tensor
                def _(te):
                    with te.If_eq(ctx.regs["rank_te"], 0):
                        te_body(te, 0)
                    with te.Else():
                        te_body(te, 1)

                def ve_body(ve, rank):
                    for it, (g, hh, qt) in enumerate(ITS):
                        kept = _l1_kept(rank, qt)
                        nk = len(kept)
                        wk = nk * 128
                        par = it % 2
                        di = [i for i, (kt, kind) in enumerate(kept)
                              if kind == "diag"][0]
                        ve.wait_ge(ctx.sems["p8_sc"], it + 1)
                        ve.tensor_add(ps_sc[:, di * 128:(di + 1) * 128],
                                      ps_sc[:, di * 128:(di + 1) * 128],
                                      diag_sb[:, :]).then_inc(
                            ctx.sems["p8_msk"], 1)
                        if it >= 2:
                            ve.wait_ge(ctx.sems["p8_ex"], it - 1)
                        ve.wait_ge(ctx.sems["p8_msk"], it + 1)
                        ve.reduce_max(negmax[par][:, :], ps_sc[:, 0:wk],
                                      axis=AX.X, negate=True).then_inc(
                            ctx.sems["p8_mx"], 1)
                        if it >= 1:
                            parp = (it - 1) % 2
                            ve.wait_ge(ctx.sems["p8_ex"], it)
                            ve.reciprocal(rsum[parp][:, :], sumexp[parp][:, :]
                                          ).then_inc(ctx.sems["p8_rc"], 1)
                            ve.wait_ge(ctx.sems["p8_rc"], it)
                            wkp = len(_l1_kept(rank, ITS[it - 1][2])) * 128
                            ve.tensor_scalar_mul(
                                p_bf[parp][:, 0:wkp], p_bf[parp][:, 0:wkp],
                                rsum[parp][:, 0:1]).then_inc(ctx.sems["p8_pn"], 1)
                    parp = (NIT - 1) % 2
                    ve.wait_ge(ctx.sems["p8_ex"], NIT)
                    ve.reciprocal(rsum[parp][:, :], sumexp[parp][:, :]).then_inc(
                        ctx.sems["p8_rc"], 1)
                    ve.wait_ge(ctx.sems["p8_rc"], NIT)
                    wkp = len(_l1_kept(rank, ITS[NIT - 1][2])) * 128
                    ve.tensor_scalar_mul(p_bf[parp][:, 0:wkp],
                                         p_bf[parp][:, 0:wkp],
                                         rsum[parp][:, 0:1]).then_inc(
                        ctx.sems["p8_pn"], 1)

                @blk.vector
                def _(ve):
                    with ve.If_eq(ctx.regs["rank_ve"], 0):
                        ve_body(ve, 0)
                    with ve.Else():
                        ve_body(ve, 1)

                def sc_body(sc, rank):
                    def S5(it):
                        par = it % 2
                        wk = len(_l1_kept(rank, ITS[it][2])) * 128
                        sc.wait_ge(ctx.sems["p8_tr"], it + 1)
                        sc.copy(pt_sb[par][:, 0:wk], ps_pt[:, 0:wk]).then_inc(
                            ctx.sems["p8_pte"], 1)

                    def S7(it):
                        g, hh, qt = ITS[it]
                        H = g * 4 + hh
                        ch, hp = H // 2, (H % 2) * 64
                        par = it % 2
                        sc.wait_ge(ctx.sems["p8_mm"], it + 1)
                        sc.copy(ctxh[ch][hp:hp + 64, qt * 128:(qt + 1) * 128],
                                ps_o[par][0:64, 0:128]).then_inc(
                            ctx.sems["p8_oe"], 1)

                    for it, (g, hh, qt) in enumerate(ITS):
                        par = it % 2
                        wk = len(_l1_kept(rank, qt)) * 128
                        sc.wait_ge(ctx.sems["p8_mx"], it + 1)
                        if it >= 2:
                            sc.wait_ge(ctx.sems["p8_tr"], it - 1)
                        sc.activation(p_bf[par][:, 0:wk], ps_sc[:, 0:wk],
                                      AF.Exp, bias=negmax[par][:, 0:1],
                                      scale=1.0,
                                      accum_out=sumexp[par][:, :]).then_inc(
                            ctx.sems["p8_ex"], 1)
                        if it >= 1:
                            S5(it - 1)
                        if it >= 2:
                            S7(it - 2)
                    S5(NIT - 1)
                    S7(NIT - 2)
                    S7(NIT - 1)

                @blk.scalar
                def _(sc):
                    with sc.If_eq(ctx.regs["rank_sc"], 0):
                        sc_body(sc, 0)
                    with sc.Else():
                        sc_body(sc, 1)

        with ExitStack() as p9:
            e9 = p9.enter_context
            wbuf = [e9(nc.sbuf_tensor(f"w9_{i}", [128, D], F32R)) for i in range(2)]
            wo_bf = [e9(nc.sbuf_tensor(f"wob{i}", [128, D], BF16)) for i in range(2)]
            sq_buf = [e9(nc.sbuf_tensor(f"sq9{i}", [128, T], F32R))
                      for i in range(2)]
            rstd_sb = e9(nc.sbuf_tensor("rstd9", [128, T], F32))
            rstd2_sb = e9(nc.sbuf_tensor("rstd9b", [128, T], F32))
            ps_sum = e9(nc.psum_tensor("ps9_sum", [128, T], F32))
            ps_var = e9(nc.psum_tensor("ps9_var", [128, T], F32))
            psq = [e9(nc.psum_tensor(f"ps9_{i}", [128, T], F32)) for i in range(2)]
            for s in ("p9_wA", "p9_wB", "p9_c", "p9_mm", "p9_res"):
                ctx.sem(s)
            p9_w = [ctx.sems["p9_wA"], ctx.sems["p9_wB"]]
            ln = LN(ctx, "ln3", stream, ctx.sb["ones"], sq_buf, rstd_sb,
                    rstd2_sb, ps_sum, ps_var, ("p9_res", lambda j: j + 1), dst)

            with nc.Block("l1_wo") as blk:
                @blk.sync
                def _(sy):
                    for j in range(NCH):
                        if j >= 2:
                            sy.wait_ge(ctx.sems["p9_c"], j - 1)
                        sy.dma_start(out=wbuf[j % 2][:, :],
                                     in_=_wblock(w["l1_wo"], j)
                                     ).then_inc(p9_w[j % 2], 16)
                    ln.store(sy)

                @blk.vector
                def _(ve):
                    for j in range(NCH):
                        ve.wait_ge(p9_w[j % 2], 16 * (j // 2 + 1))
                        if j >= 2:
                            ve.wait_ge(ctx.sems["p9_mm"], j - 1)
                        ve.tensor_copy(wo_bf[j % 2][:, :], wbuf[j % 2][:, :]
                                       ).then_inc(ctx.sems["p9_c"], 1)
                        ve.wait_ge(ctx.sems["p9_mm"], j + 1)
                        ve.scalar_tensor_tensor(
                            stream[j][:, :], psq[j % 2][:, :], 0.0,
                            stream[j][:, :], ALU.add, ALU.add).then_inc(
                            ctx.sems["p9_res"], 1)
                    ln.ve(ve)

                @blk.tensor
                def _(te):
                    for j in range(NCH):
                        te.wait_ge(ctx.sems["p9_c"], j + 1)
                        if j >= 2:
                            te.wait_ge(ctx.sems["p9_res"], j - 1)
                        for oc in range(NCH):
                            for half in range(2):
                                mm = te.matmul(
                                    psq[j % 2][:, half * 512:(half + 1) * 512],
                                    wo_bf[j % 2][:, oc * 128:(oc + 1) * 128],
                                    ctxh[oc][:, half * 512:(half + 1) * 512],
                                    start=(oc == 0), stop=(oc == NCH - 1))
                        mm.then_inc(ctx.sems["p9_mm"], 1)
                    ln.te(te)

                @blk.scalar
                def _(sc):
                    ln.sc(sc)


def _group_l1_ffn(ctx, src, dst):
    nc = ctx.nc
    w = ctx.dram["w"]
    spill = ctx.dram["spill"]
    NHC1 = HFF1 // 128

    with ExitStack() as ph:
        e = ph.enter_context
        stat_acc = e(nc.sbuf_tensor("stat10", [128, T], F32))
        rgb = [e(nc.sbuf_tensor(f"rgb{m}", [128, 1], F32)) for m in range(3)]
        magic_sb = e(nc.sbuf_tensor("magic10", [128, 1], F32))

        with ExitStack() as pa:
            ea = pa.enter_context
            gbuf = [ea(nc.sbuf_tensor(f"gb{i}", [128, HFF1], F32))
                    for i in range(2)]
            tmp_r = ea(nc.sbuf_tensor("gtmp", [128, 1], F32))
            acc = ea(nc.sbuf_tensor("gacc", [128, 1], F32))
            idf = ea(nc.sbuf_tensor("idf", [128, 128], F32))
            gsum = ea(nc.sbuf_tensor("gsum", [1, 1], F32))
            gt2 = ea(nc.sbuf_tensor("gt2", [1, 1], F32))
            gt2b = ea(nc.sbuf_tensor("gt2b", [1, 1], F32))
            grc = ea(nc.sbuf_tensor("grc", [1, 2], F32R))
            ps_t = ea(nc.psum_tensor("ps10t", [128, 512], F32))
            for s in ("pa_ldA", "pa_ldB", "pa_rd", "pa_tr", "pa_trb", "pa_sum",
                      "pa_bc", "pa_done", "pa_fin", "pa_v1", "pa_v2", "pa_v3", "pa_mg"):
                ctx.sem(s)
            pa_ld = [ctx.sems["pa_ldA"], ctx.sems["pa_ldB"]]
            mats = [("l1_w1", NHC1, D), ("l1_w2", NHC1, D), ("l1_w3", NCH, HFF1)]

            with nc.Block("l1_gamma") as blk:
                @blk.sync
                def _(sy):
                    sy.dma_start(out=magic_sb[:, :],
                                 in_=w["magic_f"][:, :]).then_inc(
                        ctx.sem("pa_mg"), 16)
                    sy.dma_start(out=idf[:, :], in_=w["ident_f"][:, :]
                                 ).then_inc(ctx.sem("pa_mg"), 16)
                    g = 0
                    for mi, (mn, nb, kd) in enumerate(mats):
                        for b in range(nb):
                            if g >= 2:
                                sy.wait_ge(ctx.sems["pa_rd"], g - 1)
                            sy.dma_start(out=gbuf[g % 2][:, 0:kd],
                                         in_=_wblock(w[mn], b)
                                         ).then_inc(pa_ld[g % 2], 16)
                            g += 1

                @blk.tensor
                def _(te):
                    te.wait_ge(ctx.sems["pa_mg"], 32)
                    for mi in range(3):
                        te.wait_ge(ctx.sems["pa_done"], mi + 1)
                        te.transpose(ps_t[0:1, 0:128], acc[:, 0:1], idf[:, :]
                                     ).then_inc(ctx.sems["pa_tr"], 1)
                        te.wait_ge(ctx.sems["pa_sum"], mi + 1)
                        mm = te.matmul(ps_t[:, 128:130],
                                       ctx.sb["ones"].ap()[0:1, 0:128],
                                       grc[0:1, 0:2],
                                       start=True, stop=True)
                        mm.then_inc(ctx.sems["pa_trb"], 1)

                @blk.vector
                def _(ve):
                    nk = {0: HFF1 * D, 1: HFF1 * D, 2: D * HFF1}
                    ch = ctx.sem("pa_ch")
                    n = 0
                    g = 0
                    for mi, (mn, nb, kd) in enumerate(mats):
                        for b in range(nb):
                            ve.wait_ge(pa_ld[g % 2], 16 * (g // 2 + 1))
                            ve.tensor_reduce(tmp_r[:, :], gbuf[g % 2][:, 0:kd],
                                             AX.X, ALU.add,
                                             apply_absolute_value=True
                                             ).then_inc(ch, 1)
                            n += 1
                            ve.wait_ge(ch, n)
                            if b == 0:
                                ve.tensor_scalar_mul(acc[:, :], tmp_r[:, :], 1.0
                                                     ).then_inc(ch, 1)
                            else:
                                ve.tensor_add(acc[:, :], acc[:, :], tmp_r[:, :]
                                              ).then_inc(ch, 1)
                            n += 1
                            ve.wait_ge(ch, n)
                            ve.sem_inc(ctx.sems["pa_rd"], 1)
                            g += 1
                        ve.sem_inc(ctx.sems["pa_done"], 1)
                        ve.wait_ge(ctx.sems["pa_tr"], mi + 1)
                        ve.reduce_sum(gsum[0:1, 0:1], ps_t[0:1, 0:128], axis=AX.X
                                      ).then_inc(ctx.sem("pa_v1"), 1)
                        ve.wait_ge(ctx.sems["pa_v1"], mi + 1)
                        ve.tensor_scalar(gt2[0:1, 0:1], gsum[0:1, 0:1],
                                         1.0 / nk[mi], EPS_BL, ALU.mult, ALU.add
                                         ).then_inc(ctx.sem("pa_v2"), 1)
                        ve.wait_ge(ctx.sems["pa_v2"], mi + 1)
                        ve.reciprocal(gt2b[0:1, 0:1], gt2[0:1, 0:1]).then_inc(
                            ctx.sem("pa_v3"), 1)
                        ve.wait_ge(ctx.sems["pa_v3"], mi + 1)
                        ve.tensor_scalar_mul(grc[0:1, 0:2],
                                             gt2b[0:1, 0:1].broadcast_to([1, 2]),
                                             1.0).then_inc(ctx.sems["pa_sum"], 1)
                        ve.wait_ge(ctx.sems["pa_trb"], mi + 1)
                        ve.tensor_scalar_mul(rgb[mi][:, :], ps_t[:, 128:129], 1.0
                                             ).then_inc(ctx.sems["pa_bc"], 1)

        with ExitStack() as pb:
            eb = pb.enter_context
            h1 = [eb(nc.sbuf_tensor(f"h1_{c}", [128, T], BF16))
                  for c in range(NHC1)]
            ph_b = pb.enter_context(ExitStack())
            e4 = ph_b.enter_context
            xh = [e4(nc.sbuf_tensor(f"x10{c}", [128, 512], F32R))
                  for c in range(NCH)]
            wr = [e4(nc.sbuf_tensor(f"wr10{i}", [128, D], F32)) for i in range(2)]
            tq = [e4(nc.sbuf_tensor(f"tq10{i}", [128, D], F32)) for i in range(2)]
            wq = [e4(nc.sbuf_tensor(f"wq10{i}", [128, D], F32R)) for i in range(4)]
            sg = [e4(nc.sbuf_tensor(f"sg10{i}", [128, 512], F32)) for i in range(2)]
            ps_x1 = [e4(nc.psum_tensor(f"px1{i}", [128, 512], F32))
                     for i in range(2)]
            ps_x2 = [e4(nc.psum_tensor(f"px2{i}", [128, 512], F32))
                     for i in range(2)]
            for s in ("pb_pre", "pb_wA", "pb_wB", "pb_t", "pb_q", "pb_mm",
                      "pb_sg", "pb_ev"):
                ctx.sem(s)
            pb_w = [ctx.sems["pb_wA"], ctx.sems["pb_wB"]]
            gst0 = ctx.gst

            with nc.Block("l1_up") as blk:
                @blk.sync
                def _(sy):
                    for hf in range(2):
                        if hf == 0 and gst0:
                            sy.wait_ge(ctx.sems["gst"], gst0)
                        if hf == 1:
                            sy.wait_ge(ctx.sems["pb_mm"], NHC1)
                        for c in range(NCH):
                            sy.dma_start(
                                out=xh[c][:, :],
                                in_=src.ap()[c * 128:(c + 1) * 128,
                                             hf * 512:(hf + 1) * 512],
                            ).then_inc(ctx.sems["pb_pre"], 16)
                        for hc in range(NHC1):
                            for m in range(2):
                                g = (hf * NHC1 + hc) * 2 + m
                                if g >= 2:
                                    sy.wait_ge(ctx.sems["pb_t"], g - 1)
                                sy.dma_start(
                                    out=wr[g % 2][:, :],
                                    in_=_wblock(w["l1_w1" if m == 0 else "l1_w2"],
                                                hc)).then_inc(pb_w[g % 2], 16)

                @blk.scalar
                def _(sc):
                    def magic(g):
                        m = g % 2
                        sc.wait_ge(pb_w[g % 2], 16 * (g // 2 + 1))
                        if g >= 2:
                            sc.wait_ge(ctx.sems["pb_q"], g - 1)
                        sc.activation(tq[g % 2][:, :], wr[g % 2][:, :],
                                      AF.Identity, bias=magic_sb[:, 0:1],
                                      scale=rgb[m][:, 0:1]).then_inc(
                            ctx.sems["pb_t"], 1)

                    def sig(u):
                        sc.wait_ge(ctx.sems["pb_mm"], u + 1)
                        if u >= 2:
                            sc.wait_ge(ctx.sems["pb_ev"], u - 1)
                        sc.activation(sg[u % 2][:, :], ps_x1[u % 2][:, :],
                                      AF.Sigmoid).then_inc(ctx.sems["pb_sg"], 1)

                    for u in range(2 * NHC1):
                        magic(2 * u)
                        magic(2 * u + 1)
                        if u >= 1:
                            sig(u - 1)
                    sig(2 * NHC1 - 1)

                @blk.vector
                def _(ve):
                    def quant(g):
                        ve.wait_ge(ctx.sems["pb_t"], g + 1)
                        ve.tensor_scalar(tq[g % 2][:, :], tq[g % 2][:, :],
                                         RND_MAGIC, -1.0, ALU.subtract, ALU.max)
                        if g >= 4:
                            ve.wait_ge(ctx.sems["pb_mm"], (g - 4) // 2 + 1)
                        ve.tensor_scalar_min(wq[g % 4][:, :], tq[g % 2][:, :],
                                             1.0).then_inc(ctx.sems["pb_q"], 1)

                    def hmul(u):
                        hf, hc = divmod(u, NHC1)
                        ve.wait_ge(ctx.sems["pb_sg"], u + 1)
                        ve.tensor_mul(
                            h1[hc][:, hf * 512:(hf + 1) * 512],
                            sg[u % 2][:, :], ps_x2[u % 2][:, :],
                        ).then_inc(ctx.sems["pb_ev"], 1)

                    for u in range(2 * NHC1):
                        quant(2 * u)
                        quant(2 * u + 1)
                        if u >= 1:
                            hmul(u - 1)
                    hmul(2 * NHC1 - 1)

                @blk.tensor
                def _(te):
                    for hf in range(2):
                        for hc in range(NHC1):
                            u = hf * NHC1 + hc
                            if hc == 0:
                                te.wait_ge(ctx.sems["pb_pre"], 16 * 16 * (hf + 1))
                            te.wait_ge(ctx.sems["pb_q"], 2 * u + 2)
                            if u >= 2:
                                te.wait_ge(ctx.sems["pb_ev"], u - 1)
                                te.wait_ge(ctx.sems["pb_sg"], u - 1)
                            for k in range(NCH):
                                mm = te.matmul(
                                    ps_x1[u % 2][:, :],
                                    wq[(2 * u) % 4][:, k * 128:(k + 1) * 128],
                                    xh[k][:, :],
                                    start=(k == 0), stop=(k == NCH - 1))
                            for k in range(NCH):
                                mm = te.matmul(
                                    ps_x2[u % 2][:, :],
                                    wq[(2 * u + 1) % 4][:, k * 128:(k + 1) * 128],
                                    xh[k][:, :],
                                    start=(k == 0), stop=(k == NCH - 1))
                            mm.then_inc(ctx.sems["pb_mm"], 1)

            ph_b.close()

            with ExitStack() as pc:
                ec = pc.enter_context
                PW = 11 * 128
                w3r = [ec(nc.sbuf_tensor(f"w3r{i}", [128, PW], F32))
                       for i in range(2)]
                t3q = [ec(nc.sbuf_tensor(f"t3q{i}", [128, PW], F32))
                       for i in range(2)]
                w3q = [ec(nc.sbuf_tensor(f"w3q{i}", [128, PW], BF16))
                       for i in range(2)]
                xres = [ec(nc.sbuf_tensor(f"xr10{i}", [128, T], F32R))
                        for i in range(2)]
                ybuf = [ec(nc.sbuf_tensor(f"yb10{i}", [128, T], F32R))
                        for i in range(2)]
                ps_y = [ec(nc.psum_tensor(f"py10{i}", [128, T], F32))
                        for i in range(2)]
                ps_stat = ec(nc.psum_tensor("ps10s", [128, T], F32))
                for s in ("pc_wA", "pc_wB", "pc_xA", "pc_xB", "pc_t", "pc_q",
                          "pc_mmq", "pc_sj", "pc_res", "pc_stA", "pc_stB",
                          "pc_sums"):
                    ctx.sem(s)
                pc_w = [ctx.sems["pc_wA"], ctx.sems["pc_wB"]]
                pc_x = [ctx.sems["pc_xA"], ctx.sems["pc_xB"]]
                pc_st = [ctx.sems["pc_stA"], ctx.sems["pc_stB"]]

                with nc.Block("l1_down") as blk:
                    @blk.sync
                    def _(sy):
                        for j in range(NCH):
                            if j >= 2:
                                sy.wait_ge(ctx.sems["pc_res"], j - 1)
                            sy.dma_start(out=xres[j % 2][:, :],
                                         in_=src.ap()[j * 128:(j + 1) * 128, :]
                                         ).then_inc(pc_x[j % 2], 16)
                            for q in range(4):
                                gq = j * 4 + q
                                if gq >= 2:
                                    sy.wait_ge(ctx.sems["pc_t"], gq - 1)
                                sy.dma_start(
                                    out=w3r[gq % 2][:, :],
                                    in_=_wblock(w["l1_w3"], j, q * PW, PW),
                                ).then_inc(pc_w[gq % 2], 16)
                            sy.wait_ge(ctx.sems["pc_res"], j + 1)
                            sy.dma_start(out=spill.ap()[j * 128:(j + 1) * 128, :],
                                         in_=ybuf[j % 2][:, :]).then_inc(
                                pc_st[j % 2], 16)

                    @blk.scalar
                    def _(sc):
                        for gq in range(64):
                            sc.wait_ge(pc_w[gq % 2], 16 * (gq // 2 + 1))
                            if gq >= 2:
                                sc.wait_ge(ctx.sems["pc_q"], gq - 1)
                            sc.activation(t3q[gq % 2][:, :], w3r[gq % 2][:, :],
                                          AF.Identity, bias=magic_sb[:, 0:1],
                                          scale=rgb[2][:, 0:1]).then_inc(
                                ctx.sems["pc_t"], 1)

                    @blk.vector
                    def _(ve):
                        for j in range(NCH):
                            for q in range(4):
                                gq = j * 4 + q
                                ve.wait_ge(ctx.sems["pc_t"], gq + 1)
                                ve.tensor_scalar(t3q[gq % 2][:, :],
                                                 t3q[gq % 2][:, :],
                                                 RND_MAGIC, -1.0,
                                                 ALU.subtract, ALU.max)
                                if gq >= 2:
                                    ve.wait_ge(ctx.sems["pc_mmq"], gq - 1)
                                ve.tensor_scalar_min(w3q[gq % 2][:, :],
                                                     t3q[gq % 2][:, :], 1.0
                                                     ).then_inc(ctx.sems["pc_q"], 1)
                            ve.wait_ge(ctx.sems["pc_mmq"], 4 * (j + 1))
                            if j >= 2:
                                ve.wait_ge(pc_st[j % 2], 16 * ((j - 2) // 2 + 1))
                                ve.wait_ge(ctx.sems["pc_sums"], j - 1)
                            ve.wait_ge(pc_x[j % 2], 16 * (j // 2 + 1))
                            ve.scalar_tensor_tensor(
                                ybuf[j % 2][:, :], ps_y[j % 2][:, :], 1.0,
                                xres[j % 2][:, :], ALU.mult, ALU.add,
                            ).then_inc(ctx.sems["pc_res"], 1)
                            ve.wait_ge(ctx.sems["pc_sj"], j + 1)
                            if j == 0:
                                ve.tensor_scalar_mul(stat_acc[:, :],
                                                     ps_stat[:, :], 1.0
                                                     ).then_inc(
                                    ctx.sems["pc_sums"], 1)
                            else:
                                ve.tensor_add(stat_acc[:, :], stat_acc[:, :],
                                              ps_stat[:, :]).then_inc(
                                    ctx.sems["pc_sums"], 1)

                    @blk.tensor
                    def _(te):
                        for j in range(NCH):
                            for q in range(4):
                                gq = j * 4 + q
                                te.wait_ge(ctx.sems["pc_q"], gq + 1)
                                if j >= 2 and q == 0:
                                    te.wait_ge(ctx.sems["pc_res"], j - 1)
                                for f in range(11):
                                    hc = q * 11 + f
                                    for half in range(2):
                                        mm = te.matmul(
                                            ps_y[j % 2][:, half * 512:
                                                        (half + 1) * 512],
                                            w3q[gq % 2][:, f * 128:(f + 1) * 128],
                                            h1[hc][:, half * 512:(half + 1) * 512],
                                            start=(hc == 0), stop=(hc == NHC1 - 1))
                                mm.then_inc(ctx.sems["pc_mmq"], 1)
                            te.wait_ge(ctx.sems["pc_res"], j + 1)
                            if j >= 1:
                                te.wait_ge(ctx.sems["pc_sums"], j)
                            for half in range(2):
                                mm = te.matmul(
                                    ps_stat[:, half * 512:(half + 1) * 512],
                                    ctx.sb["ones"][:, :],
                                    ybuf[j % 2][:, half * 512:(half + 1) * 512],
                                    start=True, stop=True)
                            mm.then_inc(ctx.sems["pc_sj"], 1)

        with ExitStack() as pd:
            ed = pd.enter_context
            stream = [ed(nc.sbuf_tensor(f"s10_{c}", [128, T], F32R))
                      for c in range(NCH)]
            sq_buf = [ed(nc.sbuf_tensor(f"sq10{i}", [128, T], F32R))
                      for i in range(2)]
            rstd_sb = ed(nc.sbuf_tensor("rstd10", [128, T], F32))
            rstd2_sb = ed(nc.sbuf_tensor("rstd10b", [128, T], F32))
            ps_var = ed(nc.psum_tensor("ps10v", [128, T], F32))
            for s in ("pd_lA", "pd_lB"):
                ctx.sem(s)
            pd_l = [ctx.sems["pd_lA"], ctx.sems["pd_lB"]]
            ln = LN(ctx, "ln4", stream, ctx.sb["ones"], sq_buf, rstd_sb,
                    rstd2_sb, stat_acc, ps_var,
                    ("pd_lA", lambda j: 16 * (j // 2 + 1)), dst, sum_done=True,
                    ve_rdy=(pd_l, lambda j: 16 * (j // 2 + 1)))

            with nc.Block("l1_ln4") as blk:
                @blk.sync
                def _(sy):
                    sy.wait_ge(ctx.sems["pc_stA"], 16 * 8)
                    sy.wait_ge(ctx.sems["pc_stB"], 16 * 8)
                    for j in range(NCH):
                        sy.dma_start(out=stream[j][:, :],
                                     in_=spill.ap()[j * 128:(j + 1) * 128, :]
                                     ).then_inc(pd_l[j % 2], 16)
                    ln.store(sy)

                @blk.tensor
                def _(te):
                    ln.te(te)

                @blk.vector
                def _(ve):
                    ln.ve(ve)

                @blk.scalar
                def _(sc):
                    ln.sc(sc)



def _exchange_flag(gp, ctx, slot, magic):
    flagv = ctx.sb["flagv_sb"]
    sh_flag = ctx.dram["shr"]["flag"]
    n_prev = getattr(ctx, "xfl_count", 0)
    if n_prev:
        gp.wait_ge(ctx.sem("xfl"), 16 * n_prev)
    gp.memset(flagv[0:1, 0:1], magic).then_inc(ctx.sem("xchm"), 1)
    gp.wait_ge(ctx.sems["xchm"], n_prev + 1)
    gp.dma_start(out=sh_flag.ap()[0:1, slot:slot + 1],
                 in_=flagv[0:1, 0:1]).then_inc(ctx.sem("xfl"), 16)


def _exchange_poll(gp, ctx, magic):
    ctx.xfl_count = getattr(ctx, "xfl_count", 0) + 1
    gp.wait_ge(ctx.sem("xfl"), 16 * ctx.xfl_count)
    flag_rd = ctx.sb["flag_rd"]
    sh_flag = ctx.dram["shr"]["flag"]
    regs = ctx.regs
    with gp.register(f"pcond{ctx.xfl_count}") as pcond:
        gp.reg_mov(pcond, 1)
        with gp.While(pcond):
            gp.reg_add(regs["poll_tgt"], regs["poll_tgt"], 16)
            gp.dma_start(out=flag_rd[0:1, 0:2],
                         in_=sh_flag.ap()[0:1, 0:2]).then_inc(ctx.sem("xpl"), 16)
            gp.wait_ge(ctx.sems["xpl"], regs["poll_tgt"])
            gp.reg_load(regs["pc"], flag_rd[0:1, 0:1])
            gp.reg_load(regs["pc1"], flag_rd[0:1, 1:2])
            gp.reg_alu(regs["pc"], regs["pc"], magic, ALU.not_equal)
            gp.reg_alu(regs["pc1"], regs["pc1"], magic, ALU.not_equal)
            gp.reg_alu(pcond, regs["pc"], regs["pc1"], ALU.logical_or)



def build(groups):
    nc = bass.Bass("TRN2", target_bir_lowering=False)
    es = ExitStack()
    ctx = Ctx(nc, es)
    first, last = groups[0], groups[-1]
    assert groups == list(range(first, last + 1))

    if first == 0:
        src0 = nc.dram_tensor("x_t", [D, T], F32R, kind="ExternalInput")
    else:
        src0 = nc.dram_tensor("stream_in", [D, T], F32R, kind="ExternalInput")
    if last == 3:
        dst_final = nc.dram_tensor("y_out", [D, T], F32R, kind="ExternalOutput")
    else:
        dst_final = nc.dram_tensor("stream_out", [D, T], F32R,
                                   kind="ExternalOutput")

    w = {}

    def wdecl(name, shape, dt=F32R):
        w[name] = nc.dram_tensor(name, shape, dt, kind="ExternalInput")

    wdecl("meta", [1, 8], I32)
    wdecl("ones_f", [128, 128])
    if 0 in groups:
        for n in ("l0_wq", "l0_wk", "l0_wv", "l0_wo"):
            wdecl(n, [D, D])
        for n in ("l0_bq", "l0_bk", "l0_bv", "l0_bo"):
            wdecl(n, [128, 16], F32)
    if 1 in groups:
        wdecl("l0_w1", [HFF0, D])
        wdecl("l0_w3", [HFF0, D])
        wdecl("l0_w2", [D, HFF0])
        wdecl("w128_bf", [128, 128], BF16)
    if 2 in groups:
        wdecl("l1_wq", [D, D])
        wdecl("l1_wk", [512, D])
        wdecl("l1_wv_r", [D, 512])
        wdecl("l1_wo", [D, D])
        wdecl("rsw", [128, 128])
        wdecl("cos_rep", [128, T], F32)
        wdecl("sin_rep", [128, T], F32)
        wdecl("diag_mask", [128, 128], F32)
    if 3 in groups:
        wdecl("l1_w1", [HFF1, D], F32)
        wdecl("l1_w2", [HFF1, D], F32)
        wdecl("l1_w3", [D, HFF1], F32)
        wdecl("ident_f", [128, 128], F32)
        wdecl("magic_f", [128, 1], F32)
    if 0 in groups or 2 in groups:
        wdecl("ident_bf", [128, 128], BF16)

    shr = {}
    if 0 in groups:
        shr["k0"] = nc.dram_tensor("sh_k0", [2 * D, T], BF16, addr_space="Shared")
        shr["v0"] = nc.dram_tensor("sh_v0", [2 * 128, NCH * T], BF16, addr_space="Shared")
    if 2 in groups:
        shr["k1"] = nc.dram_tensor("sh_k1", [2 * 512, T], BF16, addr_space="Shared")
        shr["v1"] = nc.dram_tensor("sh_v1", [2 * 128, 8 * 512], BF16, addr_space="Shared")
    if shr:
        shr["flag"] = nc.dram_tensor("sh_flag", [1, 16], I32, addr_space="Shared")

    spill = nc.dram_tensor("spill", [D, T], F32R)
    handoff = {}
    for g in groups[:-1]:
        handoff[g] = nc.dram_tensor(f"handoff{g}", [D, T], F32R)

    ctx.dram = dict(w=w, shr=shr, spill=spill)

    with es:
        for rn in ("rank", "poll_tgt", "pc", "pc1"):
            ctx.regs[rn] = nc.gpsimd.alloc_register(rn)
        ctx.regs["rank_te"] = nc.tensor.alloc_register("rank_te")
        ctx.regs["rank_ve"] = nc.vector.alloc_register("rank_ve")
        ctx.regs["rank_sc"] = nc.scalar.alloc_register("rank_sc")
        ctx.sb["meta_sb"] = es.enter_context(nc.sbuf_tensor("meta_sb", [1, 8], I32))
        ctx.sb["flagv_sb"] = es.enter_context(
            nc.sbuf_tensor("flagv_sb", [1, 4], I32))
        ctx.sb["flag_rd"] = es.enter_context(
            nc.sbuf_tensor("flag_rd", [1, 16], I32))
        ctx.sb["ones"] = es.enter_context(nc.sbuf_tensor("ones_g", [128, 128], F32R))
        ctx.sb["eps"] = es.enter_context(nc.sbuf_tensor("eps_g", [128, 1], F32))

        with nc.Block("init") as blk:
            @blk.gpsimd
            def _(gp):
                gp.dma_start(out=ctx.sb["meta_sb"][:, :],
                             in_=w["meta"][:, :]).then_inc(ctx.sem("mld"), 16)
                gp.dma_start(out=ctx.sb["ones"][:, :],
                             in_=w["ones_f"][:, :]).then_inc(ctx.sem("mld"), 16)
                gp.wait_ge(ctx.sems["mld"], 32)
                gp.reg_load(ctx.regs["rank"], ctx.sb["meta_sb"][0:1, 0:1])
                gp.reg_mov(ctx.regs["poll_tgt"], 0)
                gp.memset(ctx.sb["eps"][:, :], EPS_LN)

            @blk.tensor
            def _(te):
                te.wait_ge(ctx.sems["mld"], 32)
                te.reg_load(ctx.regs["rank_te"], ctx.sb["meta_sb"][0:1, 0:1])

            @blk.vector
            def _(ve):
                ve.wait_ge(ctx.sems["mld"], 32)
                ve.reg_load(ctx.regs["rank_ve"], ctx.sb["meta_sb"][0:1, 0:1])

            @blk.scalar
            def _(sc):
                sc.wait_ge(ctx.sems["mld"], 32)
                sc.reg_load(ctx.regs["rank_sc"], ctx.sb["meta_sb"][0:1, 0:1])

        cur = src0
        for g in groups:
            dstg = dst_final if g == last else handoff[g]
            ctx.begin_group()
            if g == 0:
                _group_l0_attn(ctx, cur, dstg)
            elif g == 1:
                _group_l0_ffn(ctx, cur, dstg)
            elif g == 2:
                _group_l1_attn(ctx, cur, dstg)
            elif g == 3:
                _group_l1_ffn(ctx, cur, dstg)
            ctx.end_group()
            cur = dstg

        with nc.Block("fin") as blk:
            @blk.sync
            def _(sy):
                sy.wait_ge(ctx.sem("gst"), ctx.gst)

    return nc


def kernel(**inputs):
    in_maps = stage_inputs(inputs)
    cur = [dict(m) for m in in_maps]
    res = None
    for grp in LAUNCH_GROUPS:
        nc = build(grp)
        res = run_bass_kernel_spmd(nc, cur, core_ids=list(range(8)))
        if grp[-1] != 3:
            for c in range(8):
                cur[c]["stream_in"] = np.asarray(res.results[c]["stream_out"])
    return unstage_output(res.results)


# version 03 (final)
# speedup vs baseline: 1.0267x; 1.0267x over previous
import math
import numpy as np
from contextlib import ExitStack

import concourse.bass as bass
import concourse.mybir as mybir
from concourse.bass_utils import run_bass_kernel_spmd
from concourse.expressions_rust import RuntimeValue

F32 = mybir.dt.float32
F32R = mybir.dt.float32r
BF16 = mybir.dt.bfloat16
I32 = mybir.dt.int32
AF = mybir.ActivationFunctionType
ALU = mybir.AluOpType
AX = mybir.AxisListType

D = 2048
T = 1024
NCH = D // 128
HFF0 = 8192
HFF1 = 5632
NH0 = 16
NH1 = 32
EPS_LN = 1e-5
EPS_BL = 1e-8
MAGIC_BASE = 0x5A5A0000
RND_MAGIC = 12582912.0

QBLOCKS = [[0, 1, 2, 3, 12, 13, 14, 15], [4, 5, 6, 7, 8, 9, 10, 11]]
GATHER_BLOCKS = QBLOCKS[0] + QBLOCKS[1]

LAUNCH_GROUPS = [[0, 1, 2, 3]]
STOP_AFTER = None
G2_SKIP = set()
LN_UPTO = None
LN_LVL = {None: 9, 't1': 1, 'sq': 2, 'var': 3, 'vb': 35, 'varnop': 36, 'ssq': 4, 'rs': 5}


def _positions(rank):
    out = []
    for b in QBLOCKS[rank]:
        out.extend(range(b * 128, (b + 1) * 128))
    return np.array(out, dtype=np.int64)


def _l1_kept(rank, qt):
    pq = QBLOCKS[rank][qt]
    out = []
    for kt in range(16):
        pk = GATHER_BLOCKS[kt]
        if pk < pq:
            out.append((kt, "full"))
        elif pk == pq:
            out.append((kt, "diag"))
    return out



def _hadamard128():
    h = np.array([[1.0]], dtype=np.float32)
    while h.shape[0] < 128:
        h = np.block([[h, h], [h, -h]])
    return h


def _rsw():
    m = np.arange(128)
    s = np.where((m % 64) < 32, -1.0, 1.0)
    r = np.zeros((128, 128), dtype=np.float32)
    r[m ^ 32, m] = s
    return r


def _diag_mask():
    q = np.arange(128)[:, None]
    k = np.arange(128)[None, :]
    return np.where(k <= q, 0.0, -1e9).astype(np.float32)


def _tile_lhsT(L):
    L = np.ascontiguousarray(L)
    Kd, Jd = L.shape
    t = L.reshape(Kd // 128, 128, Jd // 128, 128).transpose(2, 1, 0, 3)
    return np.ascontiguousarray(t.reshape(Jd, Kd))


def stage_inputs(inp):
    import ml_dtypes
    asc = np.ascontiguousarray
    x = np.asarray(inp["L0_attn_hidden_state"])
    cos = np.asarray(inp["L1_attn_cos"])
    sin = np.asarray(inp["L1_attn_sin"])

    sw = {
        "l0_wq": _tile_lhsT(np.asarray(inp["L0_attn_Wq"]).T),
        "l0_wk": _tile_lhsT(np.asarray(inp["L0_attn_Wk"]).T),
        "l0_wv": _tile_lhsT(np.asarray(inp["L0_attn_Wv"]).T),
        "l0_wo": _tile_lhsT(np.asarray(inp["L0_attn_Wo"]).T),
        "l0_bq": asc(np.asarray(inp["L0_attn_bq"]).reshape(16, 128).T),
        "l0_bk": asc(np.asarray(inp["L0_attn_bk"]).reshape(16, 128).T),
        "l0_bv": asc(np.asarray(inp["L0_attn_bv"]).reshape(16, 128).T),
        "l0_bo": asc(np.asarray(inp["L0_attn_bo"]).reshape(16, 128).T),
        "l0_w1": _tile_lhsT(np.asarray(inp["L0_ffn_W1"]).T),
        "l0_w3": _tile_lhsT(np.asarray(inp["L0_ffn_W3"]).T),
        "l0_w2": _tile_lhsT(np.asarray(inp["L0_ffn_W2"]).T),
        "l1_wq": _tile_lhsT(np.asarray(inp["L1_attn_Wq"])),
        "l1_wk": _tile_lhsT(np.asarray(inp["L1_attn_Wk"])),
        "l1_wv_r": asc(np.asarray(inp["L1_attn_Wv"], dtype=np.float32)),
        "l1_wo": _tile_lhsT(np.asarray(inp["L1_attn_Wo"])),
        "l1_w1": _tile_lhsT(np.asarray(inp["L1_ffn_w1"]).T),
        "l1_w2": _tile_lhsT(np.asarray(inp["L1_ffn_w2"]).T),
        "l1_w3": _tile_lhsT(np.asarray(inp["L1_ffn_w3"]).T),
        "ident_bf": np.eye(128, dtype=ml_dtypes.bfloat16),
        "ident_f": np.eye(128, dtype=np.float32),
        "magic_f": np.full((128, 1), 12582912.0, dtype=np.float32),
        "ones_f": np.ones((128, 128), dtype=np.float32),
        "w128_bf": _hadamard128().astype(ml_dtypes.bfloat16),
        "rsw": _rsw(),
        "diag_mask": _diag_mask(),
    }

    col32 = np.tile(np.arange(32), 4)
    in_maps = []
    for c in range(8):
        b, r = divmod(c, 2)
        pos = _positions(r)
        m = dict(sw)
        m["x_t"] = asc(x[b][pos].T)
        m["cos_rep"] = asc(cos[pos][:, col32].T)
        m["sin_rep"] = asc(sin[pos][:, col32].T)
        m["meta"] = np.array([[r, r ^ 1, r * 2048, r * 1024,
                               r * 512, r * 1024, r, 0]], dtype=np.int32)
        in_maps.append(m)
    return in_maps


def unstage_output(results):
    out = np.empty((4, 2048, 2048), dtype=np.float32)
    for c in range(8):
        b, r = divmod(c, 2)
        out[b][_positions(r)] = np.asarray(results[c]["y_out"]).T
    return out



GLOBAL_SEMS = {"gst", "mld", "xfl", "xpl", "xchm"}


class Ctx:
    def __init__(self, nc, es):
        self.nc = nc
        self.es = es
        self.sb = {}
        self.dram = {}
        self.sems = {}
        self.regs = {}
        self.magic_seq = 0
        self.gst = 0
        self.scoped_es = None
        self.group_sems = []

    def sem(self, name):
        if name not in self.sems:
            if self.scoped_es is not None and name not in GLOBAL_SEMS:
                self.sems[name] = self.scoped_es.enter_context(
                    self.nc.semaphore(name))
                self.group_sems.append(name)
            else:
                self.sems[name] = self.es.enter_context(self.nc.semaphore(name))
        return self.sems[name]

    def begin_group(self):
        self.scoped_es = ExitStack()
        self.group_sems = []

    def end_group(self):
        nc = self.nc
        self._clr = getattr(self, "_clr", 0) + 1
        sems = [self.sems[n] for n in self.group_sems]
        if sems:
            with nc.Block(f"semclr{self._clr}") as blk:
                @blk.sync
                def _(sy):
                    for s in sems:
                        sy.sem_clear(s)
        for n in self.group_sems:
            del self.sems[n]
        self.scoped_es.close()
        self.scoped_es = None
        self.group_sems = []


def _wblock(wdram, jb, k0=0, kn=None):
    kn = kn if kn is not None else wdram.shape[1] - k0
    return wdram.ap()[jb * 128:(jb + 1) * 128, k0:k0 + kn]



class LN:

    def __init__(self, ctx, tag, stream, ones_sb, sq_buf, rstd_sb, rstd2_sb,
                 ps_sum, ps_var, rdy, dst, sum_done=False, ve_rdy=None):
        self.ctx, self.tag = ctx, tag
        self.stream, self.ones, self.sq, self.rstd = stream, ones_sb, sq_buf, rstd_sb
        self.rstd2 = rstd2_sb
        self.ps_sum, self.ps_var = ps_sum, ps_var
        self.rdy_sem, self.rdy_val = rdy
        self.ve_rdy = ve_rdy
        self.dst = dst
        self.sum_done = sum_done
        for s in ("sum", "t1", "sq", "varj", "var", "ssq", "nm", "rs", "vb", "vc"):
            ctx.sem(f"{tag}_{s}")

    def te(self, te):
        c, tag = self.ctx, self.tag
        lvl = LN_LVL[LN_UPTO]
        if lvl == 36:
            lvl = 3
        if not self.sum_done:
            for j in range(NCH):
                te.wait_ge(c.sems[self.rdy_sem], self.rdy_val(j))
                for half in range(2):
                    mm = te.matmul(
                        self.ps_sum[:, half * 512:(half + 1) * 512],
                        self.ones[:, :],
                        self.stream[j][:, half * 512:(half + 1) * 512],
                        start=(j == 0), stop=(j == NCH - 1))
                if j == NCH - 1:
                    mm.then_inc(c.sems[f"{tag}_sum"], 1)
        if lvl < 3:
            return
        for j in range(NCH):
            te.wait_ge(c.sems[f"{tag}_sq"], j + 1)
            for half in range(2):
                mm = te.matmul(
                    self.ps_var[:, half * 512:(half + 1) * 512],
                    self.ones[:, :],
                    self.sq[j % 2][:, half * 512:(half + 1) * 512],
                    start=(j == 0), stop=(j == NCH - 1))
            mm.then_inc(c.sems[f"{tag}_varj"], 1)

    def ve(self, ve):
        c, tag = self.ctx, self.tag
        lvl = LN_LVL[LN_UPTO]
        for j in range(NCH):
            if j == 0 and not self.sum_done:
                ve.wait_ge(c.sems[f"{tag}_sum"], 1)
            if self.ve_rdy is not None:
                sems, vfn = self.ve_rdy
                ve.wait_ge(sems[j % 2], vfn(j))
            ve.scalar_tensor_tensor(
                self.stream[j][:, :], self.ps_sum[:, :], -1.0 / D,
                self.stream[j][:, :], ALU.mult, ALU.add,
            ).then_inc(c.sems[f"{tag}_t1"], 1)
        if lvl < 9:
            if lvl >= 2:
                ve.wait_ge(c.sems[f"{tag}_sq"], NCH)
            if lvl >= 4 or lvl == 35:
                ve.tensor_copy(self.rstd2[:, :], self.rstd[:, :]).then_inc(
                    c.sems[f"{tag}_vb"], 1)
            if lvl == 36:
                ve.engine_nop()
            if lvl >= 4:
                ve.wait_ge(c.sems[f"{tag}_ssq"], 1)
            if lvl >= 5:
                ve.reciprocal(self.rstd[:, :], self.rstd2[:, :]).then_inc(
                    c.sems[f"{tag}_rs"], 1)
                ve.wait_ge(c.sems[f"{tag}_rs"], 1)
            for j in range(NCH):
                ve.sem_inc(c.sems[f"{tag}_nm"], 1)
            return
        ve.wait_ge(c.sems[f"{tag}_varj"], NCH)
        eps_b = c.sb["eps"][:, 0:1].broadcast_to([128, T])
        ve.scalar_tensor_tensor(
            self.rstd[:, :], self.ps_var[:, :], 1.0 / D, eps_b,
            ALU.mult, ALU.add).then_inc(c.sems[f"{tag}_vb"], 1)
        ve.wait_ge(c.sems[f"{tag}_ssq"], 1)
        ve.reciprocal(self.rstd[:, :], self.rstd2[:, :]).then_inc(
            c.sems[f"{tag}_rs"], 1)
        ve.wait_ge(c.sems[f"{tag}_rs"], 1)
        for j in range(NCH):
            ve.tensor_mul(self.stream[j][:, :], self.stream[j][:, :],
                          self.rstd[:, :]).then_inc(c.sems[f"{tag}_nm"], 1)

    def sc(self, sc):
        c, tag = self.ctx, self.tag
        lvl = LN_LVL[LN_UPTO]
        if lvl in (35, 36):
            lvl = 3
        if lvl < 2:
            return
        for j in range(NCH):
            sc.wait_ge(c.sems[f"{tag}_t1"], j + 1)
            if j >= 2 and lvl >= 3:
                sc.wait_ge(c.sems[f"{tag}_varj"], j - 1)
            sc.activation(self.sq[j % 2][:, :], self.stream[j][:, :],
                          AF.Square).then_inc(c.sems[f"{tag}_sq"], 1)
        if lvl < 4:
            return
        sc.wait_ge(c.sems[f"{tag}_vb"], 1)
        sc.sqrt(self.rstd2[:, :], self.rstd[:, :]).then_inc(
            c.sems[f"{tag}_ssq"], 1)

    def store(self, sy):
        c, tag = self.ctx, self.tag
        for j in range(NCH):
            sy.wait_ge(c.sems[f"{tag}_nm"], j + 1)
            sy.dma_start(out=self.dst.ap()[j * 128:(j + 1) * 128, :],
                         in_=self.stream[j][:, :]).then_inc(c.sem("gst"), 16)
        c.gst += 16 * NCH



def _group_l0_attn(ctx, src, dst):
    nc = ctx.nc
    w = ctx.dram["w"]
    shr = ctx.dram["shr"]
    SQ = 1.0 / math.sqrt(128.0)
    NBLK = 48
    NIT = NH0 * 8

    with ExitStack() as ph:
        e = ph.enter_context
        stream = [e(nc.sbuf_tensor(f"st{c}", [128, T], F32R)) for c in range(NCH)]
        o_f = [e(nc.sbuf_tensor(f"o{c}", [128, T], F32R)) for c in range(NCH)]
        bias_sb = e(nc.sbuf_tensor("bias4", [128, 64], F32))
        bqs_sb = e(nc.sbuf_tensor("bqs", [128, 16], F32))
        ident = e(nc.sbuf_tensor("ident", [128, 128], BF16))
        ones_sb = ctx.sb["ones"]
        ph12 = ph.enter_context(ExitStack())
        e12 = ph12.enter_context
        q_bf = [e12(nc.sbuf_tensor(f"q{c}", [128, T], BF16)) for c in range(NCH)]

        gst0 = ctx.gst
        for s in ("p1_pre", "p1_wA", "p1_wB", "p1_mm", "p1_ev", "p1_vtr",
                  "p1_vte", "p1_kwrA", "p1_kwrB", "p1_vwrA", "p1_vwrB",
                  "p1_xch"):
            ctx.sem(s)
        p1_w = [ctx.sems["p1_wA"], ctx.sems["p1_wB"]]
        p1_kwr = [ctx.sems["p1_kwrA"], ctx.sems["p1_kwrB"]]
        p1_vwr = [ctx.sems["p1_vwrA"], ctx.sems["p1_vwrB"]]

        ph1 = ph12.enter_context(ExitStack())
        e1 = ph1.enter_context
        wbuf = [e1(nc.sbuf_tensor(f"wb{i}", [128, D], F32R)) for i in range(2)]
        kbuf = [e1(nc.sbuf_tensor(f"kb{i}", [128, T], BF16)) for i in range(2)]
        vbuf = [e1(nc.sbuf_tensor(f"vb{i}", [128, T], BF16)) for i in range(2)]
        vt_sb = [e1(nc.sbuf_tensor(f"vt{i}", [128, T], BF16)) for i in range(2)]
        ps_a = e1(nc.psum_tensor("ps_a", [128, T], F32))
        ps_b = e1(nc.psum_tensor("ps_b", [128, T], F32))
        ps_t = e1(nc.psum_tensor("ps_t", [128, T], BF16))
        psq = [ps_a, ps_b]
        with nc.Block("l0_qkv") as blk:
            @blk.sync
            def _(sy):
                if gst0:
                    sy.wait_ge(ctx.sems["gst"], gst0)
                for c in range(NCH):
                    sy.dma_start(out=stream[c][:, :],
                                 in_=src.ap()[c * 128:(c + 1) * 128, :]
                                 ).then_inc(ctx.sems["p1_pre"], 16)
                sy.dma_start(out=ident[:, :],
                             in_=w["ident_bf"][:, :]).then_inc(ctx.sems["p1_pre"], 16)
                for bi, bn in enumerate(("l0_bq", "l0_bk", "l0_bv", "l0_bo")):
                    sy.dma_start(out=bias_sb[:, bi * 16:(bi + 1) * 16],
                                 in_=w[bn][:, :]).then_inc(ctx.sems["p1_pre"], 16)
                for i in range(NBLK):
                    mat = ("l0_wq", "l0_wk", "l0_wv")[i // 16]
                    if i >= 2:
                        sy.wait_ge(ctx.sems["p1_mm"], i - 1)
                    sy.dma_start(out=wbuf[i % 2][:, :],
                                 in_=_wblock(w[mat], i % 16)
                                 ).then_inc(p1_w[i % 2], 16)

            @blk.tensor
            def _(te):
                def trans(ci):
                    te.wait_ge(ctx.sems["p1_ev"], 33 + ci)
                    if ci >= 1:
                        te.wait_ge(ctx.sems["p1_vte"], ci)
                    for tt in range(8):
                        mm = te.transpose(ps_t[:, tt * 128:(tt + 1) * 128],
                                          vbuf[ci % 2][:, tt * 128:(tt + 1) * 128],
                                          ident[:, :])
                    mm.then_inc(ctx.sems["p1_vtr"], 1)

                for i in range(NBLK):
                    if i == 0:
                        te.wait_ge(ctx.sems["p1_pre"], 16 * 21)
                    te.wait_ge(p1_w[i % 2], 16 * (i // 2 + 1))
                    if i >= 2:
                        te.wait_ge(ctx.sems["p1_ev"], i - 1)
                    for k in range(NCH):
                        lhs = wbuf[i % 2][:, k * 128:(k + 1) * 128]
                        for half in range(2):
                            mm = te.matmul(
                                psq[i % 2][:, half * 512:(half + 1) * 512],
                                lhs, stream[k][:, half * 512:(half + 1) * 512],
                                start=(k == 0), stop=(k == NCH - 1))
                    mm.then_inc(ctx.sems["p1_mm"], 1)
                    if i >= 34:
                        trans(i - 34)
                trans(14)
                trans(15)

            @blk.scalar
            def _(sc):
                sc.wait_ge(ctx.sems["p1_pre"], 16 * 21)
                sc.mul(bqs_sb[:, :], bias_sb[:, 0:16], SQ).then_inc(
                    ctx.sem("p1_bq"), 1)
                sc.wait_ge(ctx.sems["p1_bq"], 1)
                for i in range(NBLK):
                    sc.wait_ge(ctx.sems["p1_mm"], i + 1)
                    if i < 16:
                        sc.activation(q_bf[i][:, :], psq[i % 2][:, :], AF.Identity,
                                      bias=bqs_sb[:, i:i + 1], scale=SQ
                                      ).then_inc(ctx.sems["p1_ev"], 1)
                    elif i < 32:
                        ci = i - 16
                        if ci >= 2:
                            sc.wait_ge(p1_kwr[ci % 2], 16 * ((ci - 2) // 2 + 1))
                        sc.activation(kbuf[ci % 2][:, :], psq[i % 2][:, :], AF.Identity,
                                      bias=bias_sb[:, 16 + ci:17 + ci], scale=1.0
                                      ).then_inc(ctx.sems["p1_ev"], 1)
                    else:
                        ci = i - 32
                        if ci >= 2:
                            sc.wait_ge(ctx.sems["p1_vtr"], ci - 1)
                        sc.activation(vbuf[ci % 2][:, :], psq[i % 2][:, :], AF.Identity,
                                      bias=bias_sb[:, 32 + ci:33 + ci], scale=1.0
                                      ).then_inc(ctx.sems["p1_ev"], 1)

            @blk.vector
            def _(ve):
                for ci in range(16):
                    ve.wait_ge(ctx.sems["p1_vtr"], ci + 1)
                    if ci >= 2:
                        ve.wait_ge(p1_vwr[ci % 2], 16 * ((ci - 2) // 2 + 1))
                    ve.tensor_copy(vt_sb[ci % 2][:, :], ps_t[:, :]).then_inc(
                        ctx.sems["p1_vte"], 1)

            @blk.gpsimd
            def _(gp):
                ctx.magic_seq += 1
                magic = MAGIC_BASE + ctx.magic_seq

                def emit(slot):
                    for ci in range(16):
                        gp.wait_ge(ctx.sems["p1_ev"], 17 + ci)
                        out = shr["k0"].ap()[slot * 2048 + ci * 128:
                                             slot * 2048 + (ci + 1) * 128, :]
                        gp.dma_start(out=out, in_=kbuf[ci % 2][:, :]).then_inc(
                            p1_kwr[ci % 2], 16)
                    for ci in range(16):
                        gp.wait_ge(ctx.sems["p1_vte"], ci + 1)
                        out = shr["v0"].ap()[slot * 128:(slot + 1) * 128,
                                             ci * 1024:(ci + 1) * 1024]
                        gp.dma_start(out=out, in_=vt_sb[ci % 2][:, :]).then_inc(
                            p1_vwr[ci % 2], 16)
                    gp.wait_ge(p1_kwr[0], 16 * 8)
                    gp.wait_ge(p1_kwr[1], 16 * 8)
                    gp.wait_ge(p1_vwr[0], 16 * 8)
                    gp.wait_ge(p1_vwr[1], 16 * 8)
                    _exchange_flag(gp, ctx, slot, magic)
                with gp.If_eq(ctx.regs["rank"], 0):
                    emit(0)
                with gp.Else():
                    emit(1)
                _exchange_poll(gp, ctx, magic)
                gp.sem_inc(ctx.sems["p1_xch"], 1)

        ph1.close()

        if STOP_AFTER == "p1":
            with nc.Block("dbg_store") as blk:
                @blk.sync
                def _(sy):
                    for j in range(NCH):
                        sy.dma_start(out=dst.ap()[j * 128:(j + 1) * 128, :],
                                     in_=stream[j][:, :]).then_inc(ctx.sem("gst"), 16)
            ctx.gst += 16 * NCH
            return

        with ExitStack() as ph2:
            e2 = ph2.enter_context
            kh = [e2(nc.sbuf_tensor(f"kh{i}", [128, 2048], BF16)) for i in range(2)]
            vh = [e2(nc.sbuf_tensor(f"vh{i}", [128, 2048], BF16)) for i in range(2)]
            p_bf = [e2(nc.sbuf_tensor(f"pb{i}", [128, 2048], BF16)) for i in range(2)]
            pt_sb = [e2(nc.sbuf_tensor(f"pt{i}", [128, 2048], BF16)) for i in range(2)]
            negmax = [e2(nc.sbuf_tensor(f"ngm{i}", [128, 1], F32)) for i in range(2)]
            sumexp = [e2(nc.sbuf_tensor(f"sme{i}", [128, 1], F32)) for i in range(2)]
            rsum = [e2(nc.sbuf_tensor(f"rsm{i}", [128, 1], F32)) for i in range(2)]
            ps_sc = e2(nc.psum_tensor("ps_sc", [128, 2048], F32))
            ps_pt = e2(nc.psum_tensor("ps_pt", [128, 2048], BF16))
            ps_o = [e2(nc.psum_tensor(f"ps_o{i}", [128, 512], F32)) for i in range(2)]

            for s in ("p2_ldA", "p2_ldB", "p2_sc", "p2_mx", "p2_ex", "p2_pn",
                      "p2_tr", "p2_pte", "p2_mm", "p2_oe", "p2_rc"):
                ctx.sem(s)
            p2_ld = [ctx.sems["p2_ldA"], ctx.sems["p2_ldB"]]

            with nc.Block("l0_attn") as blk:
                @blk.sync
                def _(sy):
                    sy.wait_ge(ctx.sems["p1_xch"], 1)
                    for h in range(NH0):
                        if h >= 2:
                            sy.wait_ge(ctx.sems["p2_mm"], 8 * (h - 1))
                        for slot in range(2):
                            sy.dma_start(
                                out=kh[h % 2][:, slot * 1024:(slot + 1) * 1024],
                                in_=shr["k0"].ap()[slot * 2048 + h * 128:
                                                   slot * 2048 + (h + 1) * 128, :],
                            ).then_inc(p2_ld[h % 2], 16)
                        for slot in range(2):
                            sy.dma_start(
                                out=vh[h % 2][:, slot * 1024:(slot + 1) * 1024],
                                in_=shr["v0"].ap()[slot * 128:(slot + 1) * 128,
                                                   h * 1024:(h + 1) * 1024],
                            ).then_inc(p2_ld[h % 2], 16)

                def _S4(te, it):
                    h, par = it // 8, it % 2
                    te.wait_ge(ctx.sems["p2_pn"], it + 1)
                    for kt in range(16):
                        mm = te.transpose(ps_pt[:, kt * 128:(kt + 1) * 128],
                                          p_bf[par][:, kt * 128:(kt + 1) * 128],
                                          ident[:, :])
                    mm.then_inc(ctx.sems["p2_tr"], 1)

                def _S6(te, it):
                    h, par = it // 8, it % 2
                    te.wait_ge(ctx.sems["p2_pte"], it + 1)
                    if it >= 2:
                        te.wait_ge(ctx.sems["p2_oe"], it - 1)
                    for kt in range(16):
                        mm = te.matmul(ps_o[par][:, 0:128],
                                       vh[h % 2][:, kt * 128:(kt + 1) * 128],
                                       pt_sb[par][:, kt * 128:(kt + 1) * 128],
                                       start=(kt == 0), stop=(kt == 15))
                    mm.then_inc(ctx.sems["p2_mm"], 1)

                @blk.tensor
                def _(te):
                    for it in range(NIT):
                        h, qt = divmod(it, 8)
                        if qt == 0:
                            te.wait_ge(p2_ld[h % 2], 64 * (h // 2 + 1))
                        if it >= 1:
                            te.wait_ge(ctx.sems["p2_ex"], it)
                        for kb in range(4):
                            mm = te.matmul(ps_sc[:, kb * 512:(kb + 1) * 512],
                                           q_bf[h][:, qt * 128:(qt + 1) * 128],
                                           kh[h % 2][:, kb * 512:(kb + 1) * 512],
                                           start=True, stop=True)
                        mm.then_inc(ctx.sems["p2_sc"], 1)
                        if it >= 1:
                            _S4(te, it - 1)
                            _S6(te, it - 1)
                    _S4(te, NIT - 1)
                    _S6(te, NIT - 1)

                @blk.vector
                def _(ve):
                    for it in range(NIT):
                        par = it % 2
                        ve.wait_ge(ctx.sems["p2_sc"], it + 1)
                        if it >= 2:
                            ve.wait_ge(ctx.sems["p2_ex"], it - 1)
                        ve.reduce_max(negmax[par][:, :], ps_sc[:, :], axis=AX.X,
                                      negate=True).then_inc(ctx.sems["p2_mx"], 1)
                        if it >= 1:
                            parp = (it - 1) % 2
                            ve.wait_ge(ctx.sems["p2_ex"], it)
                            ve.reciprocal(rsum[parp][:, :], sumexp[parp][:, :]
                                          ).then_inc(ctx.sems["p2_rc"], 1)
                            ve.wait_ge(ctx.sems["p2_rc"], it)
                            ve.tensor_scalar_mul(
                                p_bf[parp][:, :], p_bf[parp][:, :],
                                rsum[parp][:, 0:1]).then_inc(ctx.sems["p2_pn"], 1)
                    parp = (NIT - 1) % 2
                    ve.wait_ge(ctx.sems["p2_ex"], NIT)
                    ve.reciprocal(rsum[parp][:, :], sumexp[parp][:, :]).then_inc(
                        ctx.sems["p2_rc"], 1)
                    ve.wait_ge(ctx.sems["p2_rc"], NIT)
                    ve.tensor_scalar_mul(p_bf[parp][:, :], p_bf[parp][:, :],
                                         rsum[parp][:, 0:1]).then_inc(
                        ctx.sems["p2_pn"], 1)

                def _S5(sc, it):
                    sc.wait_ge(ctx.sems["p2_tr"], it + 1)
                    sc.copy(pt_sb[it % 2][:, :], ps_pt[:, :]).then_inc(
                        ctx.sems["p2_pte"], 1)

                def _S7(sc, it):
                    h, qt = divmod(it, 8)
                    sc.wait_ge(ctx.sems["p2_mm"], it + 1)
                    sc.copy(o_f[h][:, qt * 128:(qt + 1) * 128],
                            ps_o[it % 2][:, 0:128]).then_inc(ctx.sems["p2_oe"], 1)

                @blk.scalar
                def _(sc):
                    for it in range(NIT):
                        par = it % 2
                        sc.wait_ge(ctx.sems["p2_mx"], it + 1)
                        if it >= 2:
                            sc.wait_ge(ctx.sems["p2_tr"], it - 1)
                        sc.activation(p_bf[par][:, :], ps_sc[:, :], AF.Exp,
                                      bias=negmax[par][:, 0:1], scale=1.0,
                                      accum_out=sumexp[par][:, :]).then_inc(
                            ctx.sems["p2_ex"], 1)
                        if it >= 1:
                            _S5(sc, it - 1)
                        if it >= 2:
                            _S7(sc, it - 2)
                    _S5(sc, NIT - 1)
                    _S7(sc, NIT - 2)
                    _S7(sc, NIT - 1)

        ph12.close()

        if STOP_AFTER == "p2":
            with nc.Block("dbg_store2") as blk:
                @blk.sync
                def _(sy):
                    for j in range(NCH):
                        sy.dma_start(out=dst.ap()[j * 128:(j + 1) * 128, :],
                                     in_=o_f[j][:, :]).then_inc(ctx.sem("gst"), 16)
            ctx.gst += 16 * NCH
            return

        with ExitStack() as ph3:
            e3 = ph3.enter_context
            wbuf = [e3(nc.sbuf_tensor(f"wb3{i}", [128, D], F32R)) for i in range(2)]
            sq_buf = [e3(nc.sbuf_tensor(f"sq{i}", [128, T], F32R)) for i in range(2)]
            rstd_sb = e3(nc.sbuf_tensor("rstd", [128, T], F32))
            rstd2_sb = e3(nc.sbuf_tensor("rstd2", [128, T], F32))
            ps_sum = e3(nc.psum_tensor("ps_sum", [128, T], F32))
            ps_var = e3(nc.psum_tensor("ps_var", [128, T], F32))
            psq = [e3(nc.psum_tensor(f"ps3_{i}", [128, T], F32)) for i in range(2)]

            for s in ("p3_wA", "p3_wB", "p3_mm", "p3_res"):
                ctx.sem(s)
            p3_w = [ctx.sems["p3_wA"], ctx.sems["p3_wB"]]
            ln = LN(ctx, "ln1", stream, ones_sb, sq_buf, rstd_sb, rstd2_sb,
                    ps_sum, ps_var, ("p3_res", lambda j: j + 1), dst)

            skip_ln = STOP_AFTER == "p3res"
            with nc.Block("l0_wo") as blk:
                @blk.sync
                def _(sy):
                    for j in range(NCH):
                        if j >= 2:
                            sy.wait_ge(ctx.sems["p3_mm"], j - 1)
                        sy.dma_start(out=wbuf[j % 2][:, :],
                                     in_=_wblock(w["l0_wo"], j)
                                     ).then_inc(p3_w[j % 2], 16)
                    if skip_ln:
                        for j in range(NCH):
                            sy.wait_ge(ctx.sems["p3_res"], j + 1)
                            sy.dma_start(out=dst.ap()[j * 128:(j + 1) * 128, :],
                                         in_=stream[j][:, :]).then_inc(
                                ctx.sem("gst"), 16)
                        ctx.gst += 16 * NCH
                    else:
                        ln.store(sy)

                @blk.tensor
                def _(te):
                    for j in range(NCH):
                        te.wait_ge(p3_w[j % 2], 16 * (j // 2 + 1))
                        if j >= 2:
                            te.wait_ge(ctx.sems["p3_res"], j - 1)
                        for oc in range(NCH):
                            lhs = wbuf[j % 2][:, oc * 128:(oc + 1) * 128]
                            for half in range(2):
                                mm = te.matmul(
                                    psq[j % 2][:, half * 512:(half + 1) * 512],
                                    lhs, o_f[oc][:, half * 512:(half + 1) * 512],
                                    start=(oc == 0),
                                    stop=(oc == NCH - 1))
                        mm.then_inc(ctx.sems["p3_mm"], 1)
                    if not skip_ln:
                        ln.te(te)

                @blk.vector
                def _(ve):
                    for j in range(NCH):
                        ve.wait_ge(ctx.sems["p3_mm"], j + 1)
                        ve.scalar_tensor_tensor(
                            stream[j][:, :], psq[j % 2][:, :],
                            bias_sb[:, 48 + j:49 + j], stream[j][:, :],
                            ALU.add, ALU.add).then_inc(ctx.sems["p3_res"], 1)
                    if not skip_ln:
                        ln.ve(ve)

                @blk.scalar
                def _(sc):
                    if not skip_ln:
                        ln.sc(sc)




def _group_l0_ffn(ctx, src, dst):
    nc = ctx.nc
    w = ctx.dram["w"]
    spill = ctx.dram["spill"]
    NHC = HFF0 // 128

    with ExitStack() as ph:
        e = ph.enter_context
        stat_acc = e(nc.sbuf_tensor("stat6", [128, T], F32))
        ph_h = ph.enter_context(ExitStack())
        h = [ph_h.enter_context(nc.sbuf_tensor(f"h{c}", [128, T], BF16))
             for c in range(NHC)]

        with ExitStack() as p4:
            e4 = p4.enter_context
            xh = [e4(nc.sbuf_tensor(f"xh{c}", [128, 512], F32R)) for c in range(NCH)]
            w1b = [e4(nc.sbuf_tensor(f"w1b{i}", [128, D], F32R)) for i in range(2)]
            w3b = [e4(nc.sbuf_tensor(f"w3b{i}", [128, D], F32R)) for i in range(2)]
            gs = [e4(nc.sbuf_tensor(f"gs{i}", [128, 512], F32)) for i in range(2)]
            ps_g = [e4(nc.psum_tensor(f"ps_g{i}", [128, 512], F32)) for i in range(2)]
            ps_u = [e4(nc.psum_tensor(f"ps_u{i}", [128, 512], F32)) for i in range(2)]
            for s in ("p4_pre", "p4_wA", "p4_wB", "p4_mm", "p4_sl", "p4_ev"):
                ctx.sem(s)
            p4_w = [ctx.sems["p4_wA"], ctx.sems["p4_wB"]]
            gst0 = ctx.gst

            with nc.Block("l0_up") as blk:
                @blk.sync
                def _(sy):
                    for hf in range(2):
                        if hf == 0 and gst0:
                            sy.wait_ge(ctx.sems["gst"], gst0)
                        if hf == 1:
                            sy.wait_ge(ctx.sems["p4_mm"], NHC)
                        for c in range(NCH):
                            sy.dma_start(
                                out=xh[c][:, :],
                                in_=src.ap()[c * 128:(c + 1) * 128,
                                             hf * 512:(hf + 1) * 512],
                            ).then_inc(ctx.sems["p4_pre"], 16)
                        for hc in range(NHC):
                            g = hf * NHC + hc
                            if g >= 2:
                                sy.wait_ge(ctx.sems["p4_mm"], g - 1)
                            sy.dma_start(out=w1b[g % 2][:, :],
                                         in_=_wblock(w["l0_w1"], hc)
                                         ).then_inc(p4_w[g % 2], 16)
                            sy.dma_start(out=w3b[g % 2][:, :],
                                         in_=_wblock(w["l0_w3"], hc)
                                         ).then_inc(p4_w[g % 2], 16)

                @blk.tensor
                def _(te):
                    for hf in range(2):
                        for hc in range(NHC):
                            g = hf * NHC + hc
                            if hc == 0:
                                te.wait_ge(ctx.sems["p4_pre"], 16 * 16 * (hf + 1))
                            te.wait_ge(p4_w[g % 2], 32 * (g // 2 + 1))
                            if g >= 2:
                                te.wait_ge(ctx.sems["p4_ev"], g - 1)
                            for k in range(NCH):
                                mm = te.matmul(ps_g[g % 2][:, :],
                                               w1b[g % 2][:, k * 128:(k + 1) * 128],
                                               xh[k][:, :],
                                               start=(k == 0), stop=(k == NCH - 1))
                            for k in range(NCH):
                                mm = te.matmul(ps_u[g % 2][:, :],
                                               w3b[g % 2][:, k * 128:(k + 1) * 128],
                                               xh[k][:, :],
                                               start=(k == 0), stop=(k == NCH - 1))
                            mm.then_inc(ctx.sems["p4_mm"], 1)

                @blk.scalar
                def _(sc):
                    for g in range(2 * NHC):
                        sc.wait_ge(ctx.sems["p4_mm"], g + 1)
                        if g >= 2:
                            sc.wait_ge(ctx.sems["p4_ev"], g - 1)
                        sc.activation(gs[g % 2][:, :], ps_g[g % 2][:, :],
                                      AF.Silu).then_inc(ctx.sems["p4_sl"], 1)

                @blk.vector
                def _(ve):
                    for hf in range(2):
                        for hc in range(NHC):
                            g = hf * NHC + hc
                            ve.wait_ge(ctx.sems["p4_sl"], g + 1)
                            ve.tensor_mul(
                                h[hc][:, hf * 512:(hf + 1) * 512],
                                gs[g % 2][:, :], ps_u[g % 2][:, :],
                            ).then_inc(ctx.sems["p4_ev"], 1)

        if "uponly" in G2_SKIP:
            with nc.Block("dbg_g2") as blk:
                @blk.sync
                def _(sy):
                    for j in range(NCH):
                        sy.dma_start(out=dst.ap()[j * 128:(j + 1) * 128, :],
                                     in_=src.ap()[j * 128:(j + 1) * 128, :]
                                     ).then_inc(ctx.sem("gst"), 16)
            ctx.gst += 16 * NCH
            return

        if "fwht" in G2_SKIP:
            p5 = None
        with ExitStack() as p5:
            e5 = p5.enter_context
            ta = e5(nc.sbuf_tensor("fw_ta", [128, T], BF16))
            tb = e5(nc.sbuf_tensor("fw_tb", [128, T], BF16))
            w128sb = e5(nc.sbuf_tensor("w128sb", [128, 128], BF16))
            ps_w = [e5(nc.psum_tensor(f"ps_w{i}", [128, T], F32)) for i in range(2)]
            for s in ("p5_pre", "p5_ch", "p5_mm", "p5_ev"):
                ctx.sem(s)

            with nc.Block("l0_fwht") as blk:
                if "fwht" in G2_SKIP:
                    blk
                @blk.sync
                def _(sy):
                    if "fwht" in G2_SKIP:
                        return
                    sy.dma_start(out=w128sb[:, :],
                                 in_=w["w128_bf"][:, :]).then_inc(
                        ctx.sems["p5_pre"], 16)

                @blk.vector
                def _(ve):
                    if "fwht" in G2_SKIP:
                        return
                    ch = ctx.sems["p5_ch"]
                    n = 0
                    for s in range(6):
                        step = 1 << s
                        pairs = [(a, a + step) for a in range(NHC)
                                 if (a // step) % 2 == 0]
                        for (a, b) in pairs:
                            ve.tensor_add(ta[:, :], h[a][:, :], h[b][:, :]
                                          ).then_inc(ch, 1)
                            ve.tensor_sub(tb[:, :], h[a][:, :], h[b][:, :]
                                          ).then_inc(ch, 1)
                            n += 2
                            ve.wait_ge(ch, n)
                            ve.tensor_copy(h[a][:, :], ta[:, :]).then_inc(ch, 1)
                            ve.tensor_copy(h[b][:, :], tb[:, :]).then_inc(ch, 1)
                            n += 2
                            ve.wait_ge(ch, n)
                    for hc in range(NHC):
                        ve.wait_ge(ctx.sems["p5_mm"], hc + 1)
                        ve.tensor_copy(h[hc][:, :], ps_w[hc % 2][:, :]).then_inc(
                            ctx.sems["p5_ev"], 1)

                @blk.tensor
                def _(te):
                    if "fwht" in G2_SKIP:
                        return
                    te.wait_ge(ctx.sems["p5_pre"], 16)
                    te.wait_ge(ctx.sems["p5_ch"], 4 * 6 * 32)
                    for hc in range(NHC):
                        if hc >= 2:
                            te.wait_ge(ctx.sems["p5_ev"], hc - 1)
                        for half in range(2):
                            mm = te.matmul(
                                ps_w[hc % 2][:, half * 512:(half + 1) * 512],
                                w128sb[:, :],
                                h[hc][:, half * 512:(half + 1) * 512],
                                start=True, stop=True)
                        mm.then_inc(ctx.sems["p5_mm"], 1)

        with ExitStack() as p6:
            e6 = p6.enter_context
            w2r = [e6(nc.sbuf_tensor(f"w2r{i}", [128, D], F32R)) for i in range(2)]
            w2q = [e6(nc.sbuf_tensor(f"w2q{i}", [128, D], BF16)) for i in range(2)]
            xres = [e6(nc.sbuf_tensor(f"xr{i}", [128, T], F32R)) for i in range(2)]
            ybuf = [e6(nc.sbuf_tensor(f"yb{i}", [128, T], F32R)) for i in range(2)]
            ps_y = [e6(nc.psum_tensor(f"ps_y{i}", [128, T], F32)) for i in range(2)]
            ps_stat = e6(nc.psum_tensor("ps_stat", [128, T], F32))
            for s in ("p6_wA", "p6_wB", "p6_xA", "p6_xB", "p6_c", "p6_mmq",
                      "p6_sj", "p6_res", "p6_stA", "p6_stB", "p6_sums"):
                ctx.sem(s)
            p6_w = [ctx.sems["p6_wA"], ctx.sems["p6_wB"]]
            p6_x = [ctx.sems["p6_xA"], ctx.sems["p6_xB"]]
            p6_st = [ctx.sems["p6_stA"], ctx.sems["p6_stB"]]
            SC128 = 1.0 / math.sqrt(float(HFF0))

            with nc.Block("l0_down") as blk:
                @blk.sync
                def _(sy):
                    for j in range(NCH):
                        if j >= 2:
                            sy.wait_ge(ctx.sems["p6_res"], j - 1)
                        sy.dma_start(out=xres[j % 2][:, :],
                                     in_=src.ap()[j * 128:(j + 1) * 128, :]
                                     ).then_inc(p6_x[j % 2], 16)
                        for q in range(4):
                            gq = j * 4 + q
                            if gq >= 2:
                                sy.wait_ge(ctx.sems["p6_c"], gq - 1)
                            sy.dma_start(
                                out=w2r[gq % 2][:, :],
                                in_=_wblock(w["l0_w2"], j, q * 2048, 2048),
                            ).then_inc(p6_w[gq % 2], 16)
                        sy.wait_ge(ctx.sems["p6_res"], j + 1)
                        sy.dma_start(out=spill.ap()[j * 128:(j + 1) * 128, :],
                                     in_=ybuf[j % 2][:, :]).then_inc(
                            p6_st[j % 2], 16)

                @blk.vector
                def _(ve):
                    for j in range(NCH):
                        for q in range(4):
                            gq = j * 4 + q
                            ve.wait_ge(p6_w[gq % 2], 16 * (gq // 2 + 1))
                            if gq >= 2:
                                ve.wait_ge(ctx.sems["p6_mmq"], gq - 1)
                            ve.tensor_scalar_mul(w2q[gq % 2][:, :],
                                                 w2r[gq % 2][:, :], SC128
                                                 ).then_inc(ctx.sems["p6_c"], 1)
                        ve.wait_ge(ctx.sems["p6_mmq"], 4 * (j + 1))
                        if j >= 2:
                            ve.wait_ge(p6_st[j % 2], 16 * ((j - 2) // 2 + 1))
                            ve.wait_ge(ctx.sems["p6_sums"], j - 1)
                        ve.wait_ge(p6_x[j % 2], 16 * (j // 2 + 1))
                        ve.scalar_tensor_tensor(
                            ybuf[j % 2][:, :], ps_y[j % 2][:, :], 1.0,
                            xres[j % 2][:, :], ALU.mult, ALU.add,
                        ).then_inc(ctx.sems["p6_res"], 1)
                        ve.wait_ge(ctx.sems["p6_sj"], j + 1)
                        if j == 0:
                            ve.tensor_scalar_mul(stat_acc[:, :], ps_stat[:, :],
                                                 1.0).then_inc(
                                ctx.sems["p6_sums"], 1)
                        else:
                            ve.tensor_add(stat_acc[:, :], stat_acc[:, :],
                                          ps_stat[:, :]).then_inc(
                                ctx.sems["p6_sums"], 1)

                @blk.tensor
                def _(te):
                    for j in range(NCH):
                        for q in range(4):
                            gq = j * 4 + q
                            te.wait_ge(ctx.sems["p6_c"], gq + 1)
                            if j >= 2 and q == 0:
                                te.wait_ge(ctx.sems["p6_res"], j - 1)
                            for f in range(16):
                                hc = q * 16 + f
                                for half in range(2):
                                    mm = te.matmul(
                                        ps_y[j % 2][:, half * 512:(half + 1) * 512],
                                        w2q[gq % 2][:, f * 128:(f + 1) * 128],
                                        h[hc][:, half * 512:(half + 1) * 512],
                                        start=(hc == 0), stop=(hc == NHC - 1))
                            mm.then_inc(ctx.sems["p6_mmq"], 1)
                        te.wait_ge(ctx.sems["p6_res"], j + 1)
                        if j >= 1:
                            te.wait_ge(ctx.sems["p6_sums"], j)
                        for half in range(2):
                            mm = te.matmul(
                                ps_stat[:, half * 512:(half + 1) * 512],
                                ctx.sb["ones"][:, :],
                                ybuf[j % 2][:, half * 512:(half + 1) * 512],
                                start=True, stop=True)
                        mm.then_inc(ctx.sems["p6_sj"], 1)

        ph_h.close()

        if "noln" in G2_SKIP:
            with ExitStack() as pdbg:
                bt = pdbg.enter_context(nc.sbuf_tensor("dbg_bt", [128, T], F32R))
                with nc.Block("dbg_g2c") as blk:
                    @blk.sync
                    def _(sy):
                        sy.wait_ge(p6_st[0], 16 * 8)
                        sy.wait_ge(p6_st[1], 16 * 8)
                        n = 0
                        for j in range(NCH):
                            sy.dma_start(out=bt[:, :],
                                         in_=spill.ap()[j * 128:(j + 1) * 128, :]
                                         ).then_inc(ctx.sem("dbgl"), 16)
                            n += 16
                            sy.wait_ge(ctx.sems["dbgl"], n)
                            sy.dma_start(out=dst.ap()[j * 128:(j + 1) * 128, :],
                                         in_=bt[:, :]).then_inc(ctx.sem("gst"), 16)
                ctx.gst += 16 * NCH
            return

        with ExitStack() as p6b:
            eb = p6b.enter_context
            stream = [eb(nc.sbuf_tensor(f"s6_{c}", [128, T], F32R))
                      for c in range(NCH)]
            sq_buf = [eb(nc.sbuf_tensor(f"sq6_{i}", [128, T], F32R))
                      for i in range(2)]
            rstd_sb = eb(nc.sbuf_tensor("rstd6", [128, T], F32))
            rstd2_sb = eb(nc.sbuf_tensor("rstd6b", [128, T], F32))
            ps_var = eb(nc.psum_tensor("ps6_var", [128, T], F32))
            for s in ("p6b_lA", "p6b_lB"):
                ctx.sem(s)
            p6b_l = [ctx.sems["p6b_lA"], ctx.sems["p6b_lB"]]

            ln = LN(ctx, "ln2", stream, ctx.sb["ones"], sq_buf, rstd_sb,
                    rstd2_sb, stat_acc, ps_var,
                    ("p6b_lA", lambda j: 16 * (j // 2 + 1)), dst, sum_done=True,
                    ve_rdy=(p6b_l, lambda j: 16 * (j // 2 + 1)))

            with nc.Block("l0_ln2") as blk:
                @blk.sync
                def _(sy):
                    sy.wait_ge(p6_st[0], 16 * 8)
                    sy.wait_ge(p6_st[1], 16 * 8)
                    for j in range(NCH):
                        sy.dma_start(out=stream[j][:, :],
                                     in_=spill.ap()[j * 128:(j + 1) * 128, :]
                                     ).then_inc(p6b_l[j % 2], 16)
                    ln.store(sy)

                @blk.tensor
                def _(te):
                    ln.te(te)

                @blk.vector
                def _(ve):
                    ln.ve(ve)

                @blk.scalar
                def _(sc):
                    ln.sc(sc)


def _group_l1_attn(ctx, src, dst):
    nc = ctx.nc
    w = ctx.dram["w"]
    shr = ctx.dram["shr"]
    SQ8 = 1.0 / 8.0

    with ExitStack() as ph:
        e = ph.enter_context
        stream = [e(nc.sbuf_tensor(f"s7_{c}", [128, T], F32R)) for c in range(NCH)]
        ctxh = [e(nc.sbuf_tensor(f"cx{c}", [128, T], BF16)) for c in range(NCH)]
        q_bf = [e(nc.sbuf_tensor(f"q7_{c}", [128, T], BF16)) for c in range(NCH)]
        ident = e(nc.sbuf_tensor("id7", [128, 128], BF16))
        diag_sb = e(nc.sbuf_tensor("dg7", [128, 128], F32))

        with ExitStack() as p7:
            e7 = p7.enter_context
            wbuf = [e7(nc.sbuf_tensor(f"w7_{i}", [128, D], F32R)) for i in range(2)]
            qf = [e7(nc.sbuf_tensor(f"qf{i}", [128, T], F32R)) for i in range(2)]
            kobf = [e7(nc.sbuf_tensor(f"ko{i}", [128, T], BF16)) for i in range(2)]
            tmp1 = e7(nc.sbuf_tensor("rt1", [128, T], F32))
            tmp2 = e7(nc.sbuf_tensor("rt2", [128, T], F32))
            cosb = e7(nc.sbuf_tensor("cosb", [128, T], F32))
            sinb = e7(nc.sbuf_tensor("sinb", [128, T], F32))
            rswb = e7(nc.sbuf_tensor("rswb", [128, 128], F32R))
            ps_q = [e7(nc.psum_tensor(f"pq{i}", [128, T], F32)) for i in range(2)]
            ps_r = [e7(nc.psum_tensor(f"pr{i}", [128, T], F32)) for i in range(2)]
            for s in ("p7_pre", "p7_wA", "p7_wB", "p7_mm", "p7_qe", "p7_rm",
                      "p7_t1", "p7_t2", "p7_qd", "p7_kwrA", "p7_kwrB",
                      "p7_vm", "p7_ve", "p7_vwrA", "p7_vwrB"):
                ctx.sem(s)
            p7_w = [ctx.sems["p7_wA"], ctx.sems["p7_wB"]]
            p7_kwr = [ctx.sems["p7_kwrA"], ctx.sems["p7_kwrB"]]
            p7_vwr = [ctx.sems["p7_vwrA"], ctx.sems["p7_vwrB"]]
            gst0 = ctx.gst
            NB = 20
            ctx.magic_seq += 1
            magic = MAGIC_BASE + ctx.magic_seq

            with nc.Block("l1_qkv") as blk:
                @blk.sync
                def _(sy):
                    sem = ctx.sems["p7_pre"]
                    if gst0:
                        sy.wait_ge(ctx.sems["gst"], gst0)
                    for c in range(NCH):
                        sy.dma_start(out=stream[c][:, :],
                                     in_=src.ap()[c * 128:(c + 1) * 128, :]
                                     ).then_inc(sem, 16)
                    sy.dma_start(out=ident[:, :], in_=w["ident_bf"][:, :]
                                 ).then_inc(sem, 16)
                    sy.dma_start(out=diag_sb[:, :], in_=w["diag_mask"][:, :]
                                 ).then_inc(sem, 16)
                    sy.dma_start(out=cosb[:, :], in_=w["cos_rep"][:, :]
                                 ).then_inc(sem, 16)
                    sy.dma_start(out=sinb[:, :], in_=w["sin_rep"][:, :]
                                 ).then_inc(sem, 16)
                    sy.dma_start(out=rswb[:, :], in_=w["rsw"][:, :]
                                 ).then_inc(sem, 16)
                    for i in range(NB):
                        mat, jb = ("l1_wq", i) if i < 16 else ("l1_wk", i - 16)
                        if i >= 2:
                            sy.wait_ge(ctx.sems["p7_mm"], i - 1)
                        sy.dma_start(out=wbuf[i % 2][:, :],
                                     in_=_wblock(w[mat], jb)
                                     ).then_inc(p7_w[i % 2], 16)

                @blk.tensor
                def _(te):
                    def rot(i):
                        te.wait_ge(ctx.sems["p7_qe"], i + 1)
                        if i >= 2:
                            te.wait_ge(ctx.sems["p7_t2"], i - 1)
                        for half in range(2):
                            mm = te.matmul(
                                ps_r[i % 2][:, half * 512:(half + 1) * 512],
                                rswb[:, :],
                                qf[i % 2][:, half * 512:(half + 1) * 512],
                                start=True, stop=True)
                        mm.then_inc(ctx.sems["p7_rm"], 1)

                    for i in range(NB):
                        if i == 0:
                            te.wait_ge(ctx.sems["p7_pre"], 16 * 21)
                        te.wait_ge(p7_w[i % 2], 16 * (i // 2 + 1))
                        if i >= 2:
                            te.wait_ge(ctx.sems["p7_qe"], i - 1)
                        for k in range(NCH):
                            for half in range(2):
                                mm = te.matmul(
                                    ps_q[i % 2][:, half * 512:(half + 1) * 512],
                                    wbuf[i % 2][:, k * 128:(k + 1) * 128],
                                    stream[k][:, half * 512:(half + 1) * 512],
                                    start=(k == 0), stop=(k == NCH - 1))
                        mm.then_inc(ctx.sems["p7_mm"], 1)
                        if i >= 1:
                            rot(i - 1)
                    rot(NB - 1)

                @blk.scalar
                def _(sc):
                    for i in range(NB):
                        sc.wait_ge(ctx.sems["p7_mm"], i + 1)
                        if i >= 2:
                            sc.wait_ge(ctx.sems["p7_t1"], i - 1)
                        scale = SQ8 if i < 16 else 1.0
                        sc.activation(qf[i % 2][:, :], ps_q[i % 2][:, :],
                                      AF.Identity, bias=0.0, scale=scale
                                      ).then_inc(ctx.sems["p7_qe"], 1)


                @blk.vector
                def _(ve):
                    for i in range(NB):
                        ve.wait_ge(ctx.sems["p7_qe"], i + 1)
                        if i >= 1:
                            ve.wait_ge(ctx.sems["p7_qd"], i)
                        ve.tensor_mul(tmp1[:, :], qf[i % 2][:, :], cosb[:, :]
                                      ).then_inc(ctx.sems["p7_t1"], 1)
                        ve.wait_ge(ctx.sems["p7_rm"], i + 1)
                        ve.tensor_mul(tmp2[:, :], ps_r[i % 2][:, :], sinb[:, :]
                                      ).then_inc(ctx.sems["p7_t2"], 1)
                        ve.wait_ge(ctx.sems["p7_t1"], i + 1)
                        ve.wait_ge(ctx.sems["p7_t2"], i + 1)
                        dstt = q_bf[i] if i < 16 else kobf[(i - 16) % 2]
                        if i >= 18:
                            ve.wait_ge(p7_kwr[i % 2], 16 * ((i - 18) // 2 + 1))
                        ve.tensor_add(dstt[:, :], tmp1[:, :], tmp2[:, :]
                                      ).then_inc(ctx.sems["p7_qd"], 1)

                @blk.gpsimd
                def _(gp):
                    def emit(slot):
                        for i in range(4):
                            gp.wait_ge(ctx.sems["p7_qd"], 17 + i)
                            out = shr["k1"].ap()[slot * 512 + i * 128:
                                                 slot * 512 + (i + 1) * 128, :]
                            gp.dma_start(out=out, in_=kobf[i % 2][:, :]
                                         ).then_inc(p7_kwr[i % 2], 16)
                    with gp.If_eq(ctx.regs["rank"], 0):
                        emit(0)
                    with gp.Else():
                        emit(1)

        with ExitStack() as p7b:
            e7b = p7b.enter_context
            wv_sb = [e7b(nc.sbuf_tensor(f"wv{k}", [128, 512], F32R))
                     for k in range(NCH)]
            vt = [e7b(nc.sbuf_tensor(f"vt7{i}", [128, 512], BF16))
                  for i in range(2)]
            ps_v = [e7b(nc.psum_tensor(f"pv7{i}", [128, 512], F32))
                    for i in range(2)]
            with nc.Block("l1_v") as blk:
                @blk.sync
                def _(sy):
                    for k in range(NCH):
                        sy.dma_start(out=wv_sb[k][:, :],
                                     in_=w["l1_wv_r"].ap()[k * 128:(k + 1) * 128, :]
                                     ).then_inc(ctx.sem("p7b_pre"), 16)

                @blk.tensor
                def _(te):
                    te.wait_ge(ctx.sems["p7b_pre"], 16 * 16)
                    for tt in range(8):
                        if tt >= 2:
                            te.wait_ge(ctx.sems["p7_ve"], tt - 1)
                        for k in range(NCH):
                            mm = te.matmul(
                                ps_v[tt % 2][:, :],
                                stream[k][:, tt * 128:(tt + 1) * 128],
                                wv_sb[k][:, :],
                                start=(k == 0), stop=(k == NCH - 1))
                        mm.then_inc(ctx.sems["p7_vm"], 1)

                @blk.scalar
                def _(sc):
                    for tt in range(8):
                        sc.wait_ge(ctx.sems["p7_vm"], tt + 1)
                        if tt >= 2:
                            sc.wait_ge(p7_vwr[tt % 2], 16 * ((tt - 2) // 2 + 1))
                        sc.activation(vt[tt % 2][:, :], ps_v[tt % 2][:, :],
                                      AF.Identity, bias=0.0, scale=1.0
                                      ).then_inc(ctx.sems["p7_ve"], 1)

                @blk.gpsimd
                def _(gp):
                    def emit(slot):
                        for tt in range(8):
                            gp.wait_ge(ctx.sems["p7_ve"], tt + 1)
                            out = shr["v1"].ap()[slot * 128:(slot + 1) * 128,
                                                 tt * 512:(tt + 1) * 512]
                            gp.dma_start(out=out, in_=vt[tt % 2][:, :]
                                         ).then_inc(p7_vwr[tt % 2], 16)
                        gp.wait_ge(p7_kwr[0], 16 * 2)
                        gp.wait_ge(p7_kwr[1], 16 * 2)
                        gp.wait_ge(p7_vwr[0], 16 * 4)
                        gp.wait_ge(p7_vwr[1], 16 * 4)
                        _exchange_flag(gp, ctx, slot, magic)
                    with gp.If_eq(ctx.regs["rank"], 0):
                        emit(0)
                    with gp.Else():
                        emit(1)
                    _exchange_poll(gp, ctx, magic)
                    gp.sem_inc(ctx.sem("p7_xch"), 1)

        with ExitStack() as p8:
            e8 = p8.enter_context
            kg = [e8(nc.sbuf_tensor(f"kg{i}", [128, 2048], BF16)) for i in range(2)]
            v_all = e8(nc.sbuf_tensor("vall", [128, 16 * 512], BF16))
            p_bf = [e8(nc.sbuf_tensor(f"p8b{i}", [128, 2048], BF16))
                    for i in range(2)]
            pt_sb = [e8(nc.sbuf_tensor(f"pt8{i}", [128, 2048], BF16))
                     for i in range(2)]
            negmax = [e8(nc.sbuf_tensor(f"nm8{i}", [128, 1], F32)) for i in range(2)]
            sumexp = [e8(nc.sbuf_tensor(f"se8{i}", [128, 1], F32)) for i in range(2)]
            rsum = [e8(nc.sbuf_tensor(f"rs8{i}", [128, 1], F32)) for i in range(2)]
            ps_sc = e8(nc.psum_tensor("ps8_sc", [128, 2048], F32))
            ps_pt = e8(nc.psum_tensor("ps8_pt", [128, 2048], BF16))
            ps_o = [e8(nc.psum_tensor(f"ps8_o{i}", [128, 512], F32))
                    for i in range(2)]
            for s in ("p8_ldA", "p8_ldB", "p8_vl", "p8_sc", "p8_msk", "p8_mx",
                      "p8_ex", "p8_pn", "p8_tr", "p8_pte", "p8_mm", "p8_oe",
                      "p8_rc"):
                ctx.sem(s)
            p8_ld = [ctx.sems["p8_ldA"], ctx.sems["p8_ldB"]]
            NIT = 8 * 4 * 8

            def its():
                out = []
                for g in range(8):
                    for hh in range(4):
                        for qt in range(8):
                            out.append((g, hh, qt))
                return out

            ITS = its()

            with nc.Block("l1_attn") as blk:
                @blk.sync
                def _(sy):
                    sy.wait_ge(ctx.sems["p7_xch"], 1)
                    for slot in range(2):
                        sy.dma_start(
                            out=v_all[:, slot * 4096:(slot + 1) * 4096],
                            in_=shr["v1"].ap()[slot * 128:(slot + 1) * 128, :],
                        ).then_inc(ctx.sems["p8_vl"], 16)
                    for g in range(8):
                        if g >= 2:
                            sy.wait_ge(ctx.sems["p8_mm"], 32 * (g - 1))
                        for slot in range(2):
                            for dup in range(2):
                                sy.dma_start(
                                    out=kg[g % 2][dup * 64:dup * 64 + 64,
                                                  slot * 1024:(slot + 1) * 1024],
                                    in_=shr["k1"].ap()[slot * 512 + g * 64:
                                                       slot * 512 + (g + 1) * 64, :],
                                ).then_inc(p8_ld[g % 2], 16)

                def q_slice(hh, g, qt):
                    H = g * 4 + hh
                    ch, par = H // 2, H % 2
                    return q_bf[ch][par * 64:par * 64 + 64,
                                    qt * 128:(qt + 1) * 128]

                def _S4(te, it, rank):
                    g, hh, qt = ITS[it]
                    nk = len(_l1_kept(rank, qt))
                    par = it % 2
                    te.wait_ge(ctx.sems["p8_pn"], it + 1)
                    for i in range(nk):
                        mm = te.transpose(ps_pt[:, i * 128:(i + 1) * 128],
                                          p_bf[par][:, i * 128:(i + 1) * 128],
                                          ident[:, :])
                    mm.then_inc(ctx.sems["p8_tr"], 1)

                def _S6(te, it, rank):
                    g, hh, qt = ITS[it]
                    par = it % 2
                    kept = _l1_kept(rank, qt)
                    te.wait_ge(ctx.sems["p8_pte"], it + 1)
                    if it >= 2:
                        te.wait_ge(ctx.sems["p8_oe"], it - 1)
                    for i, (kt, kind) in enumerate(kept):
                        mm = te.matmul(
                            ps_o[par][0:64, 0:128],
                            v_all[:, kt * 512 + g * 64:kt * 512 + g * 64 + 64],
                            pt_sb[par][:, i * 128:(i + 1) * 128],
                            start=(i == 0), stop=(i == len(kept) - 1))
                    mm.then_inc(ctx.sems["p8_mm"], 1)

                def te_body(te, rank):
                    for it, (g, hh, qt) in enumerate(ITS):
                        kept = _l1_kept(rank, qt)
                        par = it % 2
                        H = g * 4 + hh
                        hb = (H % 2) * 64
                        if it == 0:
                            te.wait_ge(ctx.sems["p8_vl"], 32)
                        if qt == 0 and hh == 0:
                            te.wait_ge(p8_ld[g % 2], 64 * (g // 2 + 1))
                        if it >= 1:
                            te.wait_ge(ctx.sems["p8_ex"], it)
                        for i, (kt, kind) in enumerate(kept):
                            mm = te.matmul(
                                ps_sc[:, i * 128:(i + 1) * 128],
                                q_slice(hh, g, qt),
                                kg[g % 2][hb:hb + 64, kt * 128:(kt + 1) * 128],
                                start=True, stop=True)
                        mm.then_inc(ctx.sems["p8_sc"], 1)
                        if it >= 1:
                            _S4(te, it - 1, rank)
                            _S6(te, it - 1, rank)
                    _S4(te, NIT - 1, rank)
                    _S6(te, NIT - 1, rank)

                @blk.tensor
                def _(te):
                    with te.If_eq(ctx.regs["rank_te"], 0):
                        te_body(te, 0)
                    with te.Else():
                        te_body(te, 1)

                def ve_body(ve, rank):
                    for it, (g, hh, qt) in enumerate(ITS):
                        kept = _l1_kept(rank, qt)
                        nk = len(kept)
                        wk = nk * 128
                        par = it % 2
                        di = [i for i, (kt, kind) in enumerate(kept)
                              if kind == "diag"][0]
                        ve.wait_ge(ctx.sems["p8_sc"], it + 1)
                        ve.tensor_add(ps_sc[:, di * 128:(di + 1) * 128],
                                      ps_sc[:, di * 128:(di + 1) * 128],
                                      diag_sb[:, :]).then_inc(
                            ctx.sems["p8_msk"], 1)
                        if it >= 2:
                            ve.wait_ge(ctx.sems["p8_ex"], it - 1)
                        ve.wait_ge(ctx.sems["p8_msk"], it + 1)
                        ve.reduce_max(negmax[par][:, :], ps_sc[:, 0:wk],
                                      axis=AX.X, negate=True).then_inc(
                            ctx.sems["p8_mx"], 1)
                        if it >= 1:
                            parp = (it - 1) % 2
                            ve.wait_ge(ctx.sems["p8_ex"], it)
                            ve.reciprocal(rsum[parp][:, :], sumexp[parp][:, :]
                                          ).then_inc(ctx.sems["p8_rc"], 1)
                            ve.wait_ge(ctx.sems["p8_rc"], it)
                            wkp = len(_l1_kept(rank, ITS[it - 1][2])) * 128
                            ve.tensor_scalar_mul(
                                p_bf[parp][:, 0:wkp], p_bf[parp][:, 0:wkp],
                                rsum[parp][:, 0:1]).then_inc(ctx.sems["p8_pn"], 1)
                    parp = (NIT - 1) % 2
                    ve.wait_ge(ctx.sems["p8_ex"], NIT)
                    ve.reciprocal(rsum[parp][:, :], sumexp[parp][:, :]).then_inc(
                        ctx.sems["p8_rc"], 1)
                    ve.wait_ge(ctx.sems["p8_rc"], NIT)
                    wkp = len(_l1_kept(rank, ITS[NIT - 1][2])) * 128
                    ve.tensor_scalar_mul(p_bf[parp][:, 0:wkp],
                                         p_bf[parp][:, 0:wkp],
                                         rsum[parp][:, 0:1]).then_inc(
                        ctx.sems["p8_pn"], 1)

                @blk.vector
                def _(ve):
                    with ve.If_eq(ctx.regs["rank_ve"], 0):
                        ve_body(ve, 0)
                    with ve.Else():
                        ve_body(ve, 1)

                def sc_body(sc, rank):
                    def S5(it):
                        par = it % 2
                        wk = len(_l1_kept(rank, ITS[it][2])) * 128
                        sc.wait_ge(ctx.sems["p8_tr"], it + 1)
                        sc.copy(pt_sb[par][:, 0:wk], ps_pt[:, 0:wk]).then_inc(
                            ctx.sems["p8_pte"], 1)

                    def S7(it):
                        g, hh, qt = ITS[it]
                        H = g * 4 + hh
                        ch, hp = H // 2, (H % 2) * 64
                        par = it % 2
                        sc.wait_ge(ctx.sems["p8_mm"], it + 1)
                        sc.copy(ctxh[ch][hp:hp + 64, qt * 128:(qt + 1) * 128],
                                ps_o[par][0:64, 0:128]).then_inc(
                            ctx.sems["p8_oe"], 1)

                    for it, (g, hh, qt) in enumerate(ITS):
                        par = it % 2
                        wk = len(_l1_kept(rank, qt)) * 128
                        sc.wait_ge(ctx.sems["p8_mx"], it + 1)
                        if it >= 2:
                            sc.wait_ge(ctx.sems["p8_tr"], it - 1)
                        sc.activation(p_bf[par][:, 0:wk], ps_sc[:, 0:wk],
                                      AF.Exp, bias=negmax[par][:, 0:1],
                                      scale=1.0,
                                      accum_out=sumexp[par][:, :]).then_inc(
                            ctx.sems["p8_ex"], 1)
                        if it >= 1:
                            S5(it - 1)
                        if it >= 2:
                            S7(it - 2)
                    S5(NIT - 1)
                    S7(NIT - 2)
                    S7(NIT - 1)

                @blk.scalar
                def _(sc):
                    with sc.If_eq(ctx.regs["rank_sc"], 0):
                        sc_body(sc, 0)
                    with sc.Else():
                        sc_body(sc, 1)

        with ExitStack() as p9:
            e9 = p9.enter_context
            wbuf = [e9(nc.sbuf_tensor(f"w9_{i}", [128, D], F32R)) for i in range(2)]
            wo_bf = [e9(nc.sbuf_tensor(f"wob{i}", [128, D], BF16)) for i in range(2)]
            sq_buf = [e9(nc.sbuf_tensor(f"sq9{i}", [128, T], F32R))
                      for i in range(2)]
            rstd_sb = e9(nc.sbuf_tensor("rstd9", [128, T], F32))
            rstd2_sb = e9(nc.sbuf_tensor("rstd9b", [128, T], F32))
            ps_sum = e9(nc.psum_tensor("ps9_sum", [128, T], F32))
            ps_var = e9(nc.psum_tensor("ps9_var", [128, T], F32))
            psq = [e9(nc.psum_tensor(f"ps9_{i}", [128, T], F32)) for i in range(2)]
            for s in ("p9_wA", "p9_wB", "p9_c", "p9_mm", "p9_res"):
                ctx.sem(s)
            p9_w = [ctx.sems["p9_wA"], ctx.sems["p9_wB"]]
            ln = LN(ctx, "ln3", stream, ctx.sb["ones"], sq_buf, rstd_sb,
                    rstd2_sb, ps_sum, ps_var, ("p9_res", lambda j: j + 1), dst)

            with nc.Block("l1_wo") as blk:
                @blk.sync
                def _(sy):
                    for j in range(NCH):
                        if j >= 2:
                            sy.wait_ge(ctx.sems["p9_c"], j - 1)
                        sy.dma_start(out=wbuf[j % 2][:, :],
                                     in_=_wblock(w["l1_wo"], j)
                                     ).then_inc(p9_w[j % 2], 16)
                    ln.store(sy)

                @blk.vector
                def _(ve):
                    for j in range(NCH):
                        ve.wait_ge(p9_w[j % 2], 16 * (j // 2 + 1))
                        if j >= 2:
                            ve.wait_ge(ctx.sems["p9_mm"], j - 1)
                        ve.tensor_copy(wo_bf[j % 2][:, :], wbuf[j % 2][:, :]
                                       ).then_inc(ctx.sems["p9_c"], 1)
                        ve.wait_ge(ctx.sems["p9_mm"], j + 1)
                        ve.scalar_tensor_tensor(
                            stream[j][:, :], psq[j % 2][:, :], 0.0,
                            stream[j][:, :], ALU.add, ALU.add).then_inc(
                            ctx.sems["p9_res"], 1)
                    ln.ve(ve)

                @blk.tensor
                def _(te):
                    for j in range(NCH):
                        te.wait_ge(ctx.sems["p9_c"], j + 1)
                        if j >= 2:
                            te.wait_ge(ctx.sems["p9_res"], j - 1)
                        for oc in range(NCH):
                            for half in range(2):
                                mm = te.matmul(
                                    psq[j % 2][:, half * 512:(half + 1) * 512],
                                    wo_bf[j % 2][:, oc * 128:(oc + 1) * 128],
                                    ctxh[oc][:, half * 512:(half + 1) * 512],
                                    start=(oc == 0), stop=(oc == NCH - 1))
                        mm.then_inc(ctx.sems["p9_mm"], 1)
                    ln.te(te)

                @blk.scalar
                def _(sc):
                    ln.sc(sc)


def _group_l1_ffn(ctx, src, dst):
    nc = ctx.nc
    w = ctx.dram["w"]
    spill = ctx.dram["spill"]
    NHC1 = HFF1 // 128

    with ExitStack() as ph:
        e = ph.enter_context
        stat_acc = e(nc.sbuf_tensor("stat10", [128, T], F32))
        rgb = [e(nc.sbuf_tensor(f"rgb{m}", [128, 1], F32)) for m in range(3)]
        magic_sb = e(nc.sbuf_tensor("magic10", [128, 1], F32))

        with ExitStack() as pa:
            ea = pa.enter_context
            gbuf = [ea(nc.sbuf_tensor(f"gb{i}", [128, HFF1], F32))
                    for i in range(2)]
            tmp_r = ea(nc.sbuf_tensor("gtmp", [128, 1], F32))
            acc = ea(nc.sbuf_tensor("gacc", [128, 1], F32))
            idf = ea(nc.sbuf_tensor("idf", [128, 128], F32))
            gsum = ea(nc.sbuf_tensor("gsum", [1, 1], F32))
            gt2 = ea(nc.sbuf_tensor("gt2", [1, 1], F32))
            gt2b = ea(nc.sbuf_tensor("gt2b", [1, 1], F32))
            grc = ea(nc.sbuf_tensor("grc", [1, 2], F32))
            ps_t = ea(nc.psum_tensor("ps10t", [128, 512], F32))
            for s in ("pa_ldA", "pa_ldB", "pa_rd", "pa_tr", "pa_trb", "pa_sum",
                      "pa_bc", "pa_done", "pa_fin", "pa_v1", "pa_v2", "pa_v3", "pa_mg"):
                ctx.sem(s)
            pa_ld = [ctx.sems["pa_ldA"], ctx.sems["pa_ldB"]]
            mats = [("l1_w1", NHC1, D), ("l1_w2", NHC1, D), ("l1_w3", NCH, HFF1)]

            with nc.Block("l1_gamma") as blk:
                @blk.sync
                def _(sy):
                    sy.dma_start(out=magic_sb[:, :],
                                 in_=w["magic_f"][:, :]).then_inc(
                        ctx.sem("pa_mg"), 16)
                    sy.dma_start(out=idf[:, :], in_=w["ident_f"][:, :]
                                 ).then_inc(ctx.sem("pa_mg"), 16)
                    g = 0
                    for mi, (mn, nb, kd) in enumerate(mats):
                        for b in range(nb):
                            if g >= 2:
                                sy.wait_ge(ctx.sems["pa_rd"], g - 1)
                            sy.dma_start(out=gbuf[g % 2][:, 0:kd],
                                         in_=_wblock(w[mn], b)
                                         ).then_inc(pa_ld[g % 2], 16)
                            g += 1

                @blk.tensor
                def _(te):
                    te.wait_ge(ctx.sems["pa_mg"], 32)
                    for mi in range(3):
                        te.wait_ge(ctx.sems["pa_done"], mi + 1)
                        te.transpose(ps_t[0:1, 0:128], acc[:, 0:1], idf[:, :]
                                     ).then_inc(ctx.sems["pa_tr"], 1)
                        te.wait_ge(ctx.sems["pa_sum"], mi + 1)
                        mm = te.matmul(ps_t[:, 128:129],
                                       ctx.sb["ones"].ap()[0:1, 0:128].bitcast(F32),
                                       grc[0:1, 0:1],
                                       start=True, stop=True)
                        mm.then_inc(ctx.sems["pa_trb"], 1)

                @blk.vector
                def _(ve):
                    nk = {0: HFF1 * D, 1: HFF1 * D, 2: D * HFF1}
                    ch = ctx.sem("pa_ch")
                    n = 0
                    g = 0
                    for mi, (mn, nb, kd) in enumerate(mats):
                        for b in range(nb):
                            ve.wait_ge(pa_ld[g % 2], 16 * (g // 2 + 1))
                            ve.tensor_reduce(tmp_r[:, :], gbuf[g % 2][:, 0:kd],
                                             AX.X, ALU.add,
                                             apply_absolute_value=True
                                             ).then_inc(ch, 1)
                            n += 1
                            ve.wait_ge(ch, n)
                            if b == 0:
                                ve.tensor_scalar_mul(acc[:, :], tmp_r[:, :], 1.0
                                                     ).then_inc(ch, 1)
                            else:
                                ve.tensor_add(acc[:, :], acc[:, :], tmp_r[:, :]
                                              ).then_inc(ch, 1)
                            n += 1
                            ve.wait_ge(ch, n)
                            ve.sem_inc(ctx.sems["pa_rd"], 1)
                            g += 1
                        ve.sem_inc(ctx.sems["pa_done"], 1)
                        ve.wait_ge(ctx.sems["pa_tr"], mi + 1)
                        ve.reduce_sum(gsum[0:1, 0:1], ps_t[0:1, 0:128], axis=AX.X
                                      ).then_inc(ctx.sem("pa_v1"), 1)
                        ve.wait_ge(ctx.sems["pa_v1"], mi + 1)
                        ve.tensor_scalar(gt2[0:1, 0:1], gsum[0:1, 0:1],
                                         1.0 / nk[mi], EPS_BL, ALU.mult, ALU.add
                                         ).then_inc(ctx.sem("pa_v2"), 1)
                        ve.wait_ge(ctx.sems["pa_v2"], mi + 1)
                        ve.reciprocal(gt2b[0:1, 0:1], gt2[0:1, 0:1]).then_inc(
                            ctx.sem("pa_v3"), 1)
                        ve.wait_ge(ctx.sems["pa_v3"], mi + 1)
                        ve.tensor_scalar_mul(grc[0:1, 0:1], gt2b[0:1, 0:1],
                                             1.0).then_inc(ctx.sems["pa_sum"], 1)
                        ve.wait_ge(ctx.sems["pa_trb"], mi + 1)
                        ve.tensor_scalar_mul(rgb[mi][:, :], ps_t[:, 128:129], 1.0
                                             ).then_inc(ctx.sems["pa_bc"], 1)

        with ExitStack() as pb:
            eb = pb.enter_context
            h1 = [eb(nc.sbuf_tensor(f"h1_{c}", [128, T], BF16))
                  for c in range(NHC1)]
            ph_b = pb.enter_context(ExitStack())
            e4 = ph_b.enter_context
            xh = [e4(nc.sbuf_tensor(f"x10{c}", [128, 512], F32R))
                  for c in range(NCH)]
            wr = [e4(nc.sbuf_tensor(f"wr10{i}", [128, D], F32)) for i in range(2)]
            tq = [e4(nc.sbuf_tensor(f"tq10{i}", [128, D], F32)) for i in range(2)]
            wq = [e4(nc.sbuf_tensor(f"wq10{i}", [128, D], F32R)) for i in range(4)]
            sg = [e4(nc.sbuf_tensor(f"sg10{i}", [128, 512], F32)) for i in range(2)]
            ps_x1 = [e4(nc.psum_tensor(f"px1{i}", [128, 512], F32))
                     for i in range(2)]
            ps_x2 = [e4(nc.psum_tensor(f"px2{i}", [128, 512], F32))
                     for i in range(2)]
            for s in ("pb_pre", "pb_wA", "pb_wB", "pb_t", "pb_q", "pb_mm",
                      "pb_sg", "pb_ev"):
                ctx.sem(s)
            pb_w = [ctx.sems["pb_wA"], ctx.sems["pb_wB"]]
            gst0 = ctx.gst

            with nc.Block("l1_up") as blk:
                @blk.sync
                def _(sy):
                    for hf in range(2):
                        if hf == 0 and gst0:
                            sy.wait_ge(ctx.sems["gst"], gst0)
                        if hf == 1:
                            sy.wait_ge(ctx.sems["pb_mm"], NHC1)
                        for c in range(NCH):
                            sy.dma_start(
                                out=xh[c][:, :],
                                in_=src.ap()[c * 128:(c + 1) * 128,
                                             hf * 512:(hf + 1) * 512],
                            ).then_inc(ctx.sems["pb_pre"], 16)
                        for hc in range(NHC1):
                            for m in range(2):
                                g = (hf * NHC1 + hc) * 2 + m
                                if g >= 2:
                                    sy.wait_ge(ctx.sems["pb_t"], g - 1)
                                sy.dma_start(
                                    out=wr[g % 2][:, :],
                                    in_=_wblock(w["l1_w1" if m == 0 else "l1_w2"],
                                                hc)).then_inc(pb_w[g % 2], 16)

                @blk.scalar
                def _(sc):
                    def magic(g):
                        m = g % 2
                        sc.wait_ge(pb_w[g % 2], 16 * (g // 2 + 1))
                        if g >= 2:
                            sc.wait_ge(ctx.sems["pb_q"], g - 1)
                        sc.activation(tq[g % 2][:, :], wr[g % 2][:, :],
                                      AF.Identity, bias=magic_sb[:, 0:1],
                                      scale=rgb[m][:, 0:1]).then_inc(
                            ctx.sems["pb_t"], 1)

                    def sig(u):
                        sc.wait_ge(ctx.sems["pb_mm"], u + 1)
                        if u >= 2:
                            sc.wait_ge(ctx.sems["pb_ev"], u - 1)
                        sc.activation(sg[u % 2][:, :], ps_x1[u % 2][:, :],
                                      AF.Sigmoid).then_inc(ctx.sems["pb_sg"], 1)

                    for u in range(2 * NHC1):
                        magic(2 * u)
                        magic(2 * u + 1)
                        if u >= 1:
                            sig(u - 1)
                    sig(2 * NHC1 - 1)

                @blk.vector
                def _(ve):
                    def quant(g):
                        ve.wait_ge(ctx.sems["pb_t"], g + 1)
                        ve.tensor_scalar(tq[g % 2][:, :], tq[g % 2][:, :],
                                         RND_MAGIC, -1.0, ALU.subtract, ALU.max)
                        if g >= 4:
                            ve.wait_ge(ctx.sems["pb_mm"], (g - 4) // 2 + 1)
                        ve.tensor_scalar_min(wq[g % 4][:, :], tq[g % 2][:, :],
                                             1.0).then_inc(ctx.sems["pb_q"], 1)

                    def hmul(u):
                        hf, hc = divmod(u, NHC1)
                        ve.wait_ge(ctx.sems["pb_sg"], u + 1)
                        ve.tensor_mul(
                            h1[hc][:, hf * 512:(hf + 1) * 512],
                            sg[u % 2][:, :], ps_x2[u % 2][:, :],
                        ).then_inc(ctx.sems["pb_ev"], 1)

                    for u in range(2 * NHC1):
                        quant(2 * u)
                        quant(2 * u + 1)
                        if u >= 1:
                            hmul(u - 1)
                    hmul(2 * NHC1 - 1)

                @blk.tensor
                def _(te):
                    for hf in range(2):
                        for hc in range(NHC1):
                            u = hf * NHC1 + hc
                            if hc == 0:
                                te.wait_ge(ctx.sems["pb_pre"], 16 * 16 * (hf + 1))
                            te.wait_ge(ctx.sems["pb_q"], 2 * u + 2)
                            if u >= 2:
                                te.wait_ge(ctx.sems["pb_ev"], u - 1)
                                te.wait_ge(ctx.sems["pb_sg"], u - 1)
                            for k in range(NCH):
                                mm = te.matmul(
                                    ps_x1[u % 2][:, :],
                                    wq[(2 * u) % 4][:, k * 128:(k + 1) * 128],
                                    xh[k][:, :],
                                    start=(k == 0), stop=(k == NCH - 1))
                            for k in range(NCH):
                                mm = te.matmul(
                                    ps_x2[u % 2][:, :],
                                    wq[(2 * u + 1) % 4][:, k * 128:(k + 1) * 128],
                                    xh[k][:, :],
                                    start=(k == 0), stop=(k == NCH - 1))
                            mm.then_inc(ctx.sems["pb_mm"], 1)

            ph_b.close()

            with ExitStack() as pc:
                ec = pc.enter_context
                PW = 11 * 128
                w3r = [ec(nc.sbuf_tensor(f"w3r{i}", [128, PW], F32))
                       for i in range(2)]
                t3q = [ec(nc.sbuf_tensor(f"t3q{i}", [128, PW], F32))
                       for i in range(2)]
                w3q = [ec(nc.sbuf_tensor(f"w3q{i}", [128, PW], BF16))
                       for i in range(2)]
                xres = [ec(nc.sbuf_tensor(f"xr10{i}", [128, T], F32R))
                        for i in range(2)]
                ybuf = [ec(nc.sbuf_tensor(f"yb10{i}", [128, T], F32R))
                        for i in range(2)]
                ps_y = [ec(nc.psum_tensor(f"py10{i}", [128, T], F32))
                        for i in range(2)]
                ps_stat = ec(nc.psum_tensor("ps10s", [128, T], F32))
                for s in ("pc_wA", "pc_wB", "pc_xA", "pc_xB", "pc_t", "pc_q",
                          "pc_mmq", "pc_sj", "pc_res", "pc_stA", "pc_stB",
                          "pc_sums"):
                    ctx.sem(s)
                pc_w = [ctx.sems["pc_wA"], ctx.sems["pc_wB"]]
                pc_x = [ctx.sems["pc_xA"], ctx.sems["pc_xB"]]
                pc_st = [ctx.sems["pc_stA"], ctx.sems["pc_stB"]]

                with nc.Block("l1_down") as blk:
                    @blk.sync
                    def _(sy):
                        for j in range(NCH):
                            if j >= 2:
                                sy.wait_ge(ctx.sems["pc_res"], j - 1)
                            sy.dma_start(out=xres[j % 2][:, :],
                                         in_=src.ap()[j * 128:(j + 1) * 128, :]
                                         ).then_inc(pc_x[j % 2], 16)
                            for q in range(4):
                                gq = j * 4 + q
                                if gq >= 2:
                                    sy.wait_ge(ctx.sems["pc_t"], gq - 1)
                                sy.dma_start(
                                    out=w3r[gq % 2][:, :],
                                    in_=_wblock(w["l1_w3"], j, q * PW, PW),
                                ).then_inc(pc_w[gq % 2], 16)
                            sy.wait_ge(ctx.sems["pc_res"], j + 1)
                            sy.dma_start(out=spill.ap()[j * 128:(j + 1) * 128, :],
                                         in_=ybuf[j % 2][:, :]).then_inc(
                                pc_st[j % 2], 16)

                    @blk.scalar
                    def _(sc):
                        for gq in range(64):
                            sc.wait_ge(pc_w[gq % 2], 16 * (gq // 2 + 1))
                            if gq >= 2:
                                sc.wait_ge(ctx.sems["pc_q"], gq - 1)
                            sc.activation(t3q[gq % 2][:, :], w3r[gq % 2][:, :],
                                          AF.Identity, bias=magic_sb[:, 0:1],
                                          scale=rgb[2][:, 0:1]).then_inc(
                                ctx.sems["pc_t"], 1)

                    @blk.vector
                    def _(ve):
                        for j in range(NCH):
                            for q in range(4):
                                gq = j * 4 + q
                                ve.wait_ge(ctx.sems["pc_t"], gq + 1)
                                ve.tensor_scalar(t3q[gq % 2][:, :],
                                                 t3q[gq % 2][:, :],
                                                 RND_MAGIC, -1.0,
                                                 ALU.subtract, ALU.max)
                                if gq >= 2:
                                    ve.wait_ge(ctx.sems["pc_mmq"], gq - 1)
                                ve.tensor_scalar_min(w3q[gq % 2][:, :],
                                                     t3q[gq % 2][:, :], 1.0
                                                     ).then_inc(ctx.sems["pc_q"], 1)
                            ve.wait_ge(ctx.sems["pc_mmq"], 4 * (j + 1))
                            if j >= 2:
                                ve.wait_ge(pc_st[j % 2], 16 * ((j - 2) // 2 + 1))
                                ve.wait_ge(ctx.sems["pc_sums"], j - 1)
                            ve.wait_ge(pc_x[j % 2], 16 * (j // 2 + 1))
                            ve.scalar_tensor_tensor(
                                ybuf[j % 2][:, :], ps_y[j % 2][:, :], 1.0,
                                xres[j % 2][:, :], ALU.mult, ALU.add,
                            ).then_inc(ctx.sems["pc_res"], 1)
                            ve.wait_ge(ctx.sems["pc_sj"], j + 1)
                            if j == 0:
                                ve.tensor_scalar_mul(stat_acc[:, :],
                                                     ps_stat[:, :], 1.0
                                                     ).then_inc(
                                    ctx.sems["pc_sums"], 1)
                            else:
                                ve.tensor_add(stat_acc[:, :], stat_acc[:, :],
                                              ps_stat[:, :]).then_inc(
                                    ctx.sems["pc_sums"], 1)

                    @blk.tensor
                    def _(te):
                        for j in range(NCH):
                            for q in range(4):
                                gq = j * 4 + q
                                te.wait_ge(ctx.sems["pc_q"], gq + 1)
                                if j >= 2 and q == 0:
                                    te.wait_ge(ctx.sems["pc_res"], j - 1)
                                for f in range(11):
                                    hc = q * 11 + f
                                    for half in range(2):
                                        mm = te.matmul(
                                            ps_y[j % 2][:, half * 512:
                                                        (half + 1) * 512],
                                            w3q[gq % 2][:, f * 128:(f + 1) * 128],
                                            h1[hc][:, half * 512:(half + 1) * 512],
                                            start=(hc == 0), stop=(hc == NHC1 - 1))
                                mm.then_inc(ctx.sems["pc_mmq"], 1)
                            te.wait_ge(ctx.sems["pc_res"], j + 1)
                            if j >= 1:
                                te.wait_ge(ctx.sems["pc_sums"], j)
                            for half in range(2):
                                mm = te.matmul(
                                    ps_stat[:, half * 512:(half + 1) * 512],
                                    ctx.sb["ones"][:, :],
                                    ybuf[j % 2][:, half * 512:(half + 1) * 512],
                                    start=True, stop=True)
                            mm.then_inc(ctx.sems["pc_sj"], 1)

        with ExitStack() as pd:
            ed = pd.enter_context
            stream = [ed(nc.sbuf_tensor(f"s10_{c}", [128, T], F32R))
                      for c in range(NCH)]
            sq_buf = [ed(nc.sbuf_tensor(f"sq10{i}", [128, T], F32R))
                      for i in range(2)]
            rstd_sb = ed(nc.sbuf_tensor("rstd10", [128, T], F32))
            rstd2_sb = ed(nc.sbuf_tensor("rstd10b", [128, T], F32))
            ps_var = ed(nc.psum_tensor("ps10v", [128, T], F32))
            for s in ("pd_lA", "pd_lB"):
                ctx.sem(s)
            pd_l = [ctx.sems["pd_lA"], ctx.sems["pd_lB"]]
            ln = LN(ctx, "ln4", stream, ctx.sb["ones"], sq_buf, rstd_sb,
                    rstd2_sb, stat_acc, ps_var,
                    ("pd_lA", lambda j: 16 * (j // 2 + 1)), dst, sum_done=True,
                    ve_rdy=(pd_l, lambda j: 16 * (j // 2 + 1)))

            with nc.Block("l1_ln4") as blk:
                @blk.sync
                def _(sy):
                    sy.wait_ge(ctx.sems["pc_stA"], 16 * 8)
                    sy.wait_ge(ctx.sems["pc_stB"], 16 * 8)
                    for j in range(NCH):
                        sy.dma_start(out=stream[j][:, :],
                                     in_=spill.ap()[j * 128:(j + 1) * 128, :]
                                     ).then_inc(pd_l[j % 2], 16)
                    ln.store(sy)

                @blk.tensor
                def _(te):
                    ln.te(te)

                @blk.vector
                def _(ve):
                    ln.ve(ve)

                @blk.scalar
                def _(sc):
                    ln.sc(sc)



def _exchange_flag(gp, ctx, slot, magic):
    flagv = ctx.sb["flagv_sb"]
    sh_flag = ctx.dram["shr"]["flag"]
    n_prev = getattr(ctx, "xfl_count", 0)
    if n_prev:
        gp.wait_ge(ctx.sem("xfl"), 16 * n_prev)
    gp.memset(flagv[0:1, 0:1], magic).then_inc(ctx.sem("xchm"), 1)
    gp.wait_ge(ctx.sems["xchm"], n_prev + 1)
    gp.dma_start(out=sh_flag.ap()[0:1, slot:slot + 1],
                 in_=flagv[0:1, 0:1]).then_inc(ctx.sem("xfl"), 16)


def _exchange_poll(gp, ctx, magic):
    ctx.xfl_count = getattr(ctx, "xfl_count", 0) + 1
    gp.wait_ge(ctx.sem("xfl"), 16 * ctx.xfl_count)
    flag_rd = ctx.sb["flag_rd"]
    sh_flag = ctx.dram["shr"]["flag"]
    regs = ctx.regs
    with gp.register(f"pcond{ctx.xfl_count}") as pcond:
        gp.reg_mov(pcond, 1)
        with gp.While(pcond):
            gp.reg_add(regs["poll_tgt"], regs["poll_tgt"], 16)
            gp.dma_start(out=flag_rd[0:1, 0:2],
                         in_=sh_flag.ap()[0:1, 0:2]).then_inc(ctx.sem("xpl"), 16)
            gp.wait_ge(ctx.sems["xpl"], regs["poll_tgt"])
            gp.reg_load(regs["pc"], flag_rd[0:1, 0:1])
            gp.reg_load(regs["pc1"], flag_rd[0:1, 1:2])
            gp.reg_alu(regs["pc"], regs["pc"], magic, ALU.not_equal)
            gp.reg_alu(regs["pc1"], regs["pc1"], magic, ALU.not_equal)
            gp.reg_alu(pcond, regs["pc"], regs["pc1"], ALU.logical_or)



def build(groups):
    nc = bass.Bass("TRN2", target_bir_lowering=False)
    es = ExitStack()
    ctx = Ctx(nc, es)
    first, last = groups[0], groups[-1]
    assert groups == list(range(first, last + 1))

    if first == 0:
        src0 = nc.dram_tensor("x_t", [D, T], F32R, kind="ExternalInput")
    else:
        src0 = nc.dram_tensor("stream_in", [D, T], F32R, kind="ExternalInput")
    if last == 3:
        dst_final = nc.dram_tensor("y_out", [D, T], F32R, kind="ExternalOutput")
    else:
        dst_final = nc.dram_tensor("stream_out", [D, T], F32R,
                                   kind="ExternalOutput")

    w = {}

    def wdecl(name, shape, dt=F32R):
        w[name] = nc.dram_tensor(name, shape, dt, kind="ExternalInput")

    wdecl("meta", [1, 8], I32)
    wdecl("ones_f", [128, 128])
    if 0 in groups:
        for n in ("l0_wq", "l0_wk", "l0_wv", "l0_wo"):
            wdecl(n, [D, D])
        for n in ("l0_bq", "l0_bk", "l0_bv", "l0_bo"):
            wdecl(n, [128, 16], F32)
    if 1 in groups:
        wdecl("l0_w1", [HFF0, D])
        wdecl("l0_w3", [HFF0, D])
        wdecl("l0_w2", [D, HFF0])
        wdecl("w128_bf", [128, 128], BF16)
    if 2 in groups:
        wdecl("l1_wq", [D, D])
        wdecl("l1_wk", [512, D])
        wdecl("l1_wv_r", [D, 512])
        wdecl("l1_wo", [D, D])
        wdecl("rsw", [128, 128])
        wdecl("cos_rep", [128, T], F32)
        wdecl("sin_rep", [128, T], F32)
        wdecl("diag_mask", [128, 128], F32)
    if 3 in groups:
        wdecl("l1_w1", [HFF1, D], F32)
        wdecl("l1_w2", [HFF1, D], F32)
        wdecl("l1_w3", [D, HFF1], F32)
        wdecl("ident_f", [128, 128], F32)
        wdecl("magic_f", [128, 1], F32)
    if 0 in groups or 2 in groups:
        wdecl("ident_bf", [128, 128], BF16)

    shr = {}
    if 0 in groups:
        shr["k0"] = nc.dram_tensor("sh_k0", [2 * D, T], BF16, addr_space="Shared")
        shr["v0"] = nc.dram_tensor("sh_v0", [2 * 128, NCH * T], BF16, addr_space="Shared")
    if 2 in groups:
        shr["k1"] = nc.dram_tensor("sh_k1", [2 * 512, T], BF16, addr_space="Shared")
        shr["v1"] = nc.dram_tensor("sh_v1", [2 * 128, 8 * 512], BF16, addr_space="Shared")
    if shr:
        shr["flag"] = nc.dram_tensor("sh_flag", [1, 16], I32, addr_space="Shared")

    spill = nc.dram_tensor("spill", [D, T], F32R)
    handoff = {}
    for g in groups[:-1]:
        handoff[g] = nc.dram_tensor(f"handoff{g}", [D, T], F32R)

    ctx.dram = dict(w=w, shr=shr, spill=spill)

    with es:
        for rn in ("rank", "poll_tgt", "pc", "pc1"):
            ctx.regs[rn] = nc.gpsimd.alloc_register(rn)
        ctx.regs["rank_te"] = nc.tensor.alloc_register("rank_te")
        ctx.regs["rank_ve"] = nc.vector.alloc_register("rank_ve")
        ctx.regs["rank_sc"] = nc.scalar.alloc_register("rank_sc")
        ctx.sb["meta_sb"] = es.enter_context(nc.sbuf_tensor("meta_sb", [1, 8], I32))
        ctx.sb["flagv_sb"] = es.enter_context(
            nc.sbuf_tensor("flagv_sb", [1, 4], I32))
        ctx.sb["flag_rd"] = es.enter_context(
            nc.sbuf_tensor("flag_rd", [1, 16], I32))
        ctx.sb["ones"] = es.enter_context(nc.sbuf_tensor("ones_g", [128, 128], F32R))
        ctx.sb["eps"] = es.enter_context(nc.sbuf_tensor("eps_g", [128, 1], F32))

        with nc.Block("init") as blk:
            @blk.gpsimd
            def _(gp):
                gp.dma_start(out=ctx.sb["meta_sb"][:, :],
                             in_=w["meta"][:, :]).then_inc(ctx.sem("mld"), 16)
                gp.dma_start(out=ctx.sb["ones"][:, :],
                             in_=w["ones_f"][:, :]).then_inc(ctx.sem("mld"), 16)
                gp.wait_ge(ctx.sems["mld"], 32)
                gp.reg_load(ctx.regs["rank"], ctx.sb["meta_sb"][0:1, 0:1])
                gp.reg_mov(ctx.regs["poll_tgt"], 0)
                gp.memset(ctx.sb["eps"][:, :], EPS_LN)

            @blk.tensor
            def _(te):
                te.wait_ge(ctx.sems["mld"], 32)
                te.reg_load(ctx.regs["rank_te"], ctx.sb["meta_sb"][0:1, 0:1])

            @blk.vector
            def _(ve):
                ve.wait_ge(ctx.sems["mld"], 32)
                ve.reg_load(ctx.regs["rank_ve"], ctx.sb["meta_sb"][0:1, 0:1])

            @blk.scalar
            def _(sc):
                sc.wait_ge(ctx.sems["mld"], 32)
                sc.reg_load(ctx.regs["rank_sc"], ctx.sb["meta_sb"][0:1, 0:1])

        cur = src0
        for g in groups:
            dstg = dst_final if g == last else handoff[g]
            ctx.begin_group()
            if g == 0:
                _group_l0_attn(ctx, cur, dstg)
            elif g == 1:
                _group_l0_ffn(ctx, cur, dstg)
            elif g == 2:
                _group_l1_attn(ctx, cur, dstg)
            elif g == 3:
                _group_l1_ffn(ctx, cur, dstg)
            ctx.end_group()
            cur = dstg

        with nc.Block("fin") as blk:
            @blk.sync
            def _(sy):
                sy.wait_ge(ctx.sem("gst"), ctx.gst)

    return nc


def kernel(**inputs):
    in_maps = stage_inputs(inputs)
    cur = [dict(m) for m in in_maps]
    res = None
    for grp in LAUNCH_GROUPS:
        nc = build(grp)
        res = run_bass_kernel_spmd(nc, cur, core_ids=list(range(8)))
        if grp[-1] != 3:
            for c in range(8):
                cur[c]["stream_in"] = np.asarray(res.results[c]["stream_out"])
    return unstage_output(res.results)


# version 04
# speedup vs baseline: 1.0986x; 1.0700x over previous
import math
import numpy as np
from contextlib import ExitStack

import concourse.bass as bass
import concourse.mybir as mybir
from concourse.bass_utils import run_bass_kernel_spmd
from concourse.expressions_rust import RuntimeValue

F32 = mybir.dt.float32
F32R = mybir.dt.float32r
BF16 = mybir.dt.bfloat16
I32 = mybir.dt.int32
AF = mybir.ActivationFunctionType
ALU = mybir.AluOpType
AX = mybir.AxisListType

D = 2048
T = 1024
NCH = D // 128
HFF0 = 8192
HFF1 = 5632
NH0 = 16
NH1 = 32
EPS_LN = 1e-5
EPS_BL = 1e-8
MAGIC_BASE = 0x5A5A0000
RND_MAGIC = 12582912.0

QBLOCKS = [[0, 1, 2, 3, 12, 13, 14, 15], [4, 5, 6, 7, 8, 9, 10, 11]]
GATHER_BLOCKS = QBLOCKS[0] + QBLOCKS[1]

LAUNCH_GROUPS = [[0, 1, 2, 3]]
STOP_AFTER = None
G2_SKIP = set()
LN_UPTO = None
LN_LVL = {None: 9, 't1': 1, 'sq': 2, 'var': 3, 'vb': 35, 'varnop': 36, 'ssq': 4, 'rs': 5}


def _positions(rank):
    out = []
    for b in QBLOCKS[rank]:
        out.extend(range(b * 128, (b + 1) * 128))
    return np.array(out, dtype=np.int64)


def _l1_kept(rank, qt):
    pq = QBLOCKS[rank][qt]
    out = []
    for kt in range(16):
        pk = GATHER_BLOCKS[kt]
        if pk < pq:
            out.append((kt, "full"))
        elif pk == pq:
            out.append((kt, "diag"))
    return out



def _hadamard128():
    h = np.array([[1.0]], dtype=np.float32)
    while h.shape[0] < 128:
        h = np.block([[h, h], [h, -h]])
    return h


def _rsw():
    m = np.arange(128)
    s = np.where((m % 64) < 32, -1.0, 1.0)
    r = np.zeros((128, 128), dtype=np.float32)
    r[m ^ 32, m] = s
    return r


def _diag_mask():
    q = np.arange(128)[:, None]
    k = np.arange(128)[None, :]
    return np.where(k <= q, 0.0, -1e9).astype(np.float32)


def _tile_lhsT(L):
    L = np.ascontiguousarray(L)
    Kd, Jd = L.shape
    t = L.reshape(Kd // 128, 128, Jd // 128, 128).transpose(2, 1, 0, 3)
    return np.ascontiguousarray(t.reshape(Jd, Kd))


def stage_inputs(inp):
    import ml_dtypes
    asc = np.ascontiguousarray
    x = np.asarray(inp["L0_attn_hidden_state"])
    cos = np.asarray(inp["L1_attn_cos"])
    sin = np.asarray(inp["L1_attn_sin"])

    sw = {
        "l0_wq": _tile_lhsT(np.asarray(inp["L0_attn_Wq"]).T),
        "l0_wk": _tile_lhsT(np.asarray(inp["L0_attn_Wk"]).T),
        "l0_wv": _tile_lhsT(np.asarray(inp["L0_attn_Wv"]).T),
        "l0_wo": _tile_lhsT(np.asarray(inp["L0_attn_Wo"]).T),
        "l0_bq": asc(np.asarray(inp["L0_attn_bq"]).reshape(16, 128).T),
        "l0_bk": asc(np.asarray(inp["L0_attn_bk"]).reshape(16, 128).T),
        "l0_bv": asc(np.asarray(inp["L0_attn_bv"]).reshape(16, 128).T),
        "l0_bo": asc(np.asarray(inp["L0_attn_bo"]).reshape(16, 128).T),
        "l0_w1": _tile_lhsT(np.asarray(inp["L0_ffn_W1"]).T),
        "l0_w3": _tile_lhsT(np.asarray(inp["L0_ffn_W3"]).T),
        "l0_w2": _tile_lhsT(np.asarray(inp["L0_ffn_W2"]).T),
        "l1_wq": _tile_lhsT(np.asarray(inp["L1_attn_Wq"])),
        "l1_wk": _tile_lhsT(np.asarray(inp["L1_attn_Wk"])),
        "l1_wv_r": asc(np.asarray(inp["L1_attn_Wv"], dtype=np.float32)),
        "l1_wo": _tile_lhsT(np.asarray(inp["L1_attn_Wo"])),
        "l1_w1": _tile_lhsT(np.asarray(inp["L1_ffn_w1"]).T),
        "l1_w2": _tile_lhsT(np.asarray(inp["L1_ffn_w2"]).T),
        "l1_w3": _tile_lhsT(np.asarray(inp["L1_ffn_w3"]).T),
        "ident_bf": np.eye(128, dtype=ml_dtypes.bfloat16),
        "ident_f": np.eye(128, dtype=np.float32),
        "magic_f": np.full((128, 1), 12582912.0, dtype=np.float32),
        "ones_f": np.ones((128, 128), dtype=np.float32),
        "w128_bf": _hadamard128().astype(ml_dtypes.bfloat16),
        "rsw": _rsw(),
        "diag_mask": _diag_mask(),
    }

    col32 = np.tile(np.arange(32), 4)
    in_maps = []
    for c in range(8):
        b, r = divmod(c, 2)
        pos = _positions(r)
        m = dict(sw)
        m["x_t"] = asc(x[b][pos].T)
        m["cos_rep"] = asc(cos[pos][:, col32].T)
        m["sin_rep"] = asc(sin[pos][:, col32].T)
        m["meta"] = np.array([[r, r ^ 1, r * 2048, r * 1024,
                               r * 512, r * 1024, r, 0]], dtype=np.int32)
        in_maps.append(m)
    return in_maps


def unstage_output(results):
    out = np.empty((4, 2048, 2048), dtype=np.float32)
    for c in range(8):
        b, r = divmod(c, 2)
        out[b][_positions(r)] = np.asarray(results[c]["y_out"]).T
    return out



GLOBAL_SEMS = {"gst", "mld", "xfl", "xpl", "xchm"}


class Ctx:
    def __init__(self, nc, es):
        self.nc = nc
        self.es = es
        self.sb = {}
        self.dram = {}
        self.sems = {}
        self.regs = {}
        self.magic_seq = 0
        self.gst = 0
        self.scoped_es = None
        self.group_sems = []

    def sem(self, name):
        if name not in self.sems:
            if self.scoped_es is not None and name not in GLOBAL_SEMS:
                self.sems[name] = self.scoped_es.enter_context(
                    self.nc.semaphore(name))
                self.group_sems.append(name)
            else:
                self.sems[name] = self.es.enter_context(self.nc.semaphore(name))
        return self.sems[name]

    def begin_group(self):
        self.scoped_es = ExitStack()
        self.group_sems = []

    def end_group(self):
        nc = self.nc
        self._clr = getattr(self, "_clr", 0) + 1
        sems = [self.sems[n] for n in self.group_sems]
        if sems:
            with nc.Block(f"semclr{self._clr}") as blk:
                @blk.sync
                def _(sy):
                    for s in sems:
                        sy.sem_clear(s)
        for n in self.group_sems:
            del self.sems[n]
        self.scoped_es.close()
        self.scoped_es = None
        self.group_sems = []


def _wblock(wdram, jb, k0=0, kn=None):
    kn = kn if kn is not None else wdram.shape[1] - k0
    return wdram.ap()[jb * 128:(jb + 1) * 128, k0:k0 + kn]



class LN:

    def __init__(self, ctx, tag, stream, ones_sb, sq_buf, rstd_sb, rstd2_sb,
                 ps_sum, ps_var, rdy, dst, sum_done=False, ve_rdy=None):
        self.ctx, self.tag = ctx, tag
        self.stream, self.ones, self.sq, self.rstd = stream, ones_sb, sq_buf, rstd_sb
        self.rstd2 = rstd2_sb
        self.ps_sum, self.ps_var = ps_sum, ps_var
        self.rdy_sem, self.rdy_val = rdy
        self.ve_rdy = ve_rdy
        self.dst = dst
        self.sum_done = sum_done
        for s in ("sum", "t1", "sq", "varj", "var", "ssq", "nm", "rs", "vb", "vc"):
            ctx.sem(f"{tag}_{s}")

    def te(self, te):
        c, tag = self.ctx, self.tag
        lvl = LN_LVL[LN_UPTO]
        if lvl == 36:
            lvl = 3
        if not self.sum_done:
            for j in range(NCH):
                te.wait_ge(c.sems[self.rdy_sem], self.rdy_val(j))
                for half in range(2):
                    mm = te.matmul(
                        self.ps_sum[:, half * 512:(half + 1) * 512],
                        self.ones[:, :],
                        self.stream[j][:, half * 512:(half + 1) * 512],
                        start=(j == 0), stop=(j == NCH - 1))
                if j == NCH - 1:
                    mm.then_inc(c.sems[f"{tag}_sum"], 1)
        if lvl < 3:
            return
        for j in range(NCH):
            te.wait_ge(c.sems[f"{tag}_sq"], j + 1)
            for half in range(2):
                mm = te.matmul(
                    self.ps_var[:, half * 512:(half + 1) * 512],
                    self.ones[:, :],
                    self.sq[j % 2][:, half * 512:(half + 1) * 512],
                    start=(j == 0), stop=(j == NCH - 1))
            mm.then_inc(c.sems[f"{tag}_varj"], 1)

    def ve(self, ve):
        c, tag = self.ctx, self.tag
        lvl = LN_LVL[LN_UPTO]
        for j in range(NCH):
            if j == 0 and not self.sum_done:
                ve.wait_ge(c.sems[f"{tag}_sum"], 1)
            if self.ve_rdy is not None:
                sems, vfn = self.ve_rdy
                ve.wait_ge(sems[j % 2], vfn(j))
            ve.scalar_tensor_tensor(
                self.stream[j][:, :], self.ps_sum[:, :], -1.0 / D,
                self.stream[j][:, :], ALU.mult, ALU.add,
            ).then_inc(c.sems[f"{tag}_t1"], 1)
        if lvl < 9:
            if lvl >= 2:
                ve.wait_ge(c.sems[f"{tag}_sq"], NCH)
            if lvl >= 4 or lvl == 35:
                ve.tensor_copy(self.rstd2[:, :], self.rstd[:, :]).then_inc(
                    c.sems[f"{tag}_vb"], 1)
            if lvl == 36:
                ve.engine_nop()
            if lvl >= 4:
                ve.wait_ge(c.sems[f"{tag}_ssq"], 1)
            if lvl >= 5:
                ve.reciprocal(self.rstd[:, :], self.rstd2[:, :]).then_inc(
                    c.sems[f"{tag}_rs"], 1)
                ve.wait_ge(c.sems[f"{tag}_rs"], 1)
            for j in range(NCH):
                ve.sem_inc(c.sems[f"{tag}_nm"], 1)
            return
        ve.wait_ge(c.sems[f"{tag}_varj"], NCH)
        eps_b = c.sb["eps"][:, 0:1].broadcast_to([128, T])
        ve.scalar_tensor_tensor(
            self.rstd[:, :], self.ps_var[:, :], 1.0 / D, eps_b,
            ALU.mult, ALU.add).then_inc(c.sems[f"{tag}_vb"], 1)
        ve.wait_ge(c.sems[f"{tag}_ssq"], 1)
        ve.reciprocal(self.rstd[:, :], self.rstd2[:, :]).then_inc(
            c.sems[f"{tag}_rs"], 1)
        ve.wait_ge(c.sems[f"{tag}_rs"], 1)
        for j in range(NCH):
            ve.tensor_mul(self.stream[j][:, :], self.stream[j][:, :],
                          self.rstd[:, :]).then_inc(c.sems[f"{tag}_nm"], 1)

    def sc(self, sc):
        c, tag = self.ctx, self.tag
        lvl = LN_LVL[LN_UPTO]
        if lvl in (35, 36):
            lvl = 3
        if lvl < 2:
            return
        for j in range(NCH):
            sc.wait_ge(c.sems[f"{tag}_t1"], j + 1)
            if j >= 2 and lvl >= 3:
                sc.wait_ge(c.sems[f"{tag}_varj"], j - 1)
            sc.activation(self.sq[j % 2][:, :], self.stream[j][:, :],
                          AF.Square).then_inc(c.sems[f"{tag}_sq"], 1)
        if lvl < 4:
            return
        sc.wait_ge(c.sems[f"{tag}_vb"], 1)
        sc.sqrt(self.rstd2[:, :], self.rstd[:, :]).then_inc(
            c.sems[f"{tag}_ssq"], 1)

    def store(self, sy):
        c, tag = self.ctx, self.tag
        for j in range(NCH):
            sy.wait_ge(c.sems[f"{tag}_nm"], j + 1)
            sy.dma_start(out=self.dst.ap()[j * 128:(j + 1) * 128, :],
                         in_=self.stream[j][:, :]).then_inc(c.sem("gst"), 16)
        c.gst += 16 * NCH



def _group_l0_attn(ctx, src, dst):
    nc = ctx.nc
    w = ctx.dram["w"]
    shr = ctx.dram["shr"]
    SQ = 1.0 / math.sqrt(128.0)
    NBLK = 48
    NIT = NH0 * 8

    with ExitStack() as ph:
        e = ph.enter_context
        stream = [e(nc.sbuf_tensor(f"st{c}", [128, T], F32R)) for c in range(NCH)]
        o_f = [e(nc.sbuf_tensor(f"o{c}", [128, T], F32R)) for c in range(NCH)]
        bias_sb = e(nc.sbuf_tensor("bias4", [128, 64], F32))
        bqs_sb = e(nc.sbuf_tensor("bqs", [128, 16], F32))
        ident = e(nc.sbuf_tensor("ident", [128, 128], BF16))
        ones_sb = ctx.sb["ones"]
        ph12 = ph.enter_context(ExitStack())
        e12 = ph12.enter_context
        q_bf = [e12(nc.sbuf_tensor(f"q{c}", [128, T], BF16)) for c in range(NCH)]

        gst0 = ctx.gst
        for s in ("p1_pre", "p1_wA", "p1_wB", "p1_mm", "p1_ev", "p1_vtr",
                  "p1_vte", "p1_kwrA", "p1_kwrB", "p1_vwrA", "p1_vwrB",
                  "p1_xch"):
            ctx.sem(s)
        p1_w = [ctx.sems["p1_wA"], ctx.sems["p1_wB"]]
        p1_kwr = [ctx.sems["p1_kwrA"], ctx.sems["p1_kwrB"]]
        p1_vwr = [ctx.sems["p1_vwrA"], ctx.sems["p1_vwrB"]]

        ph1 = ph12.enter_context(ExitStack())
        e1 = ph1.enter_context
        wbuf = [e1(nc.sbuf_tensor(f"wb{i}", [128, D], F32R)) for i in range(2)]
        kbuf = [e1(nc.sbuf_tensor(f"kb{i}", [128, T], BF16)) for i in range(2)]
        vbuf = [e1(nc.sbuf_tensor(f"vb{i}", [128, T], BF16)) for i in range(2)]
        vt_sb = [e1(nc.sbuf_tensor(f"vt{i}", [128, T], BF16)) for i in range(2)]
        ps_a = e1(nc.psum_tensor("ps_a", [128, T], F32))
        ps_b = e1(nc.psum_tensor("ps_b", [128, T], F32))
        ps_t = e1(nc.psum_tensor("ps_t", [128, T], BF16))
        psq = [ps_a, ps_b]
        with nc.Block("l0_qkv") as blk:
            @blk.sync
            def _(sy):
                if gst0:
                    sy.wait_ge(ctx.sems["gst"], gst0)
                for c in range(NCH):
                    sy.dma_start(out=stream[c][:, :],
                                 in_=src.ap()[c * 128:(c + 1) * 128, :]
                                 ).then_inc(ctx.sems["p1_pre"], 16)
                sy.dma_start(out=ident[:, :],
                             in_=w["ident_bf"][:, :]).then_inc(ctx.sems["p1_pre"], 16)
                for bi, bn in enumerate(("l0_bq", "l0_bk", "l0_bv", "l0_bo")):
                    sy.dma_start(out=bias_sb[:, bi * 16:(bi + 1) * 16],
                                 in_=w[bn][:, :]).then_inc(ctx.sems["p1_pre"], 16)
                for i in range(NBLK):
                    mat = ("l0_wq", "l0_wk", "l0_wv")[i // 16]
                    if i >= 2:
                        sy.wait_ge(ctx.sems["p1_mm"], i - 1)
                    sy.dma_start(out=wbuf[i % 2][:, :],
                                 in_=_wblock(w[mat], i % 16)
                                 ).then_inc(p1_w[i % 2], 16)

            @blk.tensor
            def _(te):
                def trans(ci):
                    te.wait_ge(ctx.sems["p1_ev"], 33 + ci)
                    if ci >= 1:
                        te.wait_ge(ctx.sems["p1_vte"], ci)
                    for tt in range(8):
                        mm = te.transpose(ps_t[:, tt * 128:(tt + 1) * 128],
                                          vbuf[ci % 2][:, tt * 128:(tt + 1) * 128],
                                          ident[:, :])
                    mm.then_inc(ctx.sems["p1_vtr"], 1)

                for i in range(NBLK):
                    if i == 0:
                        te.wait_ge(ctx.sems["p1_pre"], 16 * 21)
                    te.wait_ge(p1_w[i % 2], 16 * (i // 2 + 1))
                    if i >= 2:
                        te.wait_ge(ctx.sems["p1_ev"], i - 1)
                    for k in range(NCH):
                        lhs = wbuf[i % 2][:, k * 128:(k + 1) * 128]
                        for half in range(2):
                            mm = te.matmul(
                                psq[i % 2][:, half * 512:(half + 1) * 512],
                                lhs, stream[k][:, half * 512:(half + 1) * 512],
                                start=(k == 0), stop=(k == NCH - 1))
                    mm.then_inc(ctx.sems["p1_mm"], 1)
                    if i >= 34:
                        trans(i - 34)
                trans(14)
                trans(15)

            @blk.scalar
            def _(sc):
                sc.wait_ge(ctx.sems["p1_pre"], 16 * 21)
                sc.mul(bqs_sb[:, :], bias_sb[:, 0:16], SQ).then_inc(
                    ctx.sem("p1_bq"), 1)
                sc.wait_ge(ctx.sems["p1_bq"], 1)
                for i in range(NBLK):
                    sc.wait_ge(ctx.sems["p1_mm"], i + 1)
                    if i < 16:
                        sc.activation(q_bf[i][:, :], psq[i % 2][:, :], AF.Identity,
                                      bias=bqs_sb[:, i:i + 1], scale=SQ
                                      ).then_inc(ctx.sems["p1_ev"], 1)
                    elif i < 32:
                        ci = i - 16
                        if ci >= 2:
                            sc.wait_ge(p1_kwr[ci % 2], 16 * ((ci - 2) // 2 + 1))
                        sc.activation(kbuf[ci % 2][:, :], psq[i % 2][:, :], AF.Identity,
                                      bias=bias_sb[:, 16 + ci:17 + ci], scale=1.0
                                      ).then_inc(ctx.sems["p1_ev"], 1)
                    else:
                        ci = i - 32
                        if ci >= 2:
                            sc.wait_ge(ctx.sems["p1_vtr"], ci - 1)
                        sc.activation(vbuf[ci % 2][:, :], psq[i % 2][:, :], AF.Identity,
                                      bias=bias_sb[:, 32 + ci:33 + ci], scale=1.0
                                      ).then_inc(ctx.sems["p1_ev"], 1)

            @blk.vector
            def _(ve):
                for ci in range(16):
                    ve.wait_ge(ctx.sems["p1_vtr"], ci + 1)
                    if ci >= 2:
                        ve.wait_ge(p1_vwr[ci % 2], 16 * ((ci - 2) // 2 + 1))
                    ve.tensor_copy(vt_sb[ci % 2][:, :], ps_t[:, :]).then_inc(
                        ctx.sems["p1_vte"], 1)

            @blk.gpsimd
            def _(gp):
                ctx.magic_seq += 1
                magic = MAGIC_BASE + ctx.magic_seq

                def emit(slot):
                    for ci in range(16):
                        gp.wait_ge(ctx.sems["p1_ev"], 17 + ci)
                        out = shr["k0"].ap()[slot * 2048 + ci * 128:
                                             slot * 2048 + (ci + 1) * 128, :]
                        gp.dma_start(out=out, in_=kbuf[ci % 2][:, :]).then_inc(
                            p1_kwr[ci % 2], 16)
                    for ci in range(16):
                        gp.wait_ge(ctx.sems["p1_vte"], ci + 1)
                        out = shr["v0"].ap()[slot * 128:(slot + 1) * 128,
                                             ci * 1024:(ci + 1) * 1024]
                        gp.dma_start(out=out, in_=vt_sb[ci % 2][:, :]).then_inc(
                            p1_vwr[ci % 2], 16)
                    gp.wait_ge(p1_kwr[0], 16 * 8)
                    gp.wait_ge(p1_kwr[1], 16 * 8)
                    gp.wait_ge(p1_vwr[0], 16 * 8)
                    gp.wait_ge(p1_vwr[1], 16 * 8)
                    _exchange_flag(gp, ctx, slot, magic)
                with gp.If_eq(ctx.regs["rank"], 0):
                    emit(0)
                with gp.Else():
                    emit(1)
                _exchange_poll(gp, ctx, magic)
                gp.sem_inc(ctx.sems["p1_xch"], 1)

        ph1.close()

        if STOP_AFTER == "p1":
            with nc.Block("dbg_store") as blk:
                @blk.sync
                def _(sy):
                    for j in range(NCH):
                        sy.dma_start(out=dst.ap()[j * 128:(j + 1) * 128, :],
                                     in_=stream[j][:, :]).then_inc(ctx.sem("gst"), 16)
            ctx.gst += 16 * NCH
            return

        with ExitStack() as ph2:
            e2 = ph2.enter_context
            kh = [e2(nc.sbuf_tensor(f"kh{i}", [128, 2048], BF16)) for i in range(2)]
            vh = [e2(nc.sbuf_tensor(f"vh{i}", [128, 2048], BF16)) for i in range(2)]
            p_bf = [e2(nc.sbuf_tensor(f"pb{i}", [128, 2048], BF16)) for i in range(2)]
            pt_sb = [e2(nc.sbuf_tensor(f"pt{i}", [128, 2048], BF16)) for i in range(2)]
            negmax = [e2(nc.sbuf_tensor(f"ngm{i}", [128, 1], F32)) for i in range(2)]
            sumexp = [e2(nc.sbuf_tensor(f"sme{i}", [128, 1], F32)) for i in range(2)]
            rsum = [e2(nc.sbuf_tensor(f"rsm{i}", [128, 1], F32)) for i in range(2)]
            ps_sc = e2(nc.psum_tensor("ps_sc", [128, 2048], F32))
            ps_pt = e2(nc.psum_tensor("ps_pt", [128, 2048], BF16))
            ps_o = [e2(nc.psum_tensor(f"ps_o{i}", [128, 512], F32)) for i in range(2)]

            for s in ("p2_ldA", "p2_ldB", "p2_sc", "p2_mx", "p2_ex", "p2_pn",
                      "p2_tr", "p2_pte", "p2_mm", "p2_oe", "p2_rc"):
                ctx.sem(s)
            p2_ld = [ctx.sems["p2_ldA"], ctx.sems["p2_ldB"]]

            with nc.Block("l0_attn") as blk:
                @blk.sync
                def _(sy):
                    sy.wait_ge(ctx.sems["p1_xch"], 1)
                    for h in range(NH0):
                        if h >= 2:
                            sy.wait_ge(ctx.sems["p2_mm"], 8 * (h - 1))
                        for slot in range(2):
                            sy.dma_start(
                                out=kh[h % 2][:, slot * 1024:(slot + 1) * 1024],
                                in_=shr["k0"].ap()[slot * 2048 + h * 128:
                                                   slot * 2048 + (h + 1) * 128, :],
                            ).then_inc(p2_ld[h % 2], 16)
                        for slot in range(2):
                            sy.dma_start(
                                out=vh[h % 2][:, slot * 1024:(slot + 1) * 1024],
                                in_=shr["v0"].ap()[slot * 128:(slot + 1) * 128,
                                                   h * 1024:(h + 1) * 1024],
                            ).then_inc(p2_ld[h % 2], 16)

                def _S4(te, it):
                    h, par = it // 8, it % 2
                    te.wait_ge(ctx.sems["p2_pn"], it + 1)
                    for kt in range(16):
                        mm = te.transpose(ps_pt[:, kt * 128:(kt + 1) * 128],
                                          p_bf[par][:, kt * 128:(kt + 1) * 128],
                                          ident[:, :])
                    mm.then_inc(ctx.sems["p2_tr"], 1)

                def _S6(te, it):
                    h, par = it // 8, it % 2
                    te.wait_ge(ctx.sems["p2_pte"], it + 1)
                    if it >= 2:
                        te.wait_ge(ctx.sems["p2_oe"], it - 1)
                    for kt in range(16):
                        mm = te.matmul(ps_o[par][:, 0:128],
                                       vh[h % 2][:, kt * 128:(kt + 1) * 128],
                                       pt_sb[par][:, kt * 128:(kt + 1) * 128],
                                       start=(kt == 0), stop=(kt == 15))
                    mm.then_inc(ctx.sems["p2_mm"], 1)

                @blk.tensor
                def _(te):
                    for it in range(NIT):
                        h, qt = divmod(it, 8)
                        if qt == 0:
                            te.wait_ge(p2_ld[h % 2], 64 * (h // 2 + 1))
                        if it >= 1:
                            te.wait_ge(ctx.sems["p2_ex"], it)
                        for kb in range(4):
                            mm = te.matmul(ps_sc[:, kb * 512:(kb + 1) * 512],
                                           q_bf[h][:, qt * 128:(qt + 1) * 128],
                                           kh[h % 2][:, kb * 512:(kb + 1) * 512],
                                           start=True, stop=True)
                        mm.then_inc(ctx.sems["p2_sc"], 1)
                        if it >= 1:
                            _S4(te, it - 1)
                            _S6(te, it - 1)
                    _S4(te, NIT - 1)
                    _S6(te, NIT - 1)

                @blk.vector
                def _(ve):
                    for it in range(NIT):
                        par = it % 2
                        ve.wait_ge(ctx.sems["p2_sc"], it + 1)
                        if it >= 2:
                            ve.wait_ge(ctx.sems["p2_ex"], it - 1)
                        ve.reduce_max(
                            negmax[par][:, :],
                            ps_sc.ap().rearrange("p (a b) -> p a b", b=4)[:, :, 0:1],
                            axis=AX.XY, negate=True).then_inc(ctx.sems["p2_mx"], 1)
                        if it >= 1:
                            parp = (it - 1) % 2
                            ve.wait_ge(ctx.sems["p2_ex"], it)
                            ve.reciprocal(rsum[parp][:, :], sumexp[parp][:, :]
                                          ).then_inc(ctx.sems["p2_rc"], 1)
                            ve.wait_ge(ctx.sems["p2_rc"], it)
                            ve.tensor_scalar_mul(
                                p_bf[parp][:, :], p_bf[parp][:, :],
                                rsum[parp][:, 0:1]).then_inc(ctx.sems["p2_pn"], 1)
                    parp = (NIT - 1) % 2
                    ve.wait_ge(ctx.sems["p2_ex"], NIT)
                    ve.reciprocal(rsum[parp][:, :], sumexp[parp][:, :]).then_inc(
                        ctx.sems["p2_rc"], 1)
                    ve.wait_ge(ctx.sems["p2_rc"], NIT)
                    ve.tensor_scalar_mul(p_bf[parp][:, :], p_bf[parp][:, :],
                                         rsum[parp][:, 0:1]).then_inc(
                        ctx.sems["p2_pn"], 1)

                def _S5(sc, it):
                    sc.wait_ge(ctx.sems["p2_tr"], it + 1)
                    sc.copy(pt_sb[it % 2][:, :], ps_pt[:, :]).then_inc(
                        ctx.sems["p2_pte"], 1)

                def _S7(sc, it):
                    h, qt = divmod(it, 8)
                    sc.wait_ge(ctx.sems["p2_mm"], it + 1)
                    sc.copy(o_f[h][:, qt * 128:(qt + 1) * 128],
                            ps_o[it % 2][:, 0:128]).then_inc(ctx.sems["p2_oe"], 1)

                @blk.scalar
                def _(sc):
                    for it in range(NIT):
                        par = it % 2
                        sc.wait_ge(ctx.sems["p2_mx"], it + 1)
                        if it >= 2:
                            sc.wait_ge(ctx.sems["p2_tr"], it - 1)
                        sc.activation(p_bf[par][:, :], ps_sc[:, :], AF.Exp,
                                      bias=negmax[par][:, 0:1], scale=1.0,
                                      accum_out=sumexp[par][:, :]).then_inc(
                            ctx.sems["p2_ex"], 1)
                        if it >= 1:
                            _S5(sc, it - 1)
                        if it >= 2:
                            _S7(sc, it - 2)
                    _S5(sc, NIT - 1)
                    _S7(sc, NIT - 2)
                    _S7(sc, NIT - 1)

        ph12.close()

        if STOP_AFTER == "p2":
            with nc.Block("dbg_store2") as blk:
                @blk.sync
                def _(sy):
                    for j in range(NCH):
                        sy.dma_start(out=dst.ap()[j * 128:(j + 1) * 128, :],
                                     in_=o_f[j][:, :]).then_inc(ctx.sem("gst"), 16)
            ctx.gst += 16 * NCH
            return

        with ExitStack() as ph3:
            e3 = ph3.enter_context
            wbuf = [e3(nc.sbuf_tensor(f"wb3{i}", [128, D], F32R)) for i in range(2)]
            sq_buf = [e3(nc.sbuf_tensor(f"sq{i}", [128, T], F32R)) for i in range(2)]
            rstd_sb = e3(nc.sbuf_tensor("rstd", [128, T], F32))
            rstd2_sb = e3(nc.sbuf_tensor("rstd2", [128, T], F32))
            ps_sum = e3(nc.psum_tensor("ps_sum", [128, T], F32))
            ps_var = e3(nc.psum_tensor("ps_var", [128, T], F32))
            psq = [e3(nc.psum_tensor(f"ps3_{i}", [128, T], F32)) for i in range(2)]

            for s in ("p3_wA", "p3_wB", "p3_mm", "p3_res"):
                ctx.sem(s)
            p3_w = [ctx.sems["p3_wA"], ctx.sems["p3_wB"]]
            ln = LN(ctx, "ln1", stream, ones_sb, sq_buf, rstd_sb, rstd2_sb,
                    ps_sum, ps_var, ("p3_res", lambda j: j + 1), dst)

            skip_ln = STOP_AFTER == "p3res"
            with nc.Block("l0_wo") as blk:
                @blk.sync
                def _(sy):
                    for j in range(NCH):
                        if j >= 2:
                            sy.wait_ge(ctx.sems["p3_mm"], j - 1)
                        sy.dma_start(out=wbuf[j % 2][:, :],
                                     in_=_wblock(w["l0_wo"], j)
                                     ).then_inc(p3_w[j % 2], 16)
                    if skip_ln:
                        for j in range(NCH):
                            sy.wait_ge(ctx.sems["p3_res"], j + 1)
                            sy.dma_start(out=dst.ap()[j * 128:(j + 1) * 128, :],
                                         in_=stream[j][:, :]).then_inc(
                                ctx.sem("gst"), 16)
                        ctx.gst += 16 * NCH
                    else:
                        ln.store(sy)

                @blk.tensor
                def _(te):
                    for j in range(NCH):
                        te.wait_ge(p3_w[j % 2], 16 * (j // 2 + 1))
                        if j >= 2:
                            te.wait_ge(ctx.sems["p3_res"], j - 1)
                        for oc in range(NCH):
                            lhs = wbuf[j % 2][:, oc * 128:(oc + 1) * 128]
                            for half in range(2):
                                mm = te.matmul(
                                    psq[j % 2][:, half * 512:(half + 1) * 512],
                                    lhs, o_f[oc][:, half * 512:(half + 1) * 512],
                                    start=(oc == 0),
                                    stop=(oc == NCH - 1))
                        mm.then_inc(ctx.sems["p3_mm"], 1)
                    if not skip_ln:
                        ln.te(te)

                @blk.vector
                def _(ve):
                    for j in range(NCH):
                        ve.wait_ge(ctx.sems["p3_mm"], j + 1)
                        ve.scalar_tensor_tensor(
                            stream[j][:, :], psq[j % 2][:, :],
                            bias_sb[:, 48 + j:49 + j], stream[j][:, :],
                            ALU.add, ALU.add).then_inc(ctx.sems["p3_res"], 1)
                    if not skip_ln:
                        ln.ve(ve)

                @blk.scalar
                def _(sc):
                    if not skip_ln:
                        ln.sc(sc)




def _group_l0_ffn(ctx, src, dst):
    nc = ctx.nc
    w = ctx.dram["w"]
    spill = ctx.dram["spill"]
    NHC = HFF0 // 128

    with ExitStack() as ph:
        e = ph.enter_context
        stat_acc = e(nc.sbuf_tensor("stat6", [128, T], F32))
        ph_h = ph.enter_context(ExitStack())
        h = [ph_h.enter_context(nc.sbuf_tensor(f"h{c}", [128, T], BF16))
             for c in range(NHC)]

        with ExitStack() as p4:
            e4 = p4.enter_context
            xh = [e4(nc.sbuf_tensor(f"xh{c}", [128, 512], F32R)) for c in range(NCH)]
            w1b = [e4(nc.sbuf_tensor(f"w1b{i}", [128, D], F32R)) for i in range(2)]
            w3b = [e4(nc.sbuf_tensor(f"w3b{i}", [128, D], F32R)) for i in range(2)]
            gs = [e4(nc.sbuf_tensor(f"gs{i}", [128, 512], F32)) for i in range(2)]
            ps_g = [e4(nc.psum_tensor(f"ps_g{i}", [128, 512], F32)) for i in range(2)]
            ps_u = [e4(nc.psum_tensor(f"ps_u{i}", [128, 512], F32)) for i in range(2)]
            for s in ("p4_pre", "p4_wA", "p4_wB", "p4_mm", "p4_sl", "p4_ev"):
                ctx.sem(s)
            p4_w = [ctx.sems["p4_wA"], ctx.sems["p4_wB"]]
            gst0 = ctx.gst

            with nc.Block("l0_up") as blk:
                @blk.sync
                def _(sy):
                    for hf in range(2):
                        if hf == 0 and gst0:
                            sy.wait_ge(ctx.sems["gst"], gst0)
                        if hf == 1:
                            sy.wait_ge(ctx.sems["p4_mm"], NHC)
                        for c in range(NCH):
                            sy.dma_start(
                                out=xh[c][:, :],
                                in_=src.ap()[c * 128:(c + 1) * 128,
                                             hf * 512:(hf + 1) * 512],
                            ).then_inc(ctx.sems["p4_pre"], 16)
                        for hc in range(NHC):
                            g = hf * NHC + hc
                            if g >= 2:
                                sy.wait_ge(ctx.sems["p4_mm"], g - 1)
                            sy.dma_start(out=w1b[g % 2][:, :],
                                         in_=_wblock(w["l0_w1"], hc)
                                         ).then_inc(p4_w[g % 2], 16)
                            sy.dma_start(out=w3b[g % 2][:, :],
                                         in_=_wblock(w["l0_w3"], hc)
                                         ).then_inc(p4_w[g % 2], 16)

                @blk.tensor
                def _(te):
                    for hf in range(2):
                        for hc in range(NHC):
                            g = hf * NHC + hc
                            if hc == 0:
                                te.wait_ge(ctx.sems["p4_pre"], 16 * 16 * (hf + 1))
                            te.wait_ge(p4_w[g % 2], 32 * (g // 2 + 1))
                            if g >= 2:
                                te.wait_ge(ctx.sems["p4_ev"], g - 1)
                            for k in range(NCH):
                                mm = te.matmul(ps_g[g % 2][:, :],
                                               w1b[g % 2][:, k * 128:(k + 1) * 128],
                                               xh[k][:, :],
                                               start=(k == 0), stop=(k == NCH - 1))
                            for k in range(NCH):
                                mm = te.matmul(ps_u[g % 2][:, :],
                                               w3b[g % 2][:, k * 128:(k + 1) * 128],
                                               xh[k][:, :],
                                               start=(k == 0), stop=(k == NCH - 1))
                            mm.then_inc(ctx.sems["p4_mm"], 1)

                @blk.scalar
                def _(sc):
                    for g in range(2 * NHC):
                        sc.wait_ge(ctx.sems["p4_mm"], g + 1)
                        if g >= 2:
                            sc.wait_ge(ctx.sems["p4_ev"], g - 1)
                        sc.activation(gs[g % 2][:, :], ps_g[g % 2][:, :],
                                      AF.Silu).then_inc(ctx.sems["p4_sl"], 1)

                @blk.vector
                def _(ve):
                    for hf in range(2):
                        for hc in range(NHC):
                            g = hf * NHC + hc
                            ve.wait_ge(ctx.sems["p4_sl"], g + 1)
                            ve.tensor_mul(
                                h[hc][:, hf * 512:(hf + 1) * 512],
                                gs[g % 2][:, :], ps_u[g % 2][:, :],
                            ).then_inc(ctx.sems["p4_ev"], 1)

        if "uponly" in G2_SKIP:
            with nc.Block("dbg_g2") as blk:
                @blk.sync
                def _(sy):
                    for j in range(NCH):
                        sy.dma_start(out=dst.ap()[j * 128:(j + 1) * 128, :],
                                     in_=src.ap()[j * 128:(j + 1) * 128, :]
                                     ).then_inc(ctx.sem("gst"), 16)
            ctx.gst += 16 * NCH
            return

        if "fwht" in G2_SKIP:
            p5 = None
        with ExitStack() as p5:
            e5 = p5.enter_context
            ta = e5(nc.sbuf_tensor("fw_ta", [128, T], BF16))
            tb = e5(nc.sbuf_tensor("fw_tb", [128, T], BF16))
            w128sb = e5(nc.sbuf_tensor("w128sb", [128, 128], BF16))
            ps_w = [e5(nc.psum_tensor(f"ps_w{i}", [128, T], F32)) for i in range(2)]
            for s in ("p5_pre", "p5_ch", "p5_mm", "p5_ev"):
                ctx.sem(s)

            with nc.Block("l0_fwht") as blk:
                if "fwht" in G2_SKIP:
                    blk
                @blk.sync
                def _(sy):
                    if "fwht" in G2_SKIP:
                        return
                    sy.dma_start(out=w128sb[:, :],
                                 in_=w["w128_bf"][:, :]).then_inc(
                        ctx.sems["p5_pre"], 16)

                @blk.vector
                def _(ve):
                    if "fwht" in G2_SKIP:
                        return
                    ch = ctx.sems["p5_ch"]
                    n = 0
                    for s in range(6):
                        step = 1 << s
                        pairs = [(a, a + step) for a in range(NHC)
                                 if (a // step) % 2 == 0]
                        for (a, b) in pairs:
                            ve.tensor_add(ta[:, :], h[a][:, :], h[b][:, :]
                                          ).then_inc(ch, 1)
                            ve.tensor_sub(tb[:, :], h[a][:, :], h[b][:, :]
                                          ).then_inc(ch, 1)
                            n += 2
                            ve.wait_ge(ch, n)
                            ve.tensor_copy(h[a][:, :], ta[:, :]).then_inc(ch, 1)
                            ve.tensor_copy(h[b][:, :], tb[:, :]).then_inc(ch, 1)
                            n += 2
                            ve.wait_ge(ch, n)
                    for hc in range(NHC):
                        ve.wait_ge(ctx.sems["p5_mm"], hc + 1)
                        ve.tensor_copy(h[hc][:, :], ps_w[hc % 2][:, :]).then_inc(
                            ctx.sems["p5_ev"], 1)

                @blk.tensor
                def _(te):
                    if "fwht" in G2_SKIP:
                        return
                    te.wait_ge(ctx.sems["p5_pre"], 16)
                    te.wait_ge(ctx.sems["p5_ch"], 4 * 6 * 32)
                    for hc in range(NHC):
                        if hc >= 2:
                            te.wait_ge(ctx.sems["p5_ev"], hc - 1)
                        for half in range(2):
                            mm = te.matmul(
                                ps_w[hc % 2][:, half * 512:(half + 1) * 512],
                                w128sb[:, :],
                                h[hc][:, half * 512:(half + 1) * 512],
                                start=True, stop=True)
                        mm.then_inc(ctx.sems["p5_mm"], 1)

        with ExitStack() as p6:
            e6 = p6.enter_context
            w2r = [e6(nc.sbuf_tensor(f"w2r{i}", [128, D], F32R)) for i in range(2)]
            w2q = [e6(nc.sbuf_tensor(f"w2q{i}", [128, D], BF16)) for i in range(2)]
            xres = [e6(nc.sbuf_tensor(f"xr{i}", [128, T], F32R)) for i in range(2)]
            ybuf = [e6(nc.sbuf_tensor(f"yb{i}", [128, T], F32R)) for i in range(2)]
            ps_y = [e6(nc.psum_tensor(f"ps_y{i}", [128, T], F32)) for i in range(2)]
            ps_stat = e6(nc.psum_tensor("ps_stat", [128, T], F32))
            for s in ("p6_wA", "p6_wB", "p6_xA", "p6_xB", "p6_c", "p6_mmq",
                      "p6_sj", "p6_res", "p6_stA", "p6_stB", "p6_sums"):
                ctx.sem(s)
            p6_w = [ctx.sems["p6_wA"], ctx.sems["p6_wB"]]
            p6_x = [ctx.sems["p6_xA"], ctx.sems["p6_xB"]]
            p6_st = [ctx.sems["p6_stA"], ctx.sems["p6_stB"]]
            SC128 = 1.0 / math.sqrt(float(HFF0))

            with nc.Block("l0_down") as blk:
                @blk.sync
                def _(sy):
                    for j in range(NCH):
                        if j >= 2:
                            sy.wait_ge(ctx.sems["p6_res"], j - 1)
                        sy.dma_start(out=xres[j % 2][:, :],
                                     in_=src.ap()[j * 128:(j + 1) * 128, :]
                                     ).then_inc(p6_x[j % 2], 16)
                        for q in range(4):
                            gq = j * 4 + q
                            if gq >= 2:
                                sy.wait_ge(ctx.sems["p6_c"], gq - 1)
                            sy.dma_start(
                                out=w2r[gq % 2][:, :],
                                in_=_wblock(w["l0_w2"], j, q * 2048, 2048),
                            ).then_inc(p6_w[gq % 2], 16)
                        sy.wait_ge(ctx.sems["p6_res"], j + 1)
                        sy.dma_start(out=spill.ap()[j * 128:(j + 1) * 128, :],
                                     in_=ybuf[j % 2][:, :]).then_inc(
                            p6_st[j % 2], 16)

                @blk.vector
                def _(ve):
                    for j in range(NCH):
                        for q in range(4):
                            gq = j * 4 + q
                            ve.wait_ge(p6_w[gq % 2], 16 * (gq // 2 + 1))
                            if gq >= 2:
                                ve.wait_ge(ctx.sems["p6_mmq"], gq - 1)
                            ve.tensor_scalar_mul(w2q[gq % 2][:, :],
                                                 w2r[gq % 2][:, :], SC128
                                                 ).then_inc(ctx.sems["p6_c"], 1)
                        ve.wait_ge(ctx.sems["p6_mmq"], 4 * (j + 1))
                        if j >= 2:
                            ve.wait_ge(p6_st[j % 2], 16 * ((j - 2) // 2 + 1))
                            ve.wait_ge(ctx.sems["p6_sums"], j - 1)
                        ve.wait_ge(p6_x[j % 2], 16 * (j // 2 + 1))
                        ve.scalar_tensor_tensor(
                            ybuf[j % 2][:, :], ps_y[j % 2][:, :], 1.0,
                            xres[j % 2][:, :], ALU.mult, ALU.add,
                        ).then_inc(ctx.sems["p6_res"], 1)
                        ve.wait_ge(ctx.sems["p6_sj"], j + 1)
                        if j == 0:
                            ve.tensor_scalar_mul(stat_acc[:, :], ps_stat[:, :],
                                                 1.0).then_inc(
                                ctx.sems["p6_sums"], 1)
                        else:
                            ve.tensor_add(stat_acc[:, :], stat_acc[:, :],
                                          ps_stat[:, :]).then_inc(
                                ctx.sems["p6_sums"], 1)

                @blk.tensor
                def _(te):
                    for j in range(NCH):
                        for q in range(4):
                            gq = j * 4 + q
                            te.wait_ge(ctx.sems["p6_c"], gq + 1)
                            if j >= 2 and q == 0:
                                te.wait_ge(ctx.sems["p6_res"], j - 1)
                            for f in range(16):
                                hc = q * 16 + f
                                for half in range(2):
                                    mm = te.matmul(
                                        ps_y[j % 2][:, half * 512:(half + 1) * 512],
                                        w2q[gq % 2][:, f * 128:(f + 1) * 128],
                                        h[hc][:, half * 512:(half + 1) * 512],
                                        start=(hc == 0), stop=(hc == NHC - 1))
                            mm.then_inc(ctx.sems["p6_mmq"], 1)
                        te.wait_ge(ctx.sems["p6_res"], j + 1)
                        if j >= 1:
                            te.wait_ge(ctx.sems["p6_sums"], j)
                        for half in range(2):
                            mm = te.matmul(
                                ps_stat[:, half * 512:(half + 1) * 512],
                                ctx.sb["ones"][:, :],
                                ybuf[j % 2][:, half * 512:(half + 1) * 512],
                                start=True, stop=True)
                        mm.then_inc(ctx.sems["p6_sj"], 1)

        ph_h.close()

        if "noln" in G2_SKIP:
            with ExitStack() as pdbg:
                bt = pdbg.enter_context(nc.sbuf_tensor("dbg_bt", [128, T], F32R))
                with nc.Block("dbg_g2c") as blk:
                    @blk.sync
                    def _(sy):
                        sy.wait_ge(p6_st[0], 16 * 8)
                        sy.wait_ge(p6_st[1], 16 * 8)
                        n = 0
                        for j in range(NCH):
                            sy.dma_start(out=bt[:, :],
                                         in_=spill.ap()[j * 128:(j + 1) * 128, :]
                                         ).then_inc(ctx.sem("dbgl"), 16)
                            n += 16
                            sy.wait_ge(ctx.sems["dbgl"], n)
                            sy.dma_start(out=dst.ap()[j * 128:(j + 1) * 128, :],
                                         in_=bt[:, :]).then_inc(ctx.sem("gst"), 16)
                ctx.gst += 16 * NCH
            return

        with ExitStack() as p6b:
            eb = p6b.enter_context
            stream = [eb(nc.sbuf_tensor(f"s6_{c}", [128, T], F32R))
                      for c in range(NCH)]
            sq_buf = [eb(nc.sbuf_tensor(f"sq6_{i}", [128, T], F32R))
                      for i in range(2)]
            rstd_sb = eb(nc.sbuf_tensor("rstd6", [128, T], F32))
            rstd2_sb = eb(nc.sbuf_tensor("rstd6b", [128, T], F32))
            ps_var = eb(nc.psum_tensor("ps6_var", [128, T], F32))
            for s in ("p6b_lA", "p6b_lB"):
                ctx.sem(s)
            p6b_l = [ctx.sems["p6b_lA"], ctx.sems["p6b_lB"]]

            ln = LN(ctx, "ln2", stream, ctx.sb["ones"], sq_buf, rstd_sb,
                    rstd2_sb, stat_acc, ps_var,
                    ("p6b_lA", lambda j: 16 * (j // 2 + 1)), dst, sum_done=True,
                    ve_rdy=(p6b_l, lambda j: 16 * (j // 2 + 1)))

            with nc.Block("l0_ln2") as blk:
                @blk.sync
                def _(sy):
                    sy.wait_ge(p6_st[0], 16 * 8)
                    sy.wait_ge(p6_st[1], 16 * 8)
                    for j in range(NCH):
                        sy.dma_start(out=stream[j][:, :],
                                     in_=spill.ap()[j * 128:(j + 1) * 128, :]
                                     ).then_inc(p6b_l[j % 2], 16)
                    ln.store(sy)

                @blk.tensor
                def _(te):
                    ln.te(te)

                @blk.vector
                def _(ve):
                    ln.ve(ve)

                @blk.scalar
                def _(sc):
                    ln.sc(sc)


def _group_l1_attn(ctx, src, dst):
    nc = ctx.nc
    w = ctx.dram["w"]
    shr = ctx.dram["shr"]
    SQ8 = 1.0 / 8.0

    with ExitStack() as ph:
        e = ph.enter_context
        stream = [e(nc.sbuf_tensor(f"s7_{c}", [128, T], F32R)) for c in range(NCH)]
        ctxh = [e(nc.sbuf_tensor(f"cx{c}", [128, T], BF16)) for c in range(NCH)]
        q_bf = [e(nc.sbuf_tensor(f"q7_{c}", [128, T], BF16)) for c in range(NCH)]
        ident = e(nc.sbuf_tensor("id7", [128, 128], BF16))
        diag_sb = e(nc.sbuf_tensor("dg7", [128, 128], F32))

        with ExitStack() as p7:
            e7 = p7.enter_context
            wbuf = [e7(nc.sbuf_tensor(f"w7_{i}", [128, D], F32R)) for i in range(2)]
            qf = [e7(nc.sbuf_tensor(f"qf{i}", [128, T], F32R)) for i in range(2)]
            kobf = [e7(nc.sbuf_tensor(f"ko{i}", [128, T], BF16)) for i in range(2)]
            tmp1 = e7(nc.sbuf_tensor("rt1", [128, T], F32))
            tmp2 = e7(nc.sbuf_tensor("rt2", [128, T], F32))
            cosb = e7(nc.sbuf_tensor("cosb", [128, T], F32))
            sinb = e7(nc.sbuf_tensor("sinb", [128, T], F32))
            rswb = e7(nc.sbuf_tensor("rswb", [128, 128], F32R))
            ps_q = [e7(nc.psum_tensor(f"pq{i}", [128, T], F32)) for i in range(2)]
            ps_r = [e7(nc.psum_tensor(f"pr{i}", [128, T], F32)) for i in range(2)]
            for s in ("p7_pre", "p7_wA", "p7_wB", "p7_mm", "p7_qe", "p7_rm",
                      "p7_t1", "p7_t2", "p7_qd", "p7_kwrA", "p7_kwrB",
                      "p7_vm", "p7_ve", "p7_vwrA", "p7_vwrB"):
                ctx.sem(s)
            p7_w = [ctx.sems["p7_wA"], ctx.sems["p7_wB"]]
            p7_kwr = [ctx.sems["p7_kwrA"], ctx.sems["p7_kwrB"]]
            p7_vwr = [ctx.sems["p7_vwrA"], ctx.sems["p7_vwrB"]]
            gst0 = ctx.gst
            NB = 20
            ctx.magic_seq += 1
            magic = MAGIC_BASE + ctx.magic_seq

            with nc.Block("l1_qkv") as blk:
                @blk.sync
                def _(sy):
                    sem = ctx.sems["p7_pre"]
                    if gst0:
                        sy.wait_ge(ctx.sems["gst"], gst0)
                    for c in range(NCH):
                        sy.dma_start(out=stream[c][:, :],
                                     in_=src.ap()[c * 128:(c + 1) * 128, :]
                                     ).then_inc(sem, 16)
                    sy.dma_start(out=ident[:, :], in_=w["ident_bf"][:, :]
                                 ).then_inc(sem, 16)
                    sy.dma_start(out=diag_sb[:, :], in_=w["diag_mask"][:, :]
                                 ).then_inc(sem, 16)
                    sy.dma_start(out=cosb[:, :], in_=w["cos_rep"][:, :]
                                 ).then_inc(sem, 16)
                    sy.dma_start(out=sinb[:, :], in_=w["sin_rep"][:, :]
                                 ).then_inc(sem, 16)
                    sy.dma_start(out=rswb[:, :], in_=w["rsw"][:, :]
                                 ).then_inc(sem, 16)
                    for i in range(NB):
                        mat, jb = ("l1_wq", i) if i < 16 else ("l1_wk", i - 16)
                        if i >= 2:
                            sy.wait_ge(ctx.sems["p7_mm"], i - 1)
                        sy.dma_start(out=wbuf[i % 2][:, :],
                                     in_=_wblock(w[mat], jb)
                                     ).then_inc(p7_w[i % 2], 16)

                @blk.tensor
                def _(te):
                    def rot(i):
                        te.wait_ge(ctx.sems["p7_qe"], i + 1)
                        if i >= 2:
                            te.wait_ge(ctx.sems["p7_t2"], i - 1)
                        for half in range(2):
                            mm = te.matmul(
                                ps_r[i % 2][:, half * 512:(half + 1) * 512],
                                rswb[:, :],
                                qf[i % 2][:, half * 512:(half + 1) * 512],
                                start=True, stop=True)
                        mm.then_inc(ctx.sems["p7_rm"], 1)

                    for i in range(NB):
                        if i == 0:
                            te.wait_ge(ctx.sems["p7_pre"], 16 * 21)
                        te.wait_ge(p7_w[i % 2], 16 * (i // 2 + 1))
                        if i >= 2:
                            te.wait_ge(ctx.sems["p7_qe"], i - 1)
                        for k in range(NCH):
                            for half in range(2):
                                mm = te.matmul(
                                    ps_q[i % 2][:, half * 512:(half + 1) * 512],
                                    wbuf[i % 2][:, k * 128:(k + 1) * 128],
                                    stream[k][:, half * 512:(half + 1) * 512],
                                    start=(k == 0), stop=(k == NCH - 1))
                        mm.then_inc(ctx.sems["p7_mm"], 1)
                        if i >= 1:
                            rot(i - 1)
                    rot(NB - 1)

                @blk.scalar
                def _(sc):
                    for i in range(NB):
                        sc.wait_ge(ctx.sems["p7_mm"], i + 1)
                        if i >= 2:
                            sc.wait_ge(ctx.sems["p7_t1"], i - 1)
                        scale = SQ8 if i < 16 else 1.0
                        sc.activation(qf[i % 2][:, :], ps_q[i % 2][:, :],
                                      AF.Identity, bias=0.0, scale=scale
                                      ).then_inc(ctx.sems["p7_qe"], 1)


                @blk.vector
                def _(ve):
                    for i in range(NB):
                        ve.wait_ge(ctx.sems["p7_qe"], i + 1)
                        if i >= 1:
                            ve.wait_ge(ctx.sems["p7_qd"], i)
                        ve.tensor_mul(tmp1[:, :], qf[i % 2][:, :], cosb[:, :]
                                      ).then_inc(ctx.sems["p7_t1"], 1)
                        ve.wait_ge(ctx.sems["p7_rm"], i + 1)
                        ve.tensor_mul(tmp2[:, :], ps_r[i % 2][:, :], sinb[:, :]
                                      ).then_inc(ctx.sems["p7_t2"], 1)
                        ve.wait_ge(ctx.sems["p7_t1"], i + 1)
                        ve.wait_ge(ctx.sems["p7_t2"], i + 1)
                        dstt = q_bf[i] if i < 16 else kobf[(i - 16) % 2]
                        if i >= 18:
                            ve.wait_ge(p7_kwr[i % 2], 16 * ((i - 18) // 2 + 1))
                        ve.tensor_add(dstt[:, :], tmp1[:, :], tmp2[:, :]
                                      ).then_inc(ctx.sems["p7_qd"], 1)

                @blk.gpsimd
                def _(gp):
                    def emit(slot):
                        for i in range(4):
                            gp.wait_ge(ctx.sems["p7_qd"], 17 + i)
                            out = shr["k1"].ap()[slot * 512 + i * 128:
                                                 slot * 512 + (i + 1) * 128, :]
                            gp.dma_start(out=out, in_=kobf[i % 2][:, :]
                                         ).then_inc(p7_kwr[i % 2], 16)
                    with gp.If_eq(ctx.regs["rank"], 0):
                        emit(0)
                    with gp.Else():
                        emit(1)

        with ExitStack() as p7b:
            e7b = p7b.enter_context
            wv_sb = [e7b(nc.sbuf_tensor(f"wv{k}", [128, 512], F32R))
                     for k in range(NCH)]
            vt = [e7b(nc.sbuf_tensor(f"vt7{i}", [128, 512], BF16))
                  for i in range(2)]
            ps_v = [e7b(nc.psum_tensor(f"pv7{i}", [128, 512], F32))
                    for i in range(2)]
            with nc.Block("l1_v") as blk:
                @blk.sync
                def _(sy):
                    for k in range(NCH):
                        sy.dma_start(out=wv_sb[k][:, :],
                                     in_=w["l1_wv_r"].ap()[k * 128:(k + 1) * 128, :]
                                     ).then_inc(ctx.sem("p7b_pre"), 16)

                @blk.tensor
                def _(te):
                    te.wait_ge(ctx.sems["p7b_pre"], 16 * 16)
                    for tt in range(8):
                        if tt >= 2:
                            te.wait_ge(ctx.sems["p7_ve"], tt - 1)
                        for k in range(NCH):
                            mm = te.matmul(
                                ps_v[tt % 2][:, :],
                                stream[k][:, tt * 128:(tt + 1) * 128],
                                wv_sb[k][:, :],
                                start=(k == 0), stop=(k == NCH - 1))
                        mm.then_inc(ctx.sems["p7_vm"], 1)

                @blk.scalar
                def _(sc):
                    for tt in range(8):
                        sc.wait_ge(ctx.sems["p7_vm"], tt + 1)
                        if tt >= 2:
                            sc.wait_ge(p7_vwr[tt % 2], 16 * ((tt - 2) // 2 + 1))
                        sc.activation(vt[tt % 2][:, :], ps_v[tt % 2][:, :],
                                      AF.Identity, bias=0.0, scale=1.0
                                      ).then_inc(ctx.sems["p7_ve"], 1)

                @blk.gpsimd
                def _(gp):
                    def emit(slot):
                        for tt in range(8):
                            gp.wait_ge(ctx.sems["p7_ve"], tt + 1)
                            out = shr["v1"].ap()[slot * 128:(slot + 1) * 128,
                                                 tt * 512:(tt + 1) * 512]
                            gp.dma_start(out=out, in_=vt[tt % 2][:, :]
                                         ).then_inc(p7_vwr[tt % 2], 16)
                        gp.wait_ge(p7_kwr[0], 16 * 2)
                        gp.wait_ge(p7_kwr[1], 16 * 2)
                        gp.wait_ge(p7_vwr[0], 16 * 4)
                        gp.wait_ge(p7_vwr[1], 16 * 4)
                        _exchange_flag(gp, ctx, slot, magic)
                    with gp.If_eq(ctx.regs["rank"], 0):
                        emit(0)
                    with gp.Else():
                        emit(1)
                    _exchange_poll(gp, ctx, magic)
                    gp.sem_inc(ctx.sem("p7_xch"), 1)

        with ExitStack() as p8:
            e8 = p8.enter_context
            kg = [e8(nc.sbuf_tensor(f"kg{i}", [128, 2048], BF16)) for i in range(2)]
            v_all = e8(nc.sbuf_tensor("vall", [128, 16 * 512], BF16))
            p_bf = [e8(nc.sbuf_tensor(f"p8b{i}", [128, 2048], BF16))
                    for i in range(2)]
            pt_sb = [e8(nc.sbuf_tensor(f"pt8{i}", [128, 2048], BF16))
                     for i in range(2)]
            negmax = [e8(nc.sbuf_tensor(f"nm8{i}", [128, 1], F32)) for i in range(2)]
            sumexp = [e8(nc.sbuf_tensor(f"se8{i}", [128, 1], F32)) for i in range(2)]
            rsum = [e8(nc.sbuf_tensor(f"rs8{i}", [128, 1], F32)) for i in range(2)]
            ps_sc = e8(nc.psum_tensor("ps8_sc", [128, 2048], F32))
            ps_pt = e8(nc.psum_tensor("ps8_pt", [128, 2048], BF16))
            ps_o = [e8(nc.psum_tensor(f"ps8_o{i}", [128, 512], F32))
                    for i in range(2)]
            for s in ("p8_ldA", "p8_ldB", "p8_vl", "p8_sc", "p8_msk", "p8_mx",
                      "p8_ex", "p8_pn", "p8_tr", "p8_pte", "p8_mm", "p8_oe",
                      "p8_rc"):
                ctx.sem(s)
            p8_ld = [ctx.sems["p8_ldA"], ctx.sems["p8_ldB"]]
            NIT = 8 * 4 * 8

            def its():
                out = []
                for g in range(8):
                    for hh in range(4):
                        for qt in range(8):
                            out.append((g, hh, qt))
                return out

            ITS = its()

            with nc.Block("l1_attn") as blk:
                @blk.sync
                def _(sy):
                    sy.wait_ge(ctx.sems["p7_xch"], 1)
                    for slot in range(2):
                        sy.dma_start(
                            out=v_all[:, slot * 4096:(slot + 1) * 4096],
                            in_=shr["v1"].ap()[slot * 128:(slot + 1) * 128, :],
                        ).then_inc(ctx.sems["p8_vl"], 16)
                    for g in range(8):
                        if g >= 2:
                            sy.wait_ge(ctx.sems["p8_mm"], 32 * (g - 1))
                        for slot in range(2):
                            for dup in range(2):
                                sy.dma_start(
                                    out=kg[g % 2][dup * 64:dup * 64 + 64,
                                                  slot * 1024:(slot + 1) * 1024],
                                    in_=shr["k1"].ap()[slot * 512 + g * 64:
                                                       slot * 512 + (g + 1) * 64, :],
                                ).then_inc(p8_ld[g % 2], 16)

                def q_slice(hh, g, qt):
                    H = g * 4 + hh
                    ch, par = H // 2, H % 2
                    return q_bf[ch][par * 64:par * 64 + 64,
                                    qt * 128:(qt + 1) * 128]

                def _S4(te, it, rank):
                    g, hh, qt = ITS[it]
                    nk = len(_l1_kept(rank, qt))
                    par = it % 2
                    te.wait_ge(ctx.sems["p8_pn"], it + 1)
                    for i in range(nk):
                        mm = te.transpose(ps_pt[:, i * 128:(i + 1) * 128],
                                          p_bf[par][:, i * 128:(i + 1) * 128],
                                          ident[:, :])
                    mm.then_inc(ctx.sems["p8_tr"], 1)

                def _S6(te, it, rank):
                    g, hh, qt = ITS[it]
                    par = it % 2
                    kept = _l1_kept(rank, qt)
                    te.wait_ge(ctx.sems["p8_pte"], it + 1)
                    if it >= 2:
                        te.wait_ge(ctx.sems["p8_oe"], it - 1)
                    for i, (kt, kind) in enumerate(kept):
                        mm = te.matmul(
                            ps_o[par][0:64, 0:128],
                            v_all[:, kt * 512 + g * 64:kt * 512 + g * 64 + 64],
                            pt_sb[par][:, i * 128:(i + 1) * 128],
                            start=(i == 0), stop=(i == len(kept) - 1))
                    mm.then_inc(ctx.sems["p8_mm"], 1)

                def te_body(te, rank):
                    for it, (g, hh, qt) in enumerate(ITS):
                        kept = _l1_kept(rank, qt)
                        par = it % 2
                        H = g * 4 + hh
                        hb = (H % 2) * 64
                        if it == 0:
                            te.wait_ge(ctx.sems["p8_vl"], 32)
                        if qt == 0 and hh == 0:
                            te.wait_ge(p8_ld[g % 2], 64 * (g // 2 + 1))
                        if it >= 1:
                            te.wait_ge(ctx.sems["p8_ex"], it)
                        for i, (kt, kind) in enumerate(kept):
                            mm = te.matmul(
                                ps_sc[:, i * 128:(i + 1) * 128],
                                q_slice(hh, g, qt),
                                kg[g % 2][hb:hb + 64, kt * 128:(kt + 1) * 128],
                                start=True, stop=True)
                        mm.then_inc(ctx.sems["p8_sc"], 1)
                        if it >= 1:
                            _S4(te, it - 1, rank)
                            _S6(te, it - 1, rank)
                    _S4(te, NIT - 1, rank)
                    _S6(te, NIT - 1, rank)

                @blk.tensor
                def _(te):
                    with te.If_eq(ctx.regs["rank_te"], 0):
                        te_body(te, 0)
                    with te.Else():
                        te_body(te, 1)

                def ve_body(ve, rank):
                    for it, (g, hh, qt) in enumerate(ITS):
                        kept = _l1_kept(rank, qt)
                        nk = len(kept)
                        wk = nk * 128
                        par = it % 2
                        di = [i for i, (kt, kind) in enumerate(kept)
                              if kind == "diag"][0]
                        ve.wait_ge(ctx.sems["p8_sc"], it + 1)
                        ve.tensor_add(ps_sc[:, di * 128:(di + 1) * 128],
                                      ps_sc[:, di * 128:(di + 1) * 128],
                                      diag_sb[:, :]).then_inc(
                            ctx.sems["p8_msk"], 1)
                        if it >= 2:
                            ve.wait_ge(ctx.sems["p8_ex"], it - 1)
                        ve.wait_ge(ctx.sems["p8_msk"], it + 1)
                        ve.reduce_max(
                            negmax[par][:, :],
                            ps_sc.ap()[:, 0:wk].rearrange(
                                "p (a b) -> p a b", b=4)[:, :, 0:1],
                            axis=AX.XY, negate=True).then_inc(
                            ctx.sems["p8_mx"], 1)
                        if it >= 1:
                            parp = (it - 1) % 2
                            ve.wait_ge(ctx.sems["p8_ex"], it)
                            ve.reciprocal(rsum[parp][:, :], sumexp[parp][:, :]
                                          ).then_inc(ctx.sems["p8_rc"], 1)
                            ve.wait_ge(ctx.sems["p8_rc"], it)
                            wkp = len(_l1_kept(rank, ITS[it - 1][2])) * 128
                            ve.tensor_scalar_mul(
                                p_bf[parp][:, 0:wkp], p_bf[parp][:, 0:wkp],
                                rsum[parp][:, 0:1]).then_inc(ctx.sems["p8_pn"], 1)
                    parp = (NIT - 1) % 2
                    ve.wait_ge(ctx.sems["p8_ex"], NIT)
                    ve.reciprocal(rsum[parp][:, :], sumexp[parp][:, :]).then_inc(
                        ctx.sems["p8_rc"], 1)
                    ve.wait_ge(ctx.sems["p8_rc"], NIT)
                    wkp = len(_l1_kept(rank, ITS[NIT - 1][2])) * 128
                    ve.tensor_scalar_mul(p_bf[parp][:, 0:wkp],
                                         p_bf[parp][:, 0:wkp],
                                         rsum[parp][:, 0:1]).then_inc(
                        ctx.sems["p8_pn"], 1)

                @blk.vector
                def _(ve):
                    with ve.If_eq(ctx.regs["rank_ve"], 0):
                        ve_body(ve, 0)
                    with ve.Else():
                        ve_body(ve, 1)

                def sc_body(sc, rank):
                    def S5(it):
                        par = it % 2
                        wk = len(_l1_kept(rank, ITS[it][2])) * 128
                        sc.wait_ge(ctx.sems["p8_tr"], it + 1)
                        sc.copy(pt_sb[par][:, 0:wk], ps_pt[:, 0:wk]).then_inc(
                            ctx.sems["p8_pte"], 1)

                    def S7(it):
                        g, hh, qt = ITS[it]
                        H = g * 4 + hh
                        ch, hp = H // 2, (H % 2) * 64
                        par = it % 2
                        sc.wait_ge(ctx.sems["p8_mm"], it + 1)
                        sc.copy(ctxh[ch][hp:hp + 64, qt * 128:(qt + 1) * 128],
                                ps_o[par][0:64, 0:128]).then_inc(
                            ctx.sems["p8_oe"], 1)

                    for it, (g, hh, qt) in enumerate(ITS):
                        par = it % 2
                        wk = len(_l1_kept(rank, qt)) * 128
                        sc.wait_ge(ctx.sems["p8_mx"], it + 1)
                        if it >= 2:
                            sc.wait_ge(ctx.sems["p8_tr"], it - 1)
                        sc.activation(p_bf[par][:, 0:wk], ps_sc[:, 0:wk],
                                      AF.Exp, bias=negmax[par][:, 0:1],
                                      scale=1.0,
                                      accum_out=sumexp[par][:, :]).then_inc(
                            ctx.sems["p8_ex"], 1)
                        if it >= 1:
                            S5(it - 1)
                        if it >= 2:
                            S7(it - 2)
                    S5(NIT - 1)
                    S7(NIT - 2)
                    S7(NIT - 1)

                @blk.scalar
                def _(sc):
                    with sc.If_eq(ctx.regs["rank_sc"], 0):
                        sc_body(sc, 0)
                    with sc.Else():
                        sc_body(sc, 1)

        with ExitStack() as p9:
            e9 = p9.enter_context
            wbuf = [e9(nc.sbuf_tensor(f"w9_{i}", [128, D], F32R)) for i in range(2)]
            wo_bf = [e9(nc.sbuf_tensor(f"wob{i}", [128, D], BF16)) for i in range(2)]
            sq_buf = [e9(nc.sbuf_tensor(f"sq9{i}", [128, T], F32R))
                      for i in range(2)]
            rstd_sb = e9(nc.sbuf_tensor("rstd9", [128, T], F32))
            rstd2_sb = e9(nc.sbuf_tensor("rstd9b", [128, T], F32))
            ps_sum = e9(nc.psum_tensor("ps9_sum", [128, T], F32))
            ps_var = e9(nc.psum_tensor("ps9_var", [128, T], F32))
            psq = [e9(nc.psum_tensor(f"ps9_{i}", [128, T], F32)) for i in range(2)]
            for s in ("p9_wA", "p9_wB", "p9_c", "p9_mm", "p9_res"):
                ctx.sem(s)
            p9_w = [ctx.sems["p9_wA"], ctx.sems["p9_wB"]]
            ln = LN(ctx, "ln3", stream, ctx.sb["ones"], sq_buf, rstd_sb,
                    rstd2_sb, ps_sum, ps_var, ("p9_res", lambda j: j + 1), dst)

            with nc.Block("l1_wo") as blk:
                @blk.sync
                def _(sy):
                    for j in range(NCH):
                        if j >= 2:
                            sy.wait_ge(ctx.sems["p9_c"], j - 1)
                        sy.dma_start(out=wbuf[j % 2][:, :],
                                     in_=_wblock(w["l1_wo"], j)
                                     ).then_inc(p9_w[j % 2], 16)
                    ln.store(sy)

                @blk.vector
                def _(ve):
                    for j in range(NCH):
                        ve.wait_ge(p9_w[j % 2], 16 * (j // 2 + 1))
                        if j >= 2:
                            ve.wait_ge(ctx.sems["p9_mm"], j - 1)
                        ve.tensor_copy(wo_bf[j % 2][:, :], wbuf[j % 2][:, :]
                                       ).then_inc(ctx.sems["p9_c"], 1)
                        ve.wait_ge(ctx.sems["p9_mm"], j + 1)
                        ve.scalar_tensor_tensor(
                            stream[j][:, :], psq[j % 2][:, :], 0.0,
                            stream[j][:, :], ALU.add, ALU.add).then_inc(
                            ctx.sems["p9_res"], 1)
                    ln.ve(ve)

                @blk.tensor
                def _(te):
                    for j in range(NCH):
                        te.wait_ge(ctx.sems["p9_c"], j + 1)
                        if j >= 2:
                            te.wait_ge(ctx.sems["p9_res"], j - 1)
                        for oc in range(NCH):
                            for half in range(2):
                                mm = te.matmul(
                                    psq[j % 2][:, half * 512:(half + 1) * 512],
                                    wo_bf[j % 2][:, oc * 128:(oc + 1) * 128],
                                    ctxh[oc][:, half * 512:(half + 1) * 512],
                                    start=(oc == 0), stop=(oc == NCH - 1))
                        mm.then_inc(ctx.sems["p9_mm"], 1)
                    ln.te(te)

                @blk.scalar
                def _(sc):
                    ln.sc(sc)


def _group_l1_ffn(ctx, src, dst):
    nc = ctx.nc
    w = ctx.dram["w"]
    spill = ctx.dram["spill"]
    NHC1 = HFF1 // 128

    with ExitStack() as ph:
        e = ph.enter_context
        stat_acc = e(nc.sbuf_tensor("stat10", [128, T], F32))
        rgb = [e(nc.sbuf_tensor(f"rgb{m}", [128, 1], F32)) for m in range(3)]
        magic_sb = e(nc.sbuf_tensor("magic10", [128, 1], F32))

        with ExitStack() as pa:
            ea = pa.enter_context
            gbuf = [ea(nc.sbuf_tensor(f"gb{i}", [128, HFF1], F32))
                    for i in range(2)]
            tmp_r = ea(nc.sbuf_tensor("gtmp", [128, 1], F32))
            acc = ea(nc.sbuf_tensor("gacc", [128, 1], F32))
            idf = ea(nc.sbuf_tensor("idf", [128, 128], F32))
            gsum = ea(nc.sbuf_tensor("gsum", [1, 1], F32))
            gt2 = ea(nc.sbuf_tensor("gt2", [1, 1], F32))
            gt2b = ea(nc.sbuf_tensor("gt2b", [1, 1], F32))
            grc = ea(nc.sbuf_tensor("grc", [1, 2], F32))
            ps_t = ea(nc.psum_tensor("ps10t", [128, 512], F32))
            for s in ("pa_ldA", "pa_ldB", "pa_rd", "pa_tr", "pa_trb", "pa_sum",
                      "pa_bc", "pa_done", "pa_fin", "pa_v1", "pa_v2", "pa_v3", "pa_mg"):
                ctx.sem(s)
            pa_ld = [ctx.sems["pa_ldA"], ctx.sems["pa_ldB"]]
            mats = [("l1_w1", NHC1, D), ("l1_w2", NHC1, D), ("l1_w3", NCH, HFF1)]

            with nc.Block("l1_gamma") as blk:
                @blk.sync
                def _(sy):
                    sy.dma_start(out=magic_sb[:, :],
                                 in_=w["magic_f"][:, :]).then_inc(
                        ctx.sem("pa_mg"), 16)
                    sy.dma_start(out=idf[:, :], in_=w["ident_f"][:, :]
                                 ).then_inc(ctx.sem("pa_mg"), 16)
                    g = 0
                    for mi, (mn, nb, kd) in enumerate(mats):
                        for b in range(nb):
                            if g >= 2:
                                sy.wait_ge(ctx.sems["pa_rd"], g - 1)
                            sy.dma_start(out=gbuf[g % 2][:, 0:kd],
                                         in_=_wblock(w[mn], b)
                                         ).then_inc(pa_ld[g % 2], 16)
                            g += 1

                @blk.tensor
                def _(te):
                    te.wait_ge(ctx.sems["pa_mg"], 32)
                    for mi in range(3):
                        te.wait_ge(ctx.sems["pa_done"], mi + 1)
                        te.transpose(ps_t[0:1, 0:128], acc[:, 0:1], idf[:, :]
                                     ).then_inc(ctx.sems["pa_tr"], 1)
                        te.wait_ge(ctx.sems["pa_sum"], mi + 1)
                        mm = te.matmul(ps_t[:, 128:129],
                                       ctx.sb["ones"].ap()[0:1, 0:128].bitcast(F32),
                                       grc[0:1, 0:1],
                                       start=True, stop=True)
                        mm.then_inc(ctx.sems["pa_trb"], 1)

                @blk.vector
                def _(ve):
                    nk = {0: HFF1 * D, 1: HFF1 * D, 2: D * HFF1}
                    ch = ctx.sem("pa_ch")
                    n = 0
                    g = 0
                    for mi, (mn, nb, kd) in enumerate(mats):
                        for b in range(nb):
                            ve.wait_ge(pa_ld[g % 2], 16 * (g // 2 + 1))
                            ve.tensor_reduce(tmp_r[:, :], gbuf[g % 2][:, 0:kd],
                                             AX.X, ALU.add,
                                             apply_absolute_value=True
                                             ).then_inc(ch, 1)
                            n += 1
                            ve.wait_ge(ch, n)
                            if b == 0:
                                ve.tensor_scalar_mul(acc[:, :], tmp_r[:, :], 1.0
                                                     ).then_inc(ch, 1)
                            else:
                                ve.tensor_add(acc[:, :], acc[:, :], tmp_r[:, :]
                                              ).then_inc(ch, 1)
                            n += 1
                            ve.wait_ge(ch, n)
                            ve.sem_inc(ctx.sems["pa_rd"], 1)
                            g += 1
                        ve.sem_inc(ctx.sems["pa_done"], 1)
                        ve.wait_ge(ctx.sems["pa_tr"], mi + 1)
                        ve.reduce_sum(gsum[0:1, 0:1], ps_t[0:1, 0:128], axis=AX.X
                                      ).then_inc(ctx.sem("pa_v1"), 1)
                        ve.wait_ge(ctx.sems["pa_v1"], mi + 1)
                        ve.tensor_scalar(gt2[0:1, 0:1], gsum[0:1, 0:1],
                                         1.0 / nk[mi], EPS_BL, ALU.mult, ALU.add
                                         ).then_inc(ctx.sem("pa_v2"), 1)
                        ve.wait_ge(ctx.sems["pa_v2"], mi + 1)
                        ve.reciprocal(gt2b[0:1, 0:1], gt2[0:1, 0:1]).then_inc(
                            ctx.sem("pa_v3"), 1)
                        ve.wait_ge(ctx.sems["pa_v3"], mi + 1)
                        ve.tensor_scalar_mul(grc[0:1, 0:1], gt2b[0:1, 0:1],
                                             1.0).then_inc(ctx.sems["pa_sum"], 1)
                        ve.wait_ge(ctx.sems["pa_trb"], mi + 1)
                        ve.tensor_scalar_mul(rgb[mi][:, :], ps_t[:, 128:129], 1.0
                                             ).then_inc(ctx.sems["pa_bc"], 1)

        with ExitStack() as pb:
            eb = pb.enter_context
            h1 = [eb(nc.sbuf_tensor(f"h1_{c}", [128, T], BF16))
                  for c in range(NHC1)]
            ph_b = pb.enter_context(ExitStack())
            e4 = ph_b.enter_context
            xh = [e4(nc.sbuf_tensor(f"x10{c}", [128, 512], F32R))
                  for c in range(NCH)]
            wr = [e4(nc.sbuf_tensor(f"wr10{i}", [128, D], F32)) for i in range(2)]
            tq = [e4(nc.sbuf_tensor(f"tq10{i}", [128, D], F32)) for i in range(2)]
            wq = [e4(nc.sbuf_tensor(f"wq10{i}", [128, D], F32R)) for i in range(4)]
            sg = [e4(nc.sbuf_tensor(f"sg10{i}", [128, 512], F32)) for i in range(2)]
            ps_x1 = [e4(nc.psum_tensor(f"px1{i}", [128, 512], F32))
                     for i in range(2)]
            ps_x2 = [e4(nc.psum_tensor(f"px2{i}", [128, 512], F32))
                     for i in range(2)]
            for s in ("pb_pre", "pb_wA", "pb_wB", "pb_t", "pb_q", "pb_mm",
                      "pb_sg", "pb_ev"):
                ctx.sem(s)
            pb_w = [ctx.sems["pb_wA"], ctx.sems["pb_wB"]]
            gst0 = ctx.gst

            with nc.Block("l1_up") as blk:
                @blk.sync
                def _(sy):
                    for hf in range(2):
                        if hf == 0 and gst0:
                            sy.wait_ge(ctx.sems["gst"], gst0)
                        if hf == 1:
                            sy.wait_ge(ctx.sems["pb_mm"], NHC1)
                        for c in range(NCH):
                            sy.dma_start(
                                out=xh[c][:, :],
                                in_=src.ap()[c * 128:(c + 1) * 128,
                                             hf * 512:(hf + 1) * 512],
                            ).then_inc(ctx.sems["pb_pre"], 16)
                        for hc in range(NHC1):
                            for m in range(2):
                                g = (hf * NHC1 + hc) * 2 + m
                                if g >= 2:
                                    sy.wait_ge(ctx.sems["pb_t"], g - 1)
                                sy.dma_start(
                                    out=wr[g % 2][:, :],
                                    in_=_wblock(w["l1_w1" if m == 0 else "l1_w2"],
                                                hc)).then_inc(pb_w[g % 2], 16)

                @blk.scalar
                def _(sc):
                    def magic(g):
                        m = g % 2
                        sc.wait_ge(pb_w[g % 2], 16 * (g // 2 + 1))
                        if g >= 2:
                            sc.wait_ge(ctx.sems["pb_q"], g - 1)
                        sc.activation(tq[g % 2][:, :], wr[g % 2][:, :],
                                      AF.Identity, bias=magic_sb[:, 0:1],
                                      scale=rgb[m][:, 0:1]).then_inc(
                            ctx.sems["pb_t"], 1)

                    def sig(u):
                        sc.wait_ge(ctx.sems["pb_mm"], u + 1)
                        if u >= 2:
                            sc.wait_ge(ctx.sems["pb_ev"], u - 1)
                        sc.activation(sg[u % 2][:, :], ps_x1[u % 2][:, :],
                                      AF.Sigmoid).then_inc(ctx.sems["pb_sg"], 1)

                    for u in range(2 * NHC1):
                        magic(2 * u)
                        magic(2 * u + 1)
                        if u >= 1:
                            sig(u - 1)
                    sig(2 * NHC1 - 1)

                @blk.vector
                def _(ve):
                    def quant(g):
                        ve.wait_ge(ctx.sems["pb_t"], g + 1)
                        ve.tensor_scalar(tq[g % 2][:, :], tq[g % 2][:, :],
                                         RND_MAGIC, -1.0, ALU.subtract, ALU.max)
                        if g >= 4:
                            ve.wait_ge(ctx.sems["pb_mm"], (g - 4) // 2 + 1)
                        ve.tensor_scalar_min(wq[g % 4][:, :], tq[g % 2][:, :],
                                             1.0).then_inc(ctx.sems["pb_q"], 1)

                    def hmul(u):
                        hf, hc = divmod(u, NHC1)
                        ve.wait_ge(ctx.sems["pb_sg"], u + 1)
                        ve.tensor_mul(
                            h1[hc][:, hf * 512:(hf + 1) * 512],
                            sg[u % 2][:, :], ps_x2[u % 2][:, :],
                        ).then_inc(ctx.sems["pb_ev"], 1)

                    for u in range(2 * NHC1):
                        quant(2 * u)
                        quant(2 * u + 1)
                        if u >= 1:
                            hmul(u - 1)
                    hmul(2 * NHC1 - 1)

                @blk.tensor
                def _(te):
                    for hf in range(2):
                        for hc in range(NHC1):
                            u = hf * NHC1 + hc
                            if hc == 0:
                                te.wait_ge(ctx.sems["pb_pre"], 16 * 16 * (hf + 1))
                            te.wait_ge(ctx.sems["pb_q"], 2 * u + 2)
                            if u >= 2:
                                te.wait_ge(ctx.sems["pb_ev"], u - 1)
                                te.wait_ge(ctx.sems["pb_sg"], u - 1)
                            for k in range(NCH):
                                mm = te.matmul(
                                    ps_x1[u % 2][:, :],
                                    wq[(2 * u) % 4][:, k * 128:(k + 1) * 128],
                                    xh[k][:, :],
                                    start=(k == 0), stop=(k == NCH - 1))
                            for k in range(NCH):
                                mm = te.matmul(
                                    ps_x2[u % 2][:, :],
                                    wq[(2 * u + 1) % 4][:, k * 128:(k + 1) * 128],
                                    xh[k][:, :],
                                    start=(k == 0), stop=(k == NCH - 1))
                            mm.then_inc(ctx.sems["pb_mm"], 1)

            ph_b.close()

            with ExitStack() as pc:
                ec = pc.enter_context
                PW = 11 * 128
                w3r = [ec(nc.sbuf_tensor(f"w3r{i}", [128, PW], F32))
                       for i in range(2)]
                t3q = [ec(nc.sbuf_tensor(f"t3q{i}", [128, PW], F32))
                       for i in range(2)]
                w3q = [ec(nc.sbuf_tensor(f"w3q{i}", [128, PW], BF16))
                       for i in range(2)]
                xres = [ec(nc.sbuf_tensor(f"xr10{i}", [128, T], F32R))
                        for i in range(2)]
                ybuf = [ec(nc.sbuf_tensor(f"yb10{i}", [128, T], F32R))
                        for i in range(2)]
                ps_y = [ec(nc.psum_tensor(f"py10{i}", [128, T], F32))
                        for i in range(2)]
                ps_stat = ec(nc.psum_tensor("ps10s", [128, T], F32))
                for s in ("pc_wA", "pc_wB", "pc_xA", "pc_xB", "pc_t", "pc_q",
                          "pc_mmq", "pc_sj", "pc_res", "pc_stA", "pc_stB",
                          "pc_sums"):
                    ctx.sem(s)
                pc_w = [ctx.sems["pc_wA"], ctx.sems["pc_wB"]]
                pc_x = [ctx.sems["pc_xA"], ctx.sems["pc_xB"]]
                pc_st = [ctx.sems["pc_stA"], ctx.sems["pc_stB"]]

                with nc.Block("l1_down") as blk:
                    @blk.sync
                    def _(sy):
                        for j in range(NCH):
                            if j >= 2:
                                sy.wait_ge(ctx.sems["pc_res"], j - 1)
                            sy.dma_start(out=xres[j % 2][:, :],
                                         in_=src.ap()[j * 128:(j + 1) * 128, :]
                                         ).then_inc(pc_x[j % 2], 16)
                            for q in range(4):
                                gq = j * 4 + q
                                if gq >= 2:
                                    sy.wait_ge(ctx.sems["pc_t"], gq - 1)
                                sy.dma_start(
                                    out=w3r[gq % 2][:, :],
                                    in_=_wblock(w["l1_w3"], j, q * PW, PW),
                                ).then_inc(pc_w[gq % 2], 16)
                            sy.wait_ge(ctx.sems["pc_res"], j + 1)
                            sy.dma_start(out=spill.ap()[j * 128:(j + 1) * 128, :],
                                         in_=ybuf[j % 2][:, :]).then_inc(
                                pc_st[j % 2], 16)

                    @blk.scalar
                    def _(sc):
                        for gq in range(64):
                            sc.wait_ge(pc_w[gq % 2], 16 * (gq // 2 + 1))
                            if gq >= 2:
                                sc.wait_ge(ctx.sems["pc_q"], gq - 1)
                            sc.activation(t3q[gq % 2][:, :], w3r[gq % 2][:, :],
                                          AF.Identity, bias=magic_sb[:, 0:1],
                                          scale=rgb[2][:, 0:1]).then_inc(
                                ctx.sems["pc_t"], 1)

                    @blk.vector
                    def _(ve):
                        for j in range(NCH):
                            for q in range(4):
                                gq = j * 4 + q
                                ve.wait_ge(ctx.sems["pc_t"], gq + 1)
                                ve.tensor_scalar(t3q[gq % 2][:, :],
                                                 t3q[gq % 2][:, :],
                                                 RND_MAGIC, -1.0,
                                                 ALU.subtract, ALU.max)
                                if gq >= 2:
                                    ve.wait_ge(ctx.sems["pc_mmq"], gq - 1)
                                ve.tensor_scalar_min(w3q[gq % 2][:, :],
                                                     t3q[gq % 2][:, :], 1.0
                                                     ).then_inc(ctx.sems["pc_q"], 1)
                            ve.wait_ge(ctx.sems["pc_mmq"], 4 * (j + 1))
                            if j >= 2:
                                ve.wait_ge(pc_st[j % 2], 16 * ((j - 2) // 2 + 1))
                                ve.wait_ge(ctx.sems["pc_sums"], j - 1)
                            ve.wait_ge(pc_x[j % 2], 16 * (j // 2 + 1))
                            ve.scalar_tensor_tensor(
                                ybuf[j % 2][:, :], ps_y[j % 2][:, :], 1.0,
                                xres[j % 2][:, :], ALU.mult, ALU.add,
                            ).then_inc(ctx.sems["pc_res"], 1)
                            ve.wait_ge(ctx.sems["pc_sj"], j + 1)
                            if j == 0:
                                ve.tensor_scalar_mul(stat_acc[:, :],
                                                     ps_stat[:, :], 1.0
                                                     ).then_inc(
                                    ctx.sems["pc_sums"], 1)
                            else:
                                ve.tensor_add(stat_acc[:, :], stat_acc[:, :],
                                              ps_stat[:, :]).then_inc(
                                    ctx.sems["pc_sums"], 1)

                    @blk.tensor
                    def _(te):
                        for j in range(NCH):
                            for q in range(4):
                                gq = j * 4 + q
                                te.wait_ge(ctx.sems["pc_q"], gq + 1)
                                if j >= 2 and q == 0:
                                    te.wait_ge(ctx.sems["pc_res"], j - 1)
                                for f in range(11):
                                    hc = q * 11 + f
                                    for half in range(2):
                                        mm = te.matmul(
                                            ps_y[j % 2][:, half * 512:
                                                        (half + 1) * 512],
                                            w3q[gq % 2][:, f * 128:(f + 1) * 128],
                                            h1[hc][:, half * 512:(half + 1) * 512],
                                            start=(hc == 0), stop=(hc == NHC1 - 1))
                                mm.then_inc(ctx.sems["pc_mmq"], 1)
                            te.wait_ge(ctx.sems["pc_res"], j + 1)
                            if j >= 1:
                                te.wait_ge(ctx.sems["pc_sums"], j)
                            for half in range(2):
                                mm = te.matmul(
                                    ps_stat[:, half * 512:(half + 1) * 512],
                                    ctx.sb["ones"][:, :],
                                    ybuf[j % 2][:, half * 512:(half + 1) * 512],
                                    start=True, stop=True)
                            mm.then_inc(ctx.sems["pc_sj"], 1)

        with ExitStack() as pd:
            ed = pd.enter_context
            stream = [ed(nc.sbuf_tensor(f"s10_{c}", [128, T], F32R))
                      for c in range(NCH)]
            sq_buf = [ed(nc.sbuf_tensor(f"sq10{i}", [128, T], F32R))
                      for i in range(2)]
            rstd_sb = ed(nc.sbuf_tensor("rstd10", [128, T], F32))
            rstd2_sb = ed(nc.sbuf_tensor("rstd10b", [128, T], F32))
            ps_var = ed(nc.psum_tensor("ps10v", [128, T], F32))
            for s in ("pd_lA", "pd_lB"):
                ctx.sem(s)
            pd_l = [ctx.sems["pd_lA"], ctx.sems["pd_lB"]]
            ln = LN(ctx, "ln4", stream, ctx.sb["ones"], sq_buf, rstd_sb,
                    rstd2_sb, stat_acc, ps_var,
                    ("pd_lA", lambda j: 16 * (j // 2 + 1)), dst, sum_done=True,
                    ve_rdy=(pd_l, lambda j: 16 * (j // 2 + 1)))

            with nc.Block("l1_ln4") as blk:
                @blk.sync
                def _(sy):
                    sy.wait_ge(ctx.sems["pc_stA"], 16 * 8)
                    sy.wait_ge(ctx.sems["pc_stB"], 16 * 8)
                    for j in range(NCH):
                        sy.dma_start(out=stream[j][:, :],
                                     in_=spill.ap()[j * 128:(j + 1) * 128, :]
                                     ).then_inc(pd_l[j % 2], 16)
                    ln.store(sy)

                @blk.tensor
                def _(te):
                    ln.te(te)

                @blk.vector
                def _(ve):
                    ln.ve(ve)

                @blk.scalar
                def _(sc):
                    ln.sc(sc)



def _exchange_flag(gp, ctx, slot, magic):
    flagv = ctx.sb["flagv_sb"]
    sh_flag = ctx.dram["shr"]["flag"]
    n_prev = getattr(ctx, "xfl_count", 0)
    if n_prev:
        gp.wait_ge(ctx.sem("xfl"), 16 * n_prev)
    gp.memset(flagv[0:1, 0:1], magic).then_inc(ctx.sem("xchm"), 1)
    gp.wait_ge(ctx.sems["xchm"], n_prev + 1)
    gp.dma_start(out=sh_flag.ap()[0:1, slot:slot + 1],
                 in_=flagv[0:1, 0:1]).then_inc(ctx.sem("xfl"), 16)


def _exchange_poll(gp, ctx, magic):
    ctx.xfl_count = getattr(ctx, "xfl_count", 0) + 1
    gp.wait_ge(ctx.sem("xfl"), 16 * ctx.xfl_count)
    flag_rd = ctx.sb["flag_rd"]
    sh_flag = ctx.dram["shr"]["flag"]
    regs = ctx.regs
    with gp.register(f"pcond{ctx.xfl_count}") as pcond:
        gp.reg_mov(pcond, 1)
        with gp.While(pcond):
            gp.reg_add(regs["poll_tgt"], regs["poll_tgt"], 16)
            gp.dma_start(out=flag_rd[0:1, 0:2],
                         in_=sh_flag.ap()[0:1, 0:2]).then_inc(ctx.sem("xpl"), 16)
            gp.wait_ge(ctx.sems["xpl"], regs["poll_tgt"])
            gp.reg_load(regs["pc"], flag_rd[0:1, 0:1])
            gp.reg_load(regs["pc1"], flag_rd[0:1, 1:2])
            gp.reg_alu(regs["pc"], regs["pc"], magic, ALU.not_equal)
            gp.reg_alu(regs["pc1"], regs["pc1"], magic, ALU.not_equal)
            gp.reg_alu(pcond, regs["pc"], regs["pc1"], ALU.logical_or)



def build(groups):
    nc = bass.Bass("TRN2", target_bir_lowering=False)
    es = ExitStack()
    ctx = Ctx(nc, es)
    first, last = groups[0], groups[-1]
    assert groups == list(range(first, last + 1))

    if first == 0:
        src0 = nc.dram_tensor("x_t", [D, T], F32R, kind="ExternalInput")
    else:
        src0 = nc.dram_tensor("stream_in", [D, T], F32R, kind="ExternalInput")
    if last == 3:
        dst_final = nc.dram_tensor("y_out", [D, T], F32R, kind="ExternalOutput")
    else:
        dst_final = nc.dram_tensor("stream_out", [D, T], F32R,
                                   kind="ExternalOutput")

    w = {}

    def wdecl(name, shape, dt=F32R):
        w[name] = nc.dram_tensor(name, shape, dt, kind="ExternalInput")

    wdecl("meta", [1, 8], I32)
    wdecl("ones_f", [128, 128])
    if 0 in groups:
        for n in ("l0_wq", "l0_wk", "l0_wv", "l0_wo"):
            wdecl(n, [D, D])
        for n in ("l0_bq", "l0_bk", "l0_bv", "l0_bo"):
            wdecl(n, [128, 16], F32)
    if 1 in groups:
        wdecl("l0_w1", [HFF0, D])
        wdecl("l0_w3", [HFF0, D])
        wdecl("l0_w2", [D, HFF0])
        wdecl("w128_bf", [128, 128], BF16)
    if 2 in groups:
        wdecl("l1_wq", [D, D])
        wdecl("l1_wk", [512, D])
        wdecl("l1_wv_r", [D, 512])
        wdecl("l1_wo", [D, D])
        wdecl("rsw", [128, 128])
        wdecl("cos_rep", [128, T], F32)
        wdecl("sin_rep", [128, T], F32)
        wdecl("diag_mask", [128, 128], F32)
    if 3 in groups:
        wdecl("l1_w1", [HFF1, D], F32)
        wdecl("l1_w2", [HFF1, D], F32)
        wdecl("l1_w3", [D, HFF1], F32)
        wdecl("ident_f", [128, 128], F32)
        wdecl("magic_f", [128, 1], F32)
    if 0 in groups or 2 in groups:
        wdecl("ident_bf", [128, 128], BF16)

    shr = {}
    if 0 in groups:
        shr["k0"] = nc.dram_tensor("sh_k0", [2 * D, T], BF16, addr_space="Shared")
        shr["v0"] = nc.dram_tensor("sh_v0", [2 * 128, NCH * T], BF16, addr_space="Shared")
    if 2 in groups:
        shr["k1"] = nc.dram_tensor("sh_k1", [2 * 512, T], BF16, addr_space="Shared")
        shr["v1"] = nc.dram_tensor("sh_v1", [2 * 128, 8 * 512], BF16, addr_space="Shared")
    if shr:
        shr["flag"] = nc.dram_tensor("sh_flag", [1, 16], I32, addr_space="Shared")

    spill = nc.dram_tensor("spill", [D, T], F32R)
    handoff = {}
    for g in groups[:-1]:
        handoff[g] = nc.dram_tensor(f"handoff{g}", [D, T], F32R)

    ctx.dram = dict(w=w, shr=shr, spill=spill)

    with es:
        for rn in ("rank", "poll_tgt", "pc", "pc1"):
            ctx.regs[rn] = nc.gpsimd.alloc_register(rn)
        ctx.regs["rank_te"] = nc.tensor.alloc_register("rank_te")
        ctx.regs["rank_ve"] = nc.vector.alloc_register("rank_ve")
        ctx.regs["rank_sc"] = nc.scalar.alloc_register("rank_sc")
        ctx.sb["meta_sb"] = es.enter_context(nc.sbuf_tensor("meta_sb", [1, 8], I32))
        ctx.sb["flagv_sb"] = es.enter_context(
            nc.sbuf_tensor("flagv_sb", [1, 4], I32))
        ctx.sb["flag_rd"] = es.enter_context(
            nc.sbuf_tensor("flag_rd", [1, 16], I32))
        ctx.sb["ones"] = es.enter_context(nc.sbuf_tensor("ones_g", [128, 128], F32R))
        ctx.sb["eps"] = es.enter_context(nc.sbuf_tensor("eps_g", [128, 1], F32))

        with nc.Block("init") as blk:
            @blk.gpsimd
            def _(gp):
                gp.dma_start(out=ctx.sb["meta_sb"][:, :],
                             in_=w["meta"][:, :]).then_inc(ctx.sem("mld"), 16)
                gp.dma_start(out=ctx.sb["ones"][:, :],
                             in_=w["ones_f"][:, :]).then_inc(ctx.sem("mld"), 16)
                gp.wait_ge(ctx.sems["mld"], 32)
                gp.reg_load(ctx.regs["rank"], ctx.sb["meta_sb"][0:1, 0:1])
                gp.reg_mov(ctx.regs["poll_tgt"], 0)
                gp.memset(ctx.sb["eps"][:, :], EPS_LN)

            @blk.tensor
            def _(te):
                te.wait_ge(ctx.sems["mld"], 32)
                te.reg_load(ctx.regs["rank_te"], ctx.sb["meta_sb"][0:1, 0:1])

            @blk.vector
            def _(ve):
                ve.wait_ge(ctx.sems["mld"], 32)
                ve.reg_load(ctx.regs["rank_ve"], ctx.sb["meta_sb"][0:1, 0:1])

            @blk.scalar
            def _(sc):
                sc.wait_ge(ctx.sems["mld"], 32)
                sc.reg_load(ctx.regs["rank_sc"], ctx.sb["meta_sb"][0:1, 0:1])

        cur = src0
        for g in groups:
            dstg = dst_final if g == last else handoff[g]
            ctx.begin_group()
            if g == 0:
                _group_l0_attn(ctx, cur, dstg)
            elif g == 1:
                _group_l0_ffn(ctx, cur, dstg)
            elif g == 2:
                _group_l1_attn(ctx, cur, dstg)
            elif g == 3:
                _group_l1_ffn(ctx, cur, dstg)
            ctx.end_group()
            cur = dstg

        with nc.Block("fin") as blk:
            @blk.sync
            def _(sy):
                sy.wait_ge(ctx.sem("gst"), ctx.gst)

    return nc


def kernel(**inputs):
    in_maps = stage_inputs(inputs)
    cur = [dict(m) for m in in_maps]
    res = None
    for grp in LAUNCH_GROUPS:
        nc = build(grp)
        res = run_bass_kernel_spmd(nc, cur, core_ids=list(range(8)))
        if grp[-1] != 3:
            for c in range(8):
                cur[c]["stream_in"] = np.asarray(res.results[c]["stream_out"])
    return unstage_output(res.results)
